# version 07
# speedup vs baseline: 1.5913x; 1.0198x over previous
import numpy as np
import ml_dtypes

import concourse.bass as bass
import concourse.mybir as mybir
from concourse.bass_utils import run_bass_kernel_spmd

F32 = mybir.dt.float32
BF16 = mybir.dt.bfloat16
AL = mybir.AluOpType
AF = mybir.ActivationFunctionType

SET_EXPLN = 6
SET_SIGMOID = 2
SET_GELU = 10
SET_SILU = 18

D = 1024
ENGINES = ("tensor", "vector", "scalar", "gpsimd", "sync")
FEV_MODE = "stt"


def load_act_set(scalar, nc, idx):
    scalar.add_instruction(mybir.InstLoadActFuncSet(
        name=nc.get_next_instruction_name(), act_func_set_id=idx, ins=[], outs=[]))
    scalar.drain()


class _EngProxy:

    def __init__(self, eng):
        object.__setattr__(self, "_eng", eng)
        object.__setattr__(self, "last", None)

    def __getattr__(self, name):
        attr = getattr(self._eng, name)
        if not callable(attr):
            return attr

        def wrapper(*a, **k):
            r = attr(*a, **k)
            if r is not None and hasattr(r, "then_inc"):
                object.__setattr__(self, "last", r)
            return r
        return wrapper


class Seq:

    def __init__(self, nc):
        self.nc = nc
        self.items = []

    def c(self, eng, fn, dma_lag=None):
        self.items.append((eng, fn, ("c2", dma_lag) if dma_lag else "c"))

    def dma(self, eng, fn, n=1):
        self.items.append((eng, fn, ("d", n)))

    def raw(self, eng_fns):
        base = len(self.items)
        for eng, fn in eng_fns.items():
            self.items.append((eng, fn, ("r", base)))

    def build(self, block, c_sem, d_sems):
        counts = {k: 0 for k in d_sems}
        ndma_at = []
        for (e, _, kind) in self.items:
            ndma_at.append(dict(counts))
            if isinstance(kind, tuple) and kind[0] == "d":
                counts[e] += kind[1]

        def make_runner(name):
            def runner(eng):
                last_dwait = {k: 0 for k in d_sems}
                prev_idx = -2
                for i, (e, fn, kind) in enumerate(self.items):
                    if e != name:
                        continue
                    if isinstance(kind, tuple) and kind[0] == "r":
                        eng.wait_ge(c_sem, kind[1])
                        for k, n in ndma_at[i].items():
                            if n > last_dwait[k]:
                                eng.wait_ge(d_sems[k], 16 * n)
                                last_dwait[k] = n
                        proxy = _EngProxy(eng)
                        fn(proxy)
                        assert proxy.last is not None
                        proxy.last.then_inc(c_sem, 1)
                        prev_idx = i
                        continue
                    eng.wait_ge(c_sem, i)
                    lag = kind[1] if isinstance(kind, tuple) and kind[0] == "c2" else {}
                    for k, n in ndma_at[i].items():
                        n_eff = max(0, n - lag.get(k, 0))
                        if n_eff > last_dwait[k]:
                            eng.wait_ge(d_sems[k], 16 * n_eff)
                            last_dwait[k] = n_eff
                    if isinstance(kind, tuple) and kind[0] == "d":
                        ins = fn(eng)
                        if not isinstance(ins, (list, tuple)):
                            ins = [ins]
                        assert len(ins) == kind[1], f"item {i}: {len(ins)} != n={kind[1]}"
                        for q in ins:
                            q.then_inc(d_sems[e], 16)
                        eng.nop().then_inc(c_sem, 1)
                    else:
                        proxy = _EngProxy(eng)
                        fn(proxy)
                        assert proxy.last is not None, f"item {i} emitted no instruction"
                        proxy.last.then_inc(c_sem, 1)
                    prev_idx = i
            return runner

        for name in ENGINES:
            getattr(block, name)(make_runner(name))
        return len(self.items), dict(counts)



def emit_ln_prep(seq, nc, ctx, T, with_mean):
    xt, ph, pl = ctx["xt"], ctx["ph"], ctx["pl"]
    sq, pf, stat = ctx["sq"], ctx["pf"], ctx["stat"]
    ones_col, ones_row = ctx["ones_col"], ctx["ones_row"]
    ps_stA, ps_stB, ps_rep = ctx["ps_stA"], ctx["ps_stB"], ctx["ps_rep"]
    eps_t = ctx.get("eps_t")
    nts = T // 512

    seq.c("scalar", lambda s: load_act_set(s, nc, SET_EXPLN))
    for ts in range(nts):
        sl = slice(512 * ts, 512 * (ts + 1))
        for i in range(8):
            def fsq(s, i=i, sl=sl):
                s.activation(sq, xt[i].ap()[:, sl], AF.Square)
            seq.c("scalar", fsq)

            def fmm(t, i=i, sl=sl):
                t.matmul(ps_stB, ones_col[:], sq,
                         start=(i == 0), stop=(i == 7))
                t.matmul(ps_stA, ones_col[:], xt[i].ap()[:, sl],
                         start=(i == 0), stop=(i == 7))
            seq.c("tensor", fmm)

        def fst(v, ts=ts):
            v.tensor_copy(stat.ap()[0:1, 512 * ts:512 * (ts + 1)], ps_stA)
            v.tensor_copy(stat.ap()[0:1, T + 512 * ts:T + 512 * (ts + 1)], ps_stB)
        seq.c("vector", fst)

    def fmu(v):
        if with_mean:
            v.tensor_scalar(stat.ap()[0:1, 0:T], stat.ap()[0:1, 0:T],
                            1.0 / 1024.0, None, AL.mult)
        else:
            v.memset(stat.ap()[0:1, 0:T], 0.0)
        v.tensor_scalar(stat.ap()[0:1, T:2 * T], stat.ap()[0:1, T:2 * T],
                        1.0 / 1024.0, None, AL.mult)
        v.tensor_tensor(pf.ap()[0:1, 0:T], stat.ap()[0:1, 0:T],
                        stat.ap()[0:1, 0:T], AL.mult)
        v.tensor_tensor(stat.ap()[0:1, T:2 * T], stat.ap()[0:1, T:2 * T],
                        pf.ap()[0:1, 0:T], AL.subtract)
        v.tensor_scalar(stat.ap()[0:1, T:2 * T], stat.ap()[0:1, T:2 * T],
                        float(ctx.get("eps", 1e-5)), None, AL.add)
    seq.c("vector", fmu)

    def frs(s):
        s.activation(stat.ap()[0:1, 2 * T:3 * T], stat.ap()[0:1, T:2 * T], AF.Ln)
        s.activation(stat.ap()[0:1, 2 * T:3 * T], stat.ap()[0:1, 2 * T:3 * T], AF.Exp, scale=-0.5)
    seq.c("scalar", frs)

    def fnewton(v):
        v.tensor_tensor(pf.ap()[0:1, 0:T], stat.ap()[0:1, 2 * T:3 * T],
                        stat.ap()[0:1, 2 * T:3 * T], AL.mult)
        v.tensor_tensor(pf.ap()[0:1, 0:T], pf.ap()[0:1, 0:T],
                        stat.ap()[0:1, T:2 * T], AL.mult)
        v.tensor_scalar(pf.ap()[0:1, 0:T], pf.ap()[0:1, 0:T],
                        -0.5, 1.5, AL.mult, AL.add)
        v.tensor_tensor(stat.ap()[0:1, T:2 * T], stat.ap()[0:1, 2 * T:3 * T],
                        pf.ap()[0:1, 0:T], AL.mult)
    seq.c("vector", fnewton)

    for i in range(8):
        for ts in range(nts):
            sl = slice(512 * ts, 512 * (ts + 1))
            if with_mean:
                def frepmu(t, ts=ts):
                    t.matmul(ps_rep[:], ones_row[:],
                             stat.ap()[0:1, 512 * ts:512 * (ts + 1)],
                             start=True, stop=True)
                seq.c("tensor", frepmu)

                def fsub(v, i=i, sl=sl):
                    v.tensor_tensor(pf.ap()[:, sl], xt[i].ap()[:, sl],
                                    ps_rep.ap()[:], AL.subtract)
                seq.c("vector", fsub)

            def freprs(t, ts=ts):
                t.matmul(ps_rep[:], ones_row[:],
                         stat.ap()[0:1, T + 512 * ts:T + 512 * (ts + 1)],
                         start=True, stop=True)
            seq.c("tensor", freprs)

            def fmul(v, i=i, sl=sl):
                src0 = pf.ap()[:, sl] if with_mean else xt[i].ap()[:, sl]
                v.tensor_tensor(pf.ap()[:, sl], src0, ps_rep.ap()[:], AL.mult)
                if ctx.get("wcol") is not None:
                    v.tensor_scalar(pf.ap()[:, sl], pf.ap()[:, sl], ctx["wcol"][i],
                                    None, AL.mult)
            seq.c("vector", fmul)

        def fcast(v, i=i):
            if ctx.get("out_f32"):
                v.tensor_copy(xt[i].ap()[:, 0:T], pf.ap()[:, 0:T])
            else:
                v.tensor_copy(ph[i].ap()[:, 0:T], pf.ap()[:, 0:T])
                v.tensor_tensor(pl[i].ap()[:, 0:T], pf.ap()[:, 0:T], ph[i].ap()[:, 0:T],
                                AL.subtract)
        seq.c("vector", fcast)



def build_ffn_phase(with_wo, max_items=None):
    T = 2048
    NT = 256
    nc = bass.Bass("TRN2", target_bir_lowering=False, detect_race_conditions=False)
    x_in = nc.dram_tensor("x", [D, T], F32, kind="ExternalInput").ap()
    if with_wo:
        ctx_in = nc.dram_tensor("ctx", [D, T], F32, kind="ExternalInput").ap()
        woT_in = nc.dram_tensor("woT", [D, D], F32, kind="ExternalInput").ap()
        bo_in = nc.dram_tensor("bo", [128, 8], F32, kind="ExternalInput").ap()
    w1T_in = nc.dram_tensor("w1T", [D, 4096], F32, kind="ExternalInput").ap()
    b1_in = nc.dram_tensor("b1", [128, 32], F32, kind="ExternalInput").ap()
    w2T_in = nc.dram_tensor("w2T", [4096, D], F32, kind="ExternalInput").ap()
    b2_in = nc.dram_tensor("b2", [128, 8], F32, kind="ExternalInput").ap()
    out_hbm = nc.dram_tensor("out", [D, T], F32, kind="ExternalOutput").ap()
    dbg_hbm = nc.dram_tensor("dbg", [4, 2048], F32, kind="ExternalOutput").ap()
    s1_scr = nc.dram_tensor("s1_scr", [D, 4096], BF16, kind="Internal").ap()
    s2_scr = nc.dram_tensor("s2_scr", [4096, D], BF16, kind="Internal").ap()

    xt = [nc.alloc_sbuf_tensor(f"xt{i}", [128, T], F32) for i in range(8)]
    bp = [nc.alloc_sbuf_tensor(f"bp{i}", [128, T], BF16) for i in range(24)]
    ph, pl = bp[0:8], bp[8:16]
    hh, hl = bp[16:20], bp[20:24]
    pf = nc.alloc_sbuf_tensor("pf", [128, T], F32)
    wstage = nc.alloc_sbuf_tensor("wstage", [128, 1024], F32)
    wbh = nc.alloc_sbuf_tensor("wbh", [128, 2048], BF16)
    wbl = nc.alloc_sbuf_tensor("wbl", [128, 2048], BF16)
    ones_col = nc.alloc_sbuf_tensor("ones_col", [128, 1], F32)
    ones_row = nc.alloc_sbuf_tensor("ones_row", [1, 128], F32)
    stat = nc.alloc_sbuf_tensor("stat", [1, 3 * T], F32)
    eps_t = nc.alloc_sbuf_tensor("eps_t", [1, 1], F32)
    alpha1 = nc.alloc_sbuf_tensor("alpha1", [128, 32], F32)
    b1c = nc.alloc_sbuf_tensor("b1c", [128, 32], F32)
    alpha2 = nc.alloc_sbuf_tensor("alpha2", [128, 8], F32)
    b2c = nc.alloc_sbuf_tensor("b2c", [128, 8], F32)
    boc = nc.alloc_sbuf_tensor("boc", [128, 8], F32) if with_wo else None
    hsc = nc.alloc_sbuf_tensor("hsc", [128, 512], F32)
    sq = hsc.ap()[:, 0:512]
    asum2 = nc.alloc_sbuf_tensor("asum2", [128, 32], F32)
    s1t = wbh.ap()[:, 0:1024]
    s2t = wbh.ap()[:, 1024:2048]

    psb = [nc.alloc_psum_tensor(f"psb{i}", [128, 512], F32) for i in range(8)]
    ps_mm = psb[0]
    ps_stA = psb[1].ap()[0:1, 0:512]
    ps_stB = psb[2].ap()[0:1, 0:512]
    ps_rep = psb[3]

    ctx = dict(xt=xt, ph=ph, pl=pl, sq=sq, pf=pf, stat=stat, ones_col=ones_col,
               ones_row=ones_row, eps_t=eps_t, eps=1e-5,
               ps_stA=ps_stA, ps_stB=ps_stB, ps_rep=ps_rep)

    seq = Seq(nc)

    for i in range(8):
        seq.dma("sync", lambda s, i=i: s.dma_start(
            out=xt[i][:], in_=x_in.tensor.ap()[128 * i:128 * (i + 1), :]))

    def fconsts(v):
        v.memset(ones_col[:], 1.0)
        v.memset(ones_row[:], 1.0)
        v.memset(eps_t[:], 1e-5)
    seq.c("vector", fconsts)

    if with_wo:
        seq.dma("sync", lambda s: s.dma_start(out=boc[:], in_=bo_in))
        cth, ctl = bp[0:8], bp[8:16]
        for i in range(8):
            seq.dma("sync", lambda s, i=i: s.dma_start(
                out=pf.ap()[:, 0:T], in_=ctx_in.tensor.ap()[128 * i:128 * (i + 1), :]))

            def fsplit(v, i=i):
                v.tensor_copy(cth[i].ap()[:, 0:T], pf.ap()[:, 0:T])
                v.tensor_tensor(ctl[i].ap()[:, 0:T], pf.ap()[:, 0:T],
                                cth[i].ap()[:, 0:T], AL.subtract)
            seq.c("vector", fsplit)
        for dt in range(8):
            for dch in range(8):
                seq.dma("sync", lambda s, dt=dt, dch=dch: s.dma_start(
                    out=wstage.ap()[:, 128 * dch:128 * (dch + 1)],
                    in_=woT_in.tensor.ap()[128 * dch:128 * (dch + 1),
                                           128 * dt:128 * (dt + 1)]))

            def fws(v, dt=dt):
                v.tensor_copy(wbh.ap()[:, 0:1024], wstage.ap()[:, 0:1024])
                v.tensor_tensor(wbl.ap()[:, 0:1024], wstage.ap()[:, 0:1024],
                                wbh.ap()[:, 0:1024], AL.subtract)
            seq.c("vector", fws)
            for tt in range(4):
                def fmm(t, dt=dt, tt=tt):
                    for dch in range(8):
                        wsh = wbh.ap()[:, 128 * dch:128 * (dch + 1)]
                        wsl = wbl.ap()[:, 128 * dch:128 * (dch + 1)]
                        rh = cth[dch].ap()[:, 512 * tt:512 * (tt + 1)]
                        rl = ctl[dch].ap()[:, 512 * tt:512 * (tt + 1)]
                        t.matmul(ps_mm[:], wsh, rh, start=(dch == 0), stop=False)
                        t.matmul(ps_mm[:], wsl, rh, start=False, stop=False)
                        t.matmul(ps_mm[:], wsh, rl, start=False, stop=(dch == 7))
                seq.c("tensor", fmm)

                def fev(v, dt=dt, tt=tt):
                    if FEV_MODE == "stt":
                        v.scalar_tensor_tensor(xt[dt].ap()[:, 512 * tt:512 * (tt + 1)],
                                               ps_mm.ap()[:], boc.ap()[:, dt:dt + 1],
                                               xt[dt].ap()[:, 512 * tt:512 * (tt + 1)],
                                               AL.add, AL.add)
                    elif FEV_MODE == "copy":
                        v.tensor_copy(pf.ap()[:, 0:512], ps_mm.ap()[:])
                    elif FEV_MODE == "sbufonly":
                        v.tensor_copy(pf.ap()[:, 0:512], xt[0].ap()[:, 0:512])
                    elif FEV_MODE == "scalarcopy":
                        pass
                    elif FEV_MODE == "tt":
                        v.tensor_tensor(xt[dt].ap()[:, 512 * tt:512 * (tt + 1)],
                                        ps_mm.ap()[:],
                                        xt[dt].ap()[:, 512 * tt:512 * (tt + 1)], AL.add)
                        v.tensor_scalar(xt[dt].ap()[:, 512 * tt:512 * (tt + 1)],
                                        xt[dt].ap()[:, 512 * tt:512 * (tt + 1)],
                                        boc.ap()[:, dt:dt + 1], None, AL.add)
                if FEV_MODE == "scalarcopy":
                    def fevs(s, dt=dt, tt=tt):
                        s.copy(pf.ap()[:, 0:512], ps_mm.ap()[:])
                    seq.c("scalar", fevs)
                else:
                    seq.c("vector", fev)

    emit_ln_prep(seq, nc, ctx, T, with_mean=True)

    seq.c("scalar", lambda s: load_act_set(s, nc, SET_EXPLN))

    def fz(v):
        v.memset(asum2[:], 0.0)
        v.memset(alpha2[:], 0.0)
    seq.c("vector", fz)
    for dch in range(8):
        for qtr in range(4):
            seq.dma("sync", lambda s, dch=dch, qtr=qtr: s.dma_start(
                out=wstage[:], in_=w1T_in.tensor.ap()[128 * dch:128 * (dch + 1),
                                                      1024 * qtr:1024 * (qtr + 1)]))

            def fsgn(s):
                s.activation(wbh.ap()[:, 0:1024], wstage[:], AF.Sign)
            seq.c("scalar", fsgn)
            seq.dma("sync", lambda s, dch=dch, qtr=qtr: s.dma_start(
                out=s1_scr.tensor.ap()[128 * dch:128 * (dch + 1),
                                       1024 * qtr:1024 * (qtr + 1)], in_=wbh.ap()[:, 0:1024]))

            def fabs(s):
                s.activation(wstage[:], wstage[:], AF.Abs)
            seq.c("scalar", fabs)

            def fcol(t, qtr=qtr):
                for j8 in range(8):
                    t.matmul(ps_rep.ap()[:, j8:j8 + 1],
                             wstage.ap()[:, 128 * j8:128 * (j8 + 1)], ones_col[:],
                             start=True, stop=True)
            seq.c("tensor", fcol)

            def fadd(v, qtr=qtr):
                v.tensor_tensor(asum2.ap()[:, 8 * qtr:8 * (qtr + 1)],
                                asum2.ap()[:, 8 * qtr:8 * (qtr + 1)],
                                ps_rep.ap()[:, 0:8], AL.add)
            seq.c("vector", fadd)

    def fsc(v):
        v.tensor_scalar(alpha1[:], asum2[:], 1.0 / 1024.0, None, AL.mult)
    seq.c("vector", fsc)
    seq.dma("sync", lambda s: s.dma_start(out=b1c[:], in_=b1_in))
    seq.dma("sync", lambda s: s.dma_start(out=b2c[:], in_=b2_in))

    for jch in range(32):
        seq.dma("sync", lambda s, jch=jch: s.dma_start(
            out=wstage.ap()[:, 0:1024],
            in_=w2T_in.tensor.ap()[128 * jch:128 * (jch + 1), :]))

        def fsgn2(s):
            s.activation(wbh.ap()[:, 0:1024], wstage.ap()[:, 0:1024], AF.Sign)
        seq.c("scalar", fsgn2)
        seq.dma("sync", lambda s, jch=jch: s.dma_start(
            out=s2_scr.tensor.ap()[128 * jch:128 * (jch + 1), :],
            in_=wbh.ap()[:, 0:1024]))

        def fabs2(s):
            s.activation(wstage.ap()[:, 0:1024], wstage.ap()[:, 0:1024], AF.Abs)
        seq.c("scalar", fabs2)

        def fcol2(t):
            for d8 in range(8):
                t.matmul(ps_rep.ap()[:, d8:d8 + 1],
                         wstage.ap()[:, 128 * d8:128 * (d8 + 1)], ones_col[:],
                         start=True, stop=True)
        seq.c("tensor", fcol2)

        def fadd2(v):
            v.tensor_tensor(alpha2[:], alpha2[:], ps_rep.ap()[:, 0:8], AL.add)
        seq.c("vector", fadd2)

    def fsc2(v):
        v.tensor_scalar(alpha2[:], alpha2[:], 1.0 / 4096.0, None, AL.mult)
    seq.c("vector", fsc2)

    seq.c("sync", lambda s: s.nop())
    seq.c("scalar", lambda s: load_act_set(s, nc, SET_GELU))
    bufs4 = [s1t, s2t, wbl.ap()[:, 0:1024], wbl.ap()[:, 1024:2048]]
    LAG1 = {"sync": 16}
    LAG2 = {"sync": 2}
    for tt in range(T // NT):
        tsl = slice(NT * tt, NT * (tt + 1))

        def femit_ld1(jp):
            pb = (bufs4[0], bufs4[1]) if jp % 2 == 0 else (bufs4[2], bufs4[3])

            def fld(s, jp=jp, pb=pb):
                out = []
                for q in range(2):
                    jt = 2 * jp + q
                    for dch in range(8):
                        out.append(s.dma_start(
                            out=pb[q][:, 128 * dch:128 * (dch + 1)],
                            in_=s1_scr.tensor.ap()[128 * dch:128 * (dch + 1),
                                                   128 * jt:128 * (jt + 1)]))
                return out
            seq.dma("sync", fld, n=16)

        femit_ld1(0)
        for jp in range(16):
            if jp + 1 < 16:
                femit_ld1(jp + 1)
            pb = (bufs4[0], bufs4[1]) if jp % 2 == 0 else (bufs4[2], bufs4[3])
            jts = (2 * jp, 2 * jp + 1)
            hscs = (hsc, pf)

            def fmm1(t, tsl=tsl, pb=pb):
                for q in range(2):
                    for dch in range(8):
                        w = pb[q][:, 128 * dch:128 * (dch + 1)]
                        t.matmul(psb[q].ap()[:, 0:NT], w, ph[dch].ap()[:, tsl],
                                 start=(dch == 0), stop=False)
                        t.matmul(psb[q].ap()[:, 0:NT], w, pl[dch].ap()[:, tsl],
                                 start=False, stop=(dch == 7))
            seq.c("tensor", fmm1, dma_lag=LAG1)

            def fev1(v, jts=jts, hscs=hscs):
                for q, jt in enumerate(jts):
                    v.tensor_scalar(hscs[q].ap()[:, 0:NT], psb[q].ap()[:, 0:NT],
                                    alpha1.ap()[:, jt:jt + 1], None, AL.mult)
            seq.c("vector", fev1, dma_lag=LAG1)

            def fge(s, jts=jts, hscs=hscs):
                for q, jt in enumerate(jts):
                    s.activation(hscs[q].ap()[:, 256:256 + NT], hscs[q].ap()[:, 0:NT],
                                 AF.Gelu, bias=b1c.ap()[:, jt:jt + 1])
            seq.c("scalar", fge, dma_lag=LAG1)

            def fsp(v, jts=jts, hscs=hscs):
                for q, jt in enumerate(jts):
                    hhd = hh[jt // 8].ap()[:, NT * (jt % 8):NT * (jt % 8 + 1)]
                    hld = hl[jt // 8].ap()[:, NT * (jt % 8):NT * (jt % 8 + 1)]
                    v.tensor_copy(hhd, hscs[q].ap()[:, 256:256 + NT])
                    v.tensor_tensor(hld, hscs[q].ap()[:, 256:256 + NT], hhd, AL.subtract)
            seq.c("vector", fsp, dma_lag=LAG1)

        def femit_ld2(jp):
            pb = (bufs4[0], bufs4[1]) if jp % 2 == 0 else (bufs4[2], bufs4[3])

            def fld2(s, jp=jp, pb=pb):
                return [s.dma_start(out=pb[q][:, 0:1024],
                                    in_=s2_scr.tensor.ap()[128 * (2 * jp + q):128 * (2 * jp + q + 1), :])
                        for q in range(2)]
            seq.dma("sync", fld2, n=2)

        femit_ld2(0)
        for jp in range(16):
            if jp + 1 < 16:
                femit_ld2(jp + 1)
            pb = (bufs4[0], bufs4[1]) if jp % 2 == 0 else (bufs4[2], bufs4[3])

            def fmm2(t, jp=jp, pb=pb):
                for q in range(2):
                    jch = 2 * jp + q
                    rh = hh[jch // 8].ap()[:, NT * (jch % 8):NT * (jch % 8 + 1)]
                    rl = hl[jch // 8].ap()[:, NT * (jch % 8):NT * (jch % 8 + 1)]
                    for dt in range(8):
                        w = pb[q][:, 128 * dt:128 * (dt + 1)]
                        t.matmul(psb[dt].ap()[:, 256:256 + NT], w, rh,
                                 start=(jch == 0), stop=False)
                        t.matmul(psb[dt].ap()[:, 256:256 + NT], w, rl,
                                 start=False, stop=(jch == 31))
            seq.c("tensor", fmm2, dma_lag=LAG2)

        def fev2(v, tsl=tsl):
            for dt in range(8):
                v.scalar_tensor_tensor(xt[dt].ap()[:, tsl], psb[dt].ap()[:, 256:256 + NT],
                                       alpha2.ap()[:, dt:dt + 1], xt[dt].ap()[:, tsl],
                                       AL.mult, AL.add)
                v.tensor_scalar(xt[dt].ap()[:, tsl], xt[dt].ap()[:, tsl],
                                b2c.ap()[:, dt:dt + 1], None, AL.add)
        seq.c("vector", fev2)

    dbgt = pf
    def ftap(v):
        v.tensor_copy(dbgt.ap()[0:1, 0:T], stat.ap()[0:1, T:2 * T])
    seq.c("vector", ftap)
    seq.dma("gpsimd", lambda g: g.dma_start(out=dbg_hbm.tensor.ap()[0:1, :], in_=dbgt.ap()[0:1, 0:T]))
    def ftap2(v):
        v.memset(dbgt.ap()[0:1, 0:2048], 0.0)
        v.tensor_copy(dbgt.ap()[0:1, 0:32], alpha1.ap()[0:1, 0:32])
        v.tensor_copy(dbgt.ap()[0:1, 32:40], alpha2.ap()[0:1, 0:8])
    seq.c("vector", ftap2)
    seq.dma("gpsimd", lambda g: g.dma_start(out=dbg_hbm.tensor.ap()[1:2, :], in_=dbgt.ap()[0:1, 0:2048]))
    def ftap3(v):
        v.tensor_copy(dbgt.ap()[0:1, 0:2048], ph[0].ap()[0:1, 0:T])
    seq.c("vector", ftap3)
    seq.dma("gpsimd", lambda g: g.dma_start(out=dbg_hbm.tensor.ap()[2:3, :], in_=dbgt.ap()[0:1, 0:T]))
    def ftap4(v):
        v.tensor_copy(dbgt.ap()[0:1, 0:2048], hh[0].ap()[0:1, 0:T])
    seq.c("vector", ftap4)
    seq.dma("gpsimd", lambda g: g.dma_start(out=dbg_hbm.tensor.ap()[3:4, :], in_=dbgt.ap()[0:1, 0:T]))

    for i in range(8):
        seq.dma("gpsimd", lambda g, i=i: g.dma_start(
            out=out_hbm.tensor.ap()[128 * i:128 * (i + 1), :], in_=xt[i][:]))

    if max_items is not None:
        seq.items = seq.items[:max_items]
    with (
        nc.semaphore("c_sem") as c_sem,
        nc.semaphore("d_sync") as d_sync,
        nc.semaphore("d_gp") as d_gp,
        nc.Block() as block,
    ):
        d_sems = {"sync": d_sync, "gpsimd": d_gp}
        n_items, n_dma = seq.build(block, c_sem, d_sems)

        @block.gpsimd
        def _(g):
            g.wait_ge(c_sem, n_items)
            for k, s in d_sems.items():
                g.wait_ge(s, 16 * n_dma[k])
    return nc



def _cols128(v):
    v = np.asarray(v, dtype=np.float32)
    return np.ascontiguousarray(v.reshape(-1, 128).T)


def hT(a):
    return np.ascontiguousarray(np.asarray(a, dtype=np.float32).T)


def run_ffn_phase(x_fm_per_core, ffn_w, wo=None):
    nc = build_ffn_phase(with_wo=wo is not None)
    w1T = hT(ffn_w["w1"])
    w2T = hT(ffn_w["w2"])
    b1 = _cols128(ffn_w["b1"])
    b2 = _cols128(ffn_w["b2"])
    in_maps = []
    for c in range(8):
        m = {"x": np.ascontiguousarray(x_fm_per_core[c]),
             "w1T": w1T, "b1": b1, "w2T": w2T, "b2": b2}
        if wo is not None:
            m["ctx"] = np.ascontiguousarray(wo["ctx_fm"][c])
            m["woT"] = hT(wo["woT_pre"]) if "woT_pre" in wo else np.ascontiguousarray(wo["woT"])
            m["bo"] = _cols128(wo["bo"])
        in_maps.append(m)
    res = run_bass_kernel_spmd(nc, in_maps, core_ids=list(range(8)))
    return [res.results[c]["out"] for c in range(8)], [res.results[c]["dbg"] for c in range(8)]


def kernel(**inputs):
    T = 4096
    x0 = np.asarray(inputs["L0_attn_a_x"], np.float32)

    nc1 = build_attn_a_phase(T)
    pp = np.arange(128)[:, None] % 32
    cc = np.arange(32)[None, :]
    maskS = np.where(pp < cc, 0.0, -1e30).astype(np.float32)
    maskI = np.where(pp <= cc, 0.0, -1e30).astype(np.float32)
    selq = np.zeros((128, 256), np.float32)
    for tile in range(4):
        selq[0:64, 64 * tile + 2 * tile] = 1.0
        selq[64:128, 64 * tile + 2 * tile + 1] = 1.0
    ident = np.eye(128, dtype=np.float32)
    Wq, Wk = inputs["L0_attn_a_Wq"], inputs["L0_attn_a_Wk"]
    Wv, Wg, Wb = inputs["L0_attn_a_Wv"], inputs["L0_attn_a_Wg"], inputs["L0_attn_a_Wb"]
    Wa = np.asarray(inputs["L0_attn_a_Wa"], np.float32)
    dtb_a = np.asarray(inputs["L0_attn_a_dt_bias"], np.float32)
    alog_a = np.asarray(inputs["L0_attn_a_A_log"], np.float32)
    nw_a = np.asarray(inputs["L0_attn_a_norm_w"], np.float32)
    in1 = []
    for c in range(8):
        seq_i, h0 = c // 2, 8 * (c % 2)
        sl = slice(h0 * 64, (h0 + 8) * 64)
        wap = np.zeros((1024, 64), np.float32)
        wap[:, 0:8] = hT(Wa[h0:h0 + 8])
        in1.append({
            "x": hT(x0[seq_i]),
            "wqT": hT(np.asarray(Wq)[sl]), "wkT": hT(np.asarray(Wk)[sl]),
            "wvT": hT(np.asarray(Wv)[sl]), "wgT": hT(np.asarray(Wg)[sl]),
            "wbT": hT(np.asarray(Wb)[sl]), "waT": wap, "selq": selq,
            "dtb": dtb_a[h0:h0 + 8, None].copy(), "alog": alog_a[h0:h0 + 8, None].copy(),
            "nwfm": np.ascontiguousarray(np.tile(nw_a, 2)[:, None]).astype(np.float32),
            "maskS": maskS, "maskI": maskI, "ident": ident,
        })
    r1 = run_bass_kernel_spmd(nc1, in1, core_ids=list(range(8)))
    ctx_flat = np.concatenate([
        np.concatenate([r1.results[2 * s]["ctx"], r1.results[2 * s + 1]["ctx"]], 0).T
        for s in range(4)], 0)

    xflat = x0.reshape(4 * T, 1024)
    NTOK = 4 * T // 8
    ffn_w = {k: inputs["L0_ffn_" + k] for k in ("w1", "b1", "w2", "b2")}
    xs = [hT(xflat[NTOK * c:NTOK * (c + 1)]) for c in range(8)]
    ctxs = [hT(ctx_flat[NTOK * c:NTOK * (c + 1)]) for c in range(8)]
    outs, _ = run_ffn_phase(xs, ffn_w, wo=dict(
        ctx_fm=ctxs, woT=hT(inputs["L0_attn_a_Wo"]), bo=inputs["L0_attn_a_bo"]))
    x3flat = np.concatenate([o.T for o in outs], 0)

    in_w = np.asarray(inputs["L0_attn_b_in_proj_w"], np.float32)
    conv_w = np.asarray(inputs["L0_attn_b_conv_w"], np.float32)
    conv_b = np.asarray(inputs["L0_attn_b_conv_b"], np.float32)
    dtb_b = np.asarray(inputs["L0_attn_b_dt_bias"], np.float32)
    alog_b = np.asarray(inputs["L0_attn_b_A_log"], np.float32)
    Dsk = np.asarray(inputs["L0_attn_b_D"], np.float32)
    gnw = np.asarray(inputs["L0_attn_b_gnorm_w"], np.float32)
    opw = np.asarray(inputs["L0_attn_b_out_proj_w"], np.float32)
    nw_b = np.asarray(inputs["L0_attn_b_norm_w"], np.float32)
    pp64 = np.arange(128)[:, None] % 64
    cc64 = np.arange(64)[None, :]
    mask64 = np.where(pp64 <= cc64, 0.0, -1e30).astype(np.float32)

    def run_mamba_layer(xflat_in, li, pre_ln):
        nca = build_mamba_a_phase(T, pre_ln=pre_ln)
        xseq = [hT(xflat_in[T * s:T * (s + 1)]) for s in range(4)]
        y_rows = [None] * 4
        z_rows = [None] * 4
        ln_seq = [None] * 4
        for lh in range(2):
            ina = []
            for c in range(8):
                seq_i, h0 = c // 2, 16 * lh + 8 * (c % 2)
                rows = np.concatenate([
                    np.arange(h0 * 64, (h0 + 8) * 64),
                    2048 + np.arange(h0 * 64, (h0 + 8) * 64),
                    np.arange(4096, 4160), np.arange(4160, 4224),
                    4224 + h0 + np.arange(8)])
                inwT = np.zeros((1024, 1216), np.float32)
                inwT[:, 0:1160] = in_w[li][rows].T
                crow = np.concatenate([np.arange(h0 * 64, (h0 + 8) * 64),
                                       np.arange(2048, 2112), np.arange(2112, 2176)])
                cw = np.zeros((128, 20), np.float32)
                cb = np.zeros((128, 5), np.float32)
                for i in range(5):
                    cw[:, 4 * i:4 * i + 4] = conv_w[li][crow[128 * i:128 * (i + 1)]]
                    cb[:, i] = conv_b[li][crow[128 * i:128 * (i + 1)]]
                dc = np.zeros((128, 4), np.float32)
                for i in range(4):
                    dc[0:64, i] = Dsk[li][h0 + 2 * i]
                    dc[64:128, i] = Dsk[li][h0 + 2 * i + 1]
                ina.append({
                    "x": xseq[seq_i], "inwT": inwT, "nw1": _cols128(nw_b[li]),
                    "convw": cw, "convb": cb,
                    "dtb": dtb_b[li][h0:h0 + 8, None].copy(),
                    "alog": alog_b[li][h0:h0 + 8, None].copy(),
                    "dcolf": dc, "mask64": mask64, "ident": ident,
                })
            ra = run_bass_kernel_spmd(nca, ina, core_ids=list(range(8)))
            for c in range(8):
                seq_i, h0 = c // 2, 16 * lh + 8 * (c % 2)
                if y_rows[seq_i] is None:
                    y_rows[seq_i] = np.zeros((2048, T), np.float32)
                    z_rows[seq_i] = np.zeros((2048, T), np.float32)
                y_rows[seq_i][h0 * 64:(h0 + 8) * 64] = ra.results[c]["y"]
                z_rows[seq_i][h0 * 64:(h0 + 8) * 64] = ra.results[c]["z"]
                if pre_ln and lh == 0 and c % 2 == 0:
                    ln_seq[seq_i] = ra.results[c]["ln0"]
        yflat = np.concatenate([y.T for y in y_rows], 0)
        zflat = np.concatenate([z.T for z in z_rows], 0)
        lnflat = (np.concatenate([l.T for l in ln_seq], 0) if pre_ln else None)
        return yflat, zflat, lnflat

    y0, z0, ln0flat = run_mamba_layer(x3flat, 0, pre_ln=True)
    ncb0 = build_mamba_out_phase(NTOK, add_outer=False)
    inb = []
    for c in range(8):
        sl = slice(NTOK * c, NTOK * (c + 1))
        inb.append({"x": hT(ln0flat[sl]), "yf": hT(y0[sl]), "zf": hT(z0[sl]),
                    "gwc": _cols128(gnw[0]), "owT": hT(opw[0])})
    rb0 = run_bass_kernel_spmd(ncb0, inb, core_ids=list(range(8)))
    r1flat = np.concatenate([rb0.results[c]["out"].T for c in range(8)], 0)

    y1, z1, _ = run_mamba_layer(r1flat, 1, pre_ln=False)
    ncb1 = build_mamba_out_phase(NTOK, add_outer=True)
    inb = []
    for c in range(8):
        sl = slice(NTOK * c, NTOK * (c + 1))
        inb.append({"x": hT(r1flat[sl]), "yf": hT(y1[sl]), "zf": hT(z1[sl]),
                    "gwc": _cols128(gnw[1]), "owT": hT(opw[1]),
                    "xo": hT(x3flat[sl])})
    rb1 = run_bass_kernel_spmd(ncb1, inb, core_ids=list(range(8)))
    x4flat = np.concatenate([rb1.results[c]["out"].T for c in range(8)], 0)

    xs = [hT(x4flat[NTOK * c:NTOK * (c + 1)]) for c in range(8)]
    outs, _ = run_ffn_phase(xs, ffn_w, wo=None)
    x5flat = np.concatenate([o.T for o in outs], 0)
    return np.ascontiguousarray(x5flat.reshape(4, T, 1024))



def mk_ap(t, offset, dims):
    return bass.AP(tensor=t, offset=offset, ap=[list(d) for d in dims])


LAST_MARKS = {}


def build_attn_a_phase(T_SEQ=4096, debug=False, trunc=None):
    SPAN = 512
    NSP = T_SEQ // SPAN
    NW = SPAN // 32
    nc = bass.Bass("TRN2", target_bir_lowering=False, detect_race_conditions=False)
    x_in = nc.dram_tensor("x", [D, T_SEQ], F32, kind="ExternalInput").ap()
    w_ins = {}
    for nm, ncol in [("wqT", 512), ("wkT", 512), ("wvT", 512), ("wgT", 512), ("wbT", 512), ("waT", 64)]:
        w_ins[nm] = nc.dram_tensor(nm, [D, ncol], F32, kind="ExternalInput").ap()
    dtb_in = nc.dram_tensor("dtb", [8, 1], F32, kind="ExternalInput").ap()
    alog_in = nc.dram_tensor("alog", [8, 1], F32, kind="ExternalInput").ap()
    nwfm_in = nc.dram_tensor("nwfm", [128, 1], F32, kind="ExternalInput").ap()
    maskS_in = nc.dram_tensor("maskS", [128, 32], F32, kind="ExternalInput").ap()
    maskI_in = nc.dram_tensor("maskI", [128, 32], F32, kind="ExternalInput").ap()
    ident_in = nc.dram_tensor("ident", [128, 128], F32, kind="ExternalInput").ap()
    selq_in = nc.dram_tensor("selq", [128, 256], F32, kind="ExternalInput").ap()
    ctx_hbm = nc.dram_tensor("ctx", [512, T_SEQ], F32, kind="ExternalOutput").ap()
    scr = nc.dram_tensor("scr_fold", [8, 32768], F32, kind="Internal")
    dbg = nc.dram_tensor("dbg", [128, 4096], F32, kind="ExternalOutput").ap() if debug else None

    A = nc.alloc_sbuf_tensor
    s_st = [A(f"s_st{i}", [128, 64], F32) for i in range(4)]
    u_t = [A(f"u_t{g}", [128, 64], F32) for g in range(2)]
    tmpc = [A(f"tmpc{g}", [128, 64], F32) for g in range(2)]
    wbd = [A(f"wbd{g}", [128, 128], F32) for g in range(2)]
    mbd = [A(f"mbd{g}", [128, 128], F32) for g in range(2)]
    ones_col = A("ones_col", [128, 1], F32)
    ones_row = A("ones_row", [1, 128], F32)
    nwfm = A("nwfm_s", [128, 1], F32)
    maskS = A("maskS_s", [128, 32], F32)
    maskI = A("maskI_s", [128, 32], F32)
    identt = A("identt", [128, 128], F32)
    sel = A("sel", [128, 256], F32)
    dtb = A("dtb_s", [8, 1], F32)
    nega = A("nega", [8, 1], F32)
    WH, WL = {}, {}
    for nm, ncol in [("wqT", 512), ("wkT", 512), ("wvT", 512), ("wgT", 512), ("wbT", 512), ("waT", 64)]:
        WH[nm] = A("h" + nm, [128, 8 * ncol], BF16)
        WL[nm] = A("l" + nm, [128, 8 * ncol], BF16)
    xt = [A(f"xt{i}", [128, SPAN], F32) for i in range(8)]
    wstage = xt[0]
    ph = [A(f"ph{i}", [128, SPAN], BF16) for i in range(8)]
    pl = [A(f"pl{i}", [128, SPAN], BF16) for i in range(8)]
    pf = A("pf", [128, SPAN], F32)
    sqs = A("sqs", [128, 512], F32)
    stat = A("stat", [1, 3 * SPAN], F32)
    qf = [A(f"qf{i}", [128, SPAN], F32) for i in range(4)]
    kf = [A(f"kf{i}", [128, SPAN], F32) for i in range(4)]
    vstage = xt[0:4]
    vch = [A(f"vch{g}", [128, NW * 64], F32) for g in range(2)]
    bch = [A(f"bch{g}", [128, NW * 64], F32) for g in range(2)]
    ktm = [A(f"ktm{g}", [128, NW * 64], F32) for g in range(2)]
    tmstg = sqs
    ych = [A(f"ych{g}", [128, NW * 64], F32) for g in range(2)]
    vrS = [A(f"vrS{g}", [128, 64], F32) for g in range(2)]
    t1S = [A(f"t1S{g}", [128, SPAN], F32) for g in range(2)]
    t1I = [A(f"t1I{g}", [128, SPAN], F32) for g in range(2)]
    colstk = [A(f"colstk{g}", [128, NW * 8], F32) for g in range(2)]
    dccol = [A(f"dccol{i}", [128, NW], F32) for i in range(4)]
    rstdw = [A(f"rstdw{g}", [128, NW], F32) for g in range(2)]
    caf = A("caf", [8, SPAN], F32)
    spf = A("spf", [8, SPAN], F32)
    spf2 = A("spf2", [8, SPAN], F32)
    knf = A("knf", [8, SPAN], F32)
    qnf = A("qnf", [8, SPAN], F32)
    cakf = A("cakf", [8, SPAN], F32)
    decf = A("decf", [8, SPAN], F32)
    d2f = A("d2f", [8, SPAN], F32)
    auxf = A("auxf", [8, SPAN], F32)
    dcf = A("dcf", [8, NW], F32)
    cprev = A("cprev", [8, NW], F32)
    dsS = [A(f"dsS{g}", [128, 32], F32) for g in range(2)]
    dsI = [A(f"dsI{g}", [128, 32], F32) for g in range(2)]
    esS = [A(f"esS{g}", [128, 32], F32) for g in range(2)]
    esI = [A(f"esI{g}", [128, 32], F32) for g in range(2)]
    ctxf = xt[4:8]

    psb = [nc.alloc_psum_tensor(f"psA{i}", [128, 512], F32) for i in range(8)]
    ps_chain = [psb[0], psb[1]]
    ps_G = [psb[2], psb[3]]
    ps_R = [psb[4], psb[5]]
    ps_Y = [psb[6], psb[7]]

    seq = Seq(nc)

    def fconst(v):
        v.memset(ones_col[:], 1.0)
        v.memset(ones_row[:], 1.0)
        v.memset(wbd[0][:], 0.0)
        v.memset(wbd[1][:], 0.0)
        v.memset(mbd[0][:], 0.0)
        v.memset(mbd[1][:], 0.0)
        v.memset(u_t[0][:], 0.0)
        v.memset(u_t[1][:], 0.0)
        for i in range(4):
            v.memset(s_st[i][:], 0.0)
    seq.c("vector", fconst)
    seq.dma("sync", lambda s: s.dma_start(out=nwfm[:], in_=nwfm_in))
    seq.dma("sync", lambda s: s.dma_start(out=maskS[:], in_=maskS_in))
    seq.dma("sync", lambda s: s.dma_start(out=maskI[:], in_=maskI_in))
    seq.dma("sync", lambda s: s.dma_start(out=identt[:], in_=ident_in))
    seq.dma("sync", lambda s: s.dma_start(out=sel[:], in_=selq_in))
    seq.dma("sync", lambda s: s.dma_start(out=dtb[:], in_=dtb_in))
    seq.dma("sync", lambda s: s.dma_start(out=nega[:], in_=alog_in))

    def fnega(s):
        load_act_set(s, nc, SET_EXPLN)
        s.activation(nega[:], nega[:], AF.Exp)
    seq.c("scalar", fnega)
    seq.c("vector", lambda v: v.tensor_scalar(nega[:], nega[:], -1.0, None, AL.mult))

    for nm, ncol in [("wqT", 512), ("wkT", 512), ("wvT", 512), ("wgT", 512), ("wbT", 512), ("waT", 64)]:
        for dch in range(8):
            seq.dma("sync", lambda s, nm=nm, dch=dch, ncol=ncol: s.dma_start(
                out=wstage.ap()[:, 0:ncol],
                in_=w_ins[nm].tensor.ap()[128 * dch:128 * (dch + 1), :]))

            def fsp_(v, nm=nm, dch=dch, ncol=ncol):
                v.tensor_copy(WH[nm].ap()[:, ncol * dch:ncol * (dch + 1)], wstage.ap()[:, 0:ncol])
                v.tensor_tensor(WL[nm].ap()[:, ncol * dch:ncol * (dch + 1)], wstage.ap()[:, 0:ncol],
                                WH[nm].ap()[:, ncol * dch:ncol * (dch + 1)], AL.subtract)
            seq.c("vector", fsp_)

    ch = {"pe": 0, "dv": 0}
    ch_pe = nc.semaphore("ch_pe").__enter__()
    ch_dv = nc.semaphore("ch_dv").__enter__()

    for sp in range(NSP):
        toff = SPAN * sp

        def fxl(s, toff=toff):
            return [s.dma_start(out=xt[i][:],
                                in_=x_in.tensor.ap()[128 * i:128 * (i + 1), toff:toff + SPAN])
                    for i in range(8)]
        seq.dma("sync", fxl, n=8)

        ctxd = dict(xt=xt, ph=ph, pl=pl, sq=sqs.ap()[:, 0:512], pf=pf, stat=stat,
                    ones_col=ones_col, ones_row=ones_row, eps=1e-5,
                    ps_stA=psb[2].ap()[0:1, 0:512], ps_stB=psb[3].ap()[0:1, 0:512],
                    ps_rep=psb[4])
        emit_ln_prep(seq, nc, ctxd, SPAN, with_mean=True)
        if debug and sp == 0:
            seq.c("vector", lambda v: v.tensor_copy(pf[:], ph[0][:]))
            seq.dma("gpsimd", lambda sy: sy.dma_start(out=dbg.tensor.ap()[:, 2960:3472], in_=pf[:]))
            seq.dma("gpsimd", lambda sy: sy.dma_start(out=dbg.tensor.ap()[0:1, 2448:2960],
                                                      in_=stat.ap()[0:1, 512:1024]))
            LAST_MARKS["ln_dump"] = len(seq.items)

        def proj_fm(nm, outs):
            for ot in range(4):
                def fmm(t, nm=nm, ot=ot):
                    for dch in range(8):
                        wsh = WH[nm].ap()[:, 512 * dch + 128 * ot:512 * dch + 128 * (ot + 1)]
                        wsl = WL[nm].ap()[:, 512 * dch + 128 * ot:512 * dch + 128 * (ot + 1)]
                        t.matmul(psb[2][:], wsh, ph[dch][:], start=(dch == 0), stop=False)
                        t.matmul(psb[2][:], wsl, ph[dch][:], start=False, stop=False)
                        t.matmul(psb[2][:], wsh, pl[dch][:], start=False, stop=(dch == 7))
                seq.c("tensor", fmm)

                def fev(v, outs=outs, ot=ot):
                    v.tensor_copy(outs[ot][:], psb[2].ap()[:])
                seq.c("vector", fev)

        def tm_fold(src_tiles, fold_fn, e_cnt=64):
            LAGF = {"gpsimd": 5}
            for g in range(2):
                for blk in range(4):
                    slot = 4 * g + blk
                    stg = sqs.ap()[:, 0:256] if slot % 2 == 0 else sqs.ap()[:, 256:512]
                    lag0 = None if slot == 0 else LAGF

                    def ftr_(t, g=g, blk=blk):
                        for half in range(2):
                            t.transpose(psb[5].ap()[:, 128 * half:128 * (half + 1)],
                                        src_tiles[2 * g + half].ap()[:, 128 * blk:128 * (blk + 1)],
                                        identt[:])
                    seq.c("tensor", ftr_, dma_lag=lag0)

                    def fte_(v, stg=stg):
                        v.tensor_copy(stg, psb[5].ap()[:, 0:256])
                    seq.c("vector", fte_, dma_lag=lag0)

                    def fst_(sy, slot=slot, stg=stg):
                        return sy.dma_start(out=scr.ap()[slot:slot + 1, :], in_=stg)
                    seq.dma("gpsimd", fst_)

                    def ff_(sy, g=g, blk=blk, slot=slot, e_cnt=e_cnt):
                        out = []
                        for hh in range(4):
                            sap = mk_ap(scr, slot * 32768 + 64 * hh,
                                        [[256, 32], [32 * 256, 4], [1, e_cnt]])
                            out.append(sy.dma_start(out=fold_fn(g, blk, hh), in_=sap))
                        return out
                    seq.dma("gpsimd", ff_, n=4)

        def chain_dims(dst, blk, hh):
            return mk_ap(dst, 4 * blk * 64 + hh * 32 * NW * 64,
                         [[NW * 64, 32], [64, 4], [1, 64]])

        LAST_MARKS[f"pre_qk_{sp}"] = len(seq.items)
        proj_fm("wqT", qf)
        if debug and sp == 0:
            seq.dma("gpsimd", lambda sy: sy.dma_start(out=dbg.tensor.ap()[:, 3472:3984], in_=qf[0][:]))
            LAST_MARKS["q_dump"] = len(seq.items)
        proj_fm("wkT", kf)
        if debug and sp == 0:
            seq.dma("gpsimd", lambda sy: sy.dma_start(out=dbg.tensor.ap()[:, 1664:2176], in_=kf[0][:]))
            LAST_MARKS["k_dump"] = len(seq.items)
        LAST_MARKS[f"post_qk_{sp}"] = len(seq.items)

        def fmma(t):
            for dch in range(8):
                t.matmul(psb[5].ap()[0:64, 0:512], WH["waT"].ap()[:, 64 * dch:64 * (dch + 1)],
                         ph[dch][:], start=(dch == 0), stop=False)
                t.matmul(psb[5].ap()[0:64, 0:512], WL["waT"].ap()[:, 64 * dch:64 * (dch + 1)],
                         ph[dch][:], start=False, stop=False)
                t.matmul(psb[5].ap()[0:64, 0:512], WH["waT"].ap()[:, 64 * dch:64 * (dch + 1)],
                         pl[dch][:], start=False, stop=(dch == 7))
        seq.c("tensor", fmma)
        seq.c("vector", lambda v: v.tensor_copy(spf[:], psb[5].ap()[0:8, 0:512]))
        LAST_MARKS[f"a_copy_{sp}"] = len(seq.items)
        if debug and sp == 0:
            seq.dma("gpsimd", lambda sy: sy.dma_start(out=dbg.tensor.ap()[8:16, 0:512], in_=spf[:]))

        def fsp1(v):
            v.tensor_scalar(spf[:], spf[:], dtb[:], None, AL.add)
            v.tensor_scalar_max(spf2[:], spf[:], 0.0)
        seq.c("vector", fsp1)
        if debug and sp == 0:
            seq.dma("gpsimd", lambda sy: sy.dma_start(out=dbg.tensor.ap()[16:24, 0:512], in_=spf[:]))
            seq.dma("gpsimd", lambda sy: sy.dma_start(out=dbg.tensor.ap()[24:32, 0:512], in_=spf2[:]))

        def fsp2(s):
            load_act_set(s, nc, SET_EXPLN)
            s.activation(caf[:], spf[:], AF.Abs)
            s.activation(caf[:], caf[:], AF.Exp, scale=-1.0)
            s.activation(caf[:], caf[:], AF.Ln, bias=1.0)
        seq.c("scalar", fsp2)
        if debug and sp == 0:
            seq.dma("gpsimd", lambda sy: sy.dma_start(out=dbg.tensor.ap()[32:40, 0:512], in_=caf[:]))

        def fsp3(v):
            v.tensor_tensor(caf[:], caf[:], spf2[:], AL.add)
            v.tensor_scalar(caf[:], caf[:], nega[:], None, AL.mult)
            v.tensor_scalar(spf[:], spf[:], 0.0, 1.0, AL.mult, AL.add)
            v.tensor_tensor_scan(spf2[:], spf[:], caf[:], 0.0, AL.mult, AL.add)
            v.tensor_copy(caf[:], spf2[:])
            v.memset(cprev.ap()[0:8, 0:1], 0.0)
            v.tensor_copy(cprev.ap()[0:8, 1:NW], mk_ap(caf, 31, [[SPAN, 8], [32, NW - 1]]))
        seq.c("vector", fsp3)

        for nm, chd, do_sig in [("wvT", vch, False), ("wbT", bch, True)]:
            proj_fm(nm, vstage)
            if do_sig:
                def fsig(s):
                    load_act_set(s, nc, SET_SIGMOID)
                    for ot in range(4):
                        s.activation(vstage[ot][:], vstage[ot][:], AF.Sigmoid)
                seq.c("scalar", fsig)
            tm_fold(vstage, lambda g, blk, hh, chd=chd: chain_dims(chd[g], blk, hh))

        for src_t, fmdst in [(kf, knf), (qf, qnf)]:
            for tile in range(4):
                def fsq(s, src_t=src_t, tile=tile):
                    s.activation(pf[:], src_t[tile][:], AF.Square)
                seq.c("scalar", fsq)

                def fnm(t, tile=tile):
                    t.matmul(psb[2].ap()[0:64, 0:512], sel.ap()[:, 64 * tile:64 * (tile + 1)],
                             pf[:], start=(tile == 0), stop=(tile == 3))
                seq.c("tensor", fnm)

            def fcp(v, fmdst=fmdst):
                v.tensor_copy(fmdst[:], psb[2].ap()[0:8, 0:512])
            seq.c("vector", fcp)

            def fns(s, fmdst=fmdst):
                load_act_set(s, nc, SET_EXPLN)
                s.activation(spf2[:], fmdst[:], AF.Ln)
                s.activation(spf2[:], spf2[:], AF.Exp, scale=-0.5)
            seq.c("scalar", fns)

            def fnw(v, fmdst=fmdst):
                v.tensor_tensor(pf.ap()[0:8, 0:512], spf2[:], spf2[:], AL.mult)
                v.tensor_tensor(pf.ap()[0:8, 0:512], pf.ap()[0:8, 0:512], fmdst[:], AL.mult)
                v.tensor_scalar(pf.ap()[0:8, 0:512], pf.ap()[0:8, 0:512], -0.5, 1.5,
                                AL.mult, AL.add)
                v.tensor_tensor(fmdst[:], spf2[:], pf.ap()[0:8, 0:512], AL.mult)
            seq.c("vector", fnw)

        tm_fold(kf, lambda g, blk, hh: chain_dims(ktm[g], blk, hh))

        def fcak(s):
            s.activation(cakf[:], knf[:], AF.Ln)
            s.activation(auxf[:], qnf[:], AF.Ln)
        seq.c("scalar", fcak)

        def fcak2(v):
            v.tensor_tensor(cakf[:], cakf[:], caf[:], AL.add)
            v.tensor_tensor(auxf[:], auxf[:], caf[:], AL.add)
            v.tensor_scalar(auxf[:], auxf[:], -2.0794415416798357, None, AL.add)
        seq.c("vector", fcak2)

        def fdec(v):
            v.tensor_tensor(decf[:], caf[:], mk_ap(cprev, 0, [[NW, 8], [1, NW], [0, 32]]),
                            AL.subtract)
            v.tensor_tensor(d2f[:], caf[:], mk_ap(caf, 31, [[SPAN, 8], [32, NW], [0, 32]]),
                            AL.subtract)
        seq.c("vector", fdec)

        def fdece(s):
            s.activation(decf[:], decf[:], AF.Exp)
            s.activation(d2f[:], d2f[:], AF.Exp, scale=-1.0)
        seq.c("scalar", fdece)

        def fdc(v):
            v.tensor_copy(dcf[:], mk_ap(decf, 31, [[SPAN, 8], [32, NW]]))
            v.tensor_tensor(spf[:], decf[:], knf[:], AL.mult)
            v.tensor_scalar(spf[:], spf[:], -1.0, None, AL.mult)
            v.tensor_tensor(spf2[:], d2f[:], knf[:], AL.mult)
            v.tensor_tensor(decf[:], decf[:], qnf[:], AL.mult)
            v.tensor_scalar(decf[:], decf[:], 0.125, None, AL.mult)
            v.tensor_scalar(d2f[:], knf[:], -1.0, None, AL.mult)
        seq.c("vector", fdc)

        for i in range(4):
            for q, src_fm in enumerate([caf, knf, spf, spf2, decf, d2f]):
                for h in range(2):
                    def frc(sy, i=i, q=q, h=h, src_fm=src_fm):
                        sap = mk_ap(src_fm, (2 * i + h) * SPAN, [[SPAN, 1], [1, SPAN]])
                        dap = mk_ap(vstage[i], (64 * h + q) * SPAN, [[SPAN, 1], [1, SPAN]])
                        return sy.dma_start(out=dap, in_=sap)
                    seq.dma("gpsimd", frc)
        tm_fold(vstage, lambda g, blk, hh: mk_ap(colstk[g], 4 * blk * 8 + hh * 32 * NW * 8,
                                                 [[NW * 8, 32], [8, 4], [1, 8]]),
                e_cnt=8)

        def fstt(sy):
            return [sy.dma_start(out=scr.ap()[0:1, 0:8 * SPAN], in_=cakf[:]),
                    sy.dma_start(out=scr.ap()[1:2, 0:8 * SPAN], in_=auxf[:])]
        seq.dma("gpsimd", fstt, n=2)

        def ft12(sy):
            out = []
            for g in range(2):
                for hp in range(4):
                    sap = mk_ap(scr, (4 * g + hp) * SPAN, [[0, 32], [1, SPAN]])
                    dap = mk_ap(t1S[g], 32 * hp * 512, [[512, 32], [1, SPAN]])
                    out.append(sy.dma_start(out=dap, in_=sap))
                    sap = mk_ap(scr, 32768 + (4 * g + hp) * SPAN, [[0, 32], [1, SPAN]])
                    dap = mk_ap(t1I[g], 32 * hp * 512, [[512, 32], [1, SPAN]])
                    out.append(sy.dma_start(out=dap, in_=sap))
            return out
        seq.dma("gpsimd", ft12, n=16)

        seq.dma("gpsimd", lambda sy: sy.dma_start(out=scr.ap()[2:3, 0:8 * NW], in_=dcf[:]))

        def fdcc(sy):
            out = []
            for pr in range(4):
                for h in range(2):
                    sap = mk_ap(scr, 2 * 32768 + (2 * pr + h) * NW, [[0, 64], [1, NW]])
                    dap = mk_ap(dccol[pr], 64 * h * NW, [[NW, 64], [1, NW]])
                    out.append(sy.dma_start(out=dap, in_=sap))
            return out
        seq.dma("gpsimd", fdcc, n=8)

        for w in range(NW):
            def fGall(t, w=w):
                for g in range(2):
                    for hp in range(4):
                        ktile = kf[2 * g + hp // 2]
                        qtile = qf[2 * g + hp // 2]
                        base = (hp % 2) * 64
                        t.matmul(psb[2 + 2 * g].ap()[32 * hp:32 * (hp + 1), 0:32],
                                 ktile.ap()[base:base + 64, 32 * w:32 * (w + 1)],
                                 ktile.ap()[base:base + 64, 32 * w:32 * (w + 1)],
                                 start=True, stop=True, tile_position=(base, 32 * hp))
                        t.matmul(psb[3 + 2 * g].ap()[32 * hp:32 * (hp + 1), 0:32],
                                 ktile.ap()[base:base + 64, 32 * w:32 * (w + 1)],
                                 qtile.ap()[base:base + 64, 32 * w:32 * (w + 1)],
                                 start=True, stop=True, tile_position=(base, 32 * hp))
            seq.c("tensor", fGall)

            def fDall(v, w=w):
                for g in range(2):
                    v.scalar_tensor_tensor(dsS[g][:], t1S[g].ap()[:, 32 * w:32 * (w + 1)],
                                           colstk[g].ap()[:, 8 * w:8 * w + 1], maskS[:],
                                           AL.subtract, AL.add)
                    v.scalar_tensor_tensor(dsI[g][:], t1I[g].ap()[:, 32 * w:32 * (w + 1)],
                                           colstk[g].ap()[:, 8 * w:8 * w + 1], maskI[:],
                                           AL.subtract, AL.add)
            seq.c("vector", fDall)

            def fEall(s, w=w):
                for g in range(2):
                    s.activation(esS[g][:], dsS[g][:], AF.Exp)
                    s.activation(esI[g][:], dsI[g][:], AF.Exp)
            seq.c("scalar", fEall)

            def fWMev(v, w=w):
                for g in range(2):
                    for hp in range(4):
                        v.scalar_tensor_tensor(
                            wbd[g].ap()[32 * hp:32 * (hp + 1), 32 * hp:32 * (hp + 1)],
                            psb[2 + 2 * g].ap()[32 * hp:32 * (hp + 1), 0:32],
                            colstk[g].ap()[32 * hp:32 * (hp + 1), 8 * w + 5:8 * w + 6],
                            esS[g].ap()[32 * hp:32 * (hp + 1), 0:32], AL.mult, AL.mult)
                        v.scalar_tensor_tensor(
                            mbd[g].ap()[32 * hp:32 * (hp + 1), 32 * hp:32 * (hp + 1)],
                            psb[3 + 2 * g].ap()[32 * hp:32 * (hp + 1), 0:32],
                            colstk[g].ap()[32 * hp:32 * (hp + 1), 8 * w + 1:8 * w + 2],
                            esI[g].ap()[32 * hp:32 * (hp + 1), 0:32], AL.mult, AL.mult)
            seq.c("vector", fWMev)

            def fRQS(t, w=w):
                for g in range(2):
                    for hp in range(4):
                        ktile = kf[2 * g + hp // 2]
                        qtile = qf[2 * g + hp // 2]
                        base = (hp % 2) * 64
                        st = s_st[2 * g + hp // 2]
                        t.matmul(psb[2 + 2 * g].ap()[32 * hp:32 * (hp + 1), 0:64],
                                 ktile.ap()[base:base + 64, 32 * w:32 * (w + 1)],
                                 st.ap()[base:base + 64, 0:64],
                                 start=True, stop=True, tile_position=(base, 32 * hp))
                        t.matmul(psb[3 + 2 * g].ap()[32 * hp:32 * (hp + 1), 0:64],
                                 qtile.ap()[base:base + 64, 32 * w:32 * (w + 1)],
                                 st.ap()[base:base + 64, 0:64],
                                 start=True, stop=True, tile_position=(base, 32 * hp))
            seq.c("tensor", fRQS)

            def fvryt(v, w=w):
                for g in range(2):
                    v.scalar_tensor_tensor(vrS[g].ap()[:, 0:64],
                                           psb[2 + 2 * g].ap()[:, 0:64],
                                           colstk[g].ap()[:, 8 * w + 2:8 * w + 3],
                                           vch[g].ap()[:, 64 * w:64 * (w + 1)],
                                           AL.mult, AL.add)
                    v.tensor_scalar(ych[g].ap()[:, 64 * w:64 * (w + 1)],
                                    psb[3 + 2 * g].ap()[:, 0:64],
                                    colstk[g].ap()[:, 8 * w + 4:8 * w + 5], None, AL.mult)
            seq.c("vector", fvryt)

            ch["pe_base"] = ch["pe"]
            ch["dv_base"] = ch["dv"]

            def chain_T2(t, w=w, pe0=ch["pe"], dv0=ch["dv"]):
                for st_i in range(32):
                    for g in range(2):
                        t.wait_ge(ch_dv, dv0 + 2 * (st_i - 1) + g + 1 if st_i > 0 else dv0)
                        mm = t.matmul(ps_chain[g].ap()[:, 0:64], wbd[g][:], u_t[g][:],
                                      start=True, stop=True)
                        mm.then_inc(ch_pe, 1)
                t.matmul(psb[6].ap()[0:1, 64:65], ones_col.ap()[0:1, 0:1],
                         ones_col.ap()[0:1, 0:1], start=True, stop=True)

            def chain_V2(v, w=w, pe0=ch["pe"], dv0=ch["dv"]):
                for st_i in range(32):
                    for g in range(2):
                        v.wait_ge(ch_pe, pe0 + 2 * st_i + g + 1)
                        v.tensor_tensor(tmpc[g].ap()[:, 0:64], ps_chain[g].ap()[:, 0:64],
                                        vrS[g].ap()[:, 0:64], AL.add)
                        uop = v.tensor_tensor(u_t[g][:], bch[g].ap()[:, 64 * w:64 * (w + 1)],
                                              tmpc[g].ap()[:, 0:64], AL.mult)
                        uop.then_inc(ch_dv, 1)
                v.tensor_copy(dsS[0].ap()[0:1, 0:1], dsS[0].ap()[0:1, 0:1])

            ch["pe"] += 64
            ch["dv"] += 64
            seq.raw({"tensor": chain_T2, "vector": chain_V2})

            def fMU2(t, w=w):
                for g in range(2):
                    t.matmul(psb[6 + g].ap()[:, 0:64], mbd[g][:], u_t[g][:],
                             start=True, stop=True)
            seq.c("tensor", fMU2)

            def fyaU(v, w=w):
                for g in range(2):
                    v.tensor_tensor(ych[g].ap()[:, 64 * w:64 * (w + 1)],
                                    ych[g].ap()[:, 64 * w:64 * (w + 1)],
                                    psb[6 + g].ap()[:, 0:64], AL.add)
                    v.tensor_scalar(tmpc[g].ap()[:, 0:64], u_t[g].ap()[:, 0:64],
                                    colstk[g].ap()[:, 8 * w + 3:8 * w + 4], None, AL.mult)
            seq.c("vector", fyaU)

            def fKU2(t, w=w):
                for g in range(2):
                    for hp in range(4):
                        dstps = psb[2 + 2 * g] if hp < 2 else psb[3 + 2 * g]
                        t.matmul(dstps.ap()[64 * (hp % 2):64 * (hp % 2) + 64, 0:64],
                                 ktm[g].ap()[32 * hp:32 * (hp + 1), 64 * w:64 * (w + 1)],
                                 tmpc[g].ap()[32 * hp:32 * (hp + 1), 0:64],
                                 start=True, stop=True,
                                 tile_position=(32 * hp, 64 * (hp % 2)))
            seq.c("tensor", fKU2)

            def fSu2(v, w=w):
                for g in range(2):
                    v.scalar_tensor_tensor(s_st[2 * g][:], s_st[2 * g][:],
                                           dccol[2 * g].ap()[:, w:w + 1],
                                           psb[2 + 2 * g].ap()[:, 0:64], AL.mult, AL.add)
                    v.scalar_tensor_tensor(s_st[2 * g + 1][:], s_st[2 * g + 1][:],
                                           dccol[2 * g + 1].ap()[:, w:w + 1],
                                           psb[3 + 2 * g].ap()[:, 0:64], AL.mult, AL.add)
            seq.c("vector", fSu2)

        if debug and sp == 0:
            seq.dma("gpsimd", lambda sy: sy.dma_start(out=dbg.tensor.ap()[:, 2448:3472],
                                                      in_=ych[0][:]))

        for g in range(2):
            for w in range(NW):
                def fct(t, g=g, w=w):
                    t.transpose(psb[5].ap()[0:64, 0:128], ych[g].ap()[:, 64 * w:64 * (w + 1)],
                                identt[:])
                seq.c("tensor", fct)

                def fce(v, g=g, w=w):
                    for hp in range(4):
                        h = 4 * g + hp
                        v.tensor_copy(ctxf[h // 2].ap()[(h % 2) * 64:(h % 2) * 64 + 64,
                                                        32 * w:32 * (w + 1)],
                                      psb[5].ap()[0:64, 32 * hp:32 * (hp + 1)])
                seq.c("vector", fce)

        proj_fm("wgT", vstage)

        def fsilu(s):
            load_act_set(s, nc, SET_SILU)
            for ot in range(4):
                s.activation(vstage[ot][:], vstage[ot][:], AF.Silu)
        seq.c("scalar", fsilu)

        for tile in range(4):
            def fsqy(s, tile=tile):
                load_act_set(s, nc, SET_EXPLN)
                s.activation(pf[:], ctxf[tile][:], AF.Square)
            seq.c("scalar", fsqy)

            def fnmy(t, tile=tile):
                t.matmul(psb[2].ap()[0:64, 0:512], sel.ap()[:, 64 * tile:64 * (tile + 1)],
                         pf[:], start=(tile == 0), stop=(tile == 3))
            seq.c("tensor", fnmy)

        def fyrs(v):
            v.tensor_copy(spf2[:], psb[2].ap()[0:8, 0:512])
            v.tensor_scalar(spf2[:], spf2[:], 1.0 / 64.0, 1e-6, AL.mult, AL.add)
        seq.c("vector", fyrs)

        def fyr0(s):
            s.activation(spf[:], spf2[:], AF.Ln)
            s.activation(spf[:], spf[:], AF.Exp, scale=-0.5)
        seq.c("scalar", fyr0)

        def fyrn(v):
            v.tensor_tensor(pf.ap()[0:8, 0:512], spf[:], spf[:], AL.mult)
            v.tensor_tensor(pf.ap()[0:8, 0:512], pf.ap()[0:8, 0:512], spf2[:], AL.mult)
            v.tensor_scalar(pf.ap()[0:8, 0:512], pf.ap()[0:8, 0:512], -0.5, 1.5, AL.mult, AL.add)
            v.tensor_tensor(spf2[:], spf[:], pf.ap()[0:8, 0:512], AL.mult)
        seq.c("vector", fyrn)

        seq.dma("gpsimd", lambda sy: sy.dma_start(out=scr.ap()[3:4, 0:8 * SPAN], in_=spf2[:]))

        def frr(sy):
            out = []
            for i in range(4):
                for hh in range(2):
                    dst_t = bch[i // 2]
                    sap = mk_ap(scr, 3 * 32768 + (2 * i + hh) * SPAN, [[0, 64], [1, SPAN]])
                    dap = mk_ap(dst_t, (i % 2) * 512 + hh * 64 * 1024, [[1024, 64], [1, SPAN]])
                    out.append(sy.dma_start(out=dap, in_=sap))
            return out
        seq.dma("gpsimd", frr, n=8)

        def fapply(v):
            for i in range(4):
                rep = bch[i // 2].ap()
                rep = mk_ap(bch[i // 2], (i % 2) * 512, [[1024, 128], [1, 512]])
                v.tensor_tensor(ctxf[i][:], ctxf[i][:], rep, AL.mult)
                v.tensor_tensor(ctxf[i][:], ctxf[i][:], vstage[i][:], AL.mult)
                v.tensor_scalar(ctxf[i][:], ctxf[i][:], nwfm[:], None, AL.mult)
        seq.c("vector", fapply)

        def fcout(sy, toff=toff):
            return [sy.dma_start(out=ctx_hbm.tensor.ap()[128 * i:128 * (i + 1),
                                                         toff:toff + SPAN],
                                 in_=ctxf[i][:]) for i in range(4)]
        seq.dma("gpsimd", fcout, n=4)

        if debug and sp == 0:
            dumps = [(caf[:], 0, 8, 0, 512), (knf[:], 0, 8, 3584, 4096),
                     (colstk[0][:], 0, 128, 512, 640), (t1S[0][:], 0, 128, 640, 1152),
                     (wbd[0][:], 0, 128, 1152, 1280), (u_t[0][:], 0, 128, 1280, 1344),
                     (ych[0][:], 0, 128, 1344, 2368), (s_st[0][:], 0, 128, 2368, 2432),
                     (vrS[0][:], 0, 128, 3472, 3536), (dccol[0][:], 0, 128, 3536, 3552)]
            for src_ap, p0, p1, c0, c1 in dumps:
                seq.dma("gpsimd", lambda sy, src_ap=src_ap, p0=p0, p1=p1, c0=c0, c1=c1:
                        sy.dma_start(out=dbg.tensor.ap()[p0:p1, c0:c1], in_=src_ap))

    LAST_MARKS["end"] = len(seq.items)
    if trunc is not None:
        seq.items = seq.items[:trunc]
    with (
        nc.semaphore("c_sem") as c_sem,
        nc.semaphore("d_sync") as d_sync,
        nc.semaphore("d_gp") as d_gp,
        nc.Block() as block,
    ):
        d_sems = {"sync": d_sync, "gpsimd": d_gp}
        n_items, n_dma = seq.build(block, c_sem, d_sems)

        @block.gpsimd
        def _(g):
            g.wait_ge(c_sem, n_items)
            for k, s in d_sems.items():
                g.wait_ge(s, 16 * n_dma[k])
    return nc



def build_mamba_a_phase(T_SEQ=4096, pre_ln=False):
    SPAN = 512
    NSP = T_SEQ // SPAN
    NCH = SPAN // 64
    NPC = 1216
    nc = bass.Bass("TRN2", target_bir_lowering=False, detect_race_conditions=False)
    x_in = nc.dram_tensor("x", [D, T_SEQ], F32, kind="ExternalInput").ap()
    inwT_in = nc.dram_tensor("inwT", [D, NPC], F32, kind="ExternalInput").ap()
    nw1_in = nc.dram_tensor("nw1", [128, 8], F32, kind="ExternalInput").ap()
    convw_in = nc.dram_tensor("convw", [128, 20], F32, kind="ExternalInput").ap()
    convb_in = nc.dram_tensor("convb", [128, 5], F32, kind="ExternalInput").ap()
    dtb_in = nc.dram_tensor("dtb", [8, 1], F32, kind="ExternalInput").ap()
    alog_in = nc.dram_tensor("alog", [8, 1], F32, kind="ExternalInput").ap()
    dcolf_in = nc.dram_tensor("dcolf", [128, 4], F32, kind="ExternalInput").ap()
    mask64_in = nc.dram_tensor("mask64", [128, 64], F32, kind="ExternalInput").ap()
    ident_in = nc.dram_tensor("ident", [128, 128], F32, kind="ExternalInput").ap()
    y_hbm = nc.dram_tensor("y", [512, T_SEQ], F32, kind="ExternalOutput").ap()
    z_hbm = nc.dram_tensor("z", [512, T_SEQ], F32, kind="ExternalOutput").ap()
    ln_hbm = nc.dram_tensor("ln0", [D, T_SEQ], F32, kind="ExternalOutput").ap() if pre_ln else None
    scr = nc.dram_tensor("scr_m", [8, 32768], F32, kind="Internal")

    A = nc.alloc_sbuf_tensor
    WH = A("hw", [128, 8 * NPC], BF16)
    WL = A("lw", [128, 8 * NPC], BF16)
    xt = [A(f"xt{i}", [128, SPAN], F32) for i in range(8)]
    wstage = xt[0]
    ph = [A(f"ph{i}", [128, SPAN], BF16) for i in range(8)]
    pl = [A(f"pl{i}", [128, SPAN], BF16) for i in range(8)]
    pf = A("pf", [128, SPAN], F32)
    sqs = A("sqs", [128, 512], F32)
    stat = A("stat", [1, 3 * SPAN], F32)
    nw1 = A("nw1c", [128, 8], F32)
    convw = A("convw_s", [128, 20], F32)
    convb = A("convb_s", [128, 5], F32)
    dtb = A("dtb_s", [8, 1], F32)
    nega = A("nega_s", [8, 1], F32)
    dcolf = A("dcolf_s", [128, 4], F32)
    mask64 = A("mask64_s", [128, 64], F32)
    identt = A("identt", [128, 128], F32)
    ones_col = A("ones_col", [128, 1], F32)
    ones_row = A("ones_row", [1, 128], F32)
    cvs = [A(f"cvs{i}", [128, SPAN + 3], F32) for i in range(5)]
    xfm = [A(f"xfm{i}", [128, SPAN], F32) for i in range(4)]
    bcfm = A("bcfm", [128, SPAN], F32)
    cC = A("cC", [128, SPAN], F32)
    xch = [A(f"xch{i}", [128, SPAN], F32) for i in range(4)]
    btm2 = A("btm2", [128, SPAN], F32)
    csrep = [A(f"csrep{i}", [128, SPAN], F32) for i in range(4)]
    colm = [A(f"colm{i}", [128, 8 * NCH], F32) for i in range(4)]
    dccm = [A(f"dccm{i}", [128, NCH], F32) for i in range(4)]
    yfm = [A(f"yfm{i}", [128, SPAN], F32) for i in range(4)]
    gts2 = A("gts2", [128, 64 * NCH], F32)
    mt = [A(f"mt{i}", [128, 64], F32) for i in range(4)]
    dscm = [A(f"dscm{i}", [128, 64], F32) for i in range(4)]
    escm = [A(f"escm{i}", [128, 64], F32) for i in range(4)]
    xsc = [A(f"xsc{i}", [128, 64], F32) for i in range(4)]
    ytp = [A(f"ytp{i}", [128, 64], F32) for i in range(4)]
    s_st = [A(f"sst{i}", [128, 64], F32) for i in range(4)]
    dtraw = A("dtraw", [8, SPAN], F32)
    dtf = A("dtf", [8, SPAN], F32)
    csr = A("csr", [8, SPAN], F32)
    decdt = A("decdt", [8, SPAN], F32)
    yod = A("yod", [8, SPAN], F32)
    rsc1 = A("rsc1", [8, SPAN], F32)
    rsc2 = A("rsc2", [8, SPAN], F32)
    dcfm = A("dcfm", [8, NCH], F32)
    cpm = A("cpm", [8, NCH], F32)

    psb = [nc.alloc_psum_tensor(f"psM{i}", [128, 512], F32) for i in range(6)]
    ps_Yd, ps_Yo, ps_proj, ps_G, ps_S, ps_T = psb[0], psb[1], psb[2], psb[3], psb[4], psb[5]

    seq = Seq(nc)

    def fconst(v):
        v.memset(ones_col[:], 1.0)
        v.memset(ones_row[:], 1.0)
        for i in range(4):
            v.memset(s_st[i][:], 0.0)
        for i in range(5):
            v.memset(cvs[i].ap()[:, 0:3], 0.0)
    seq.c("vector", fconst)
    for nm, dst in [("nw1", nw1), ("convw", convw), ("convb", convb), ("dtb", dtb),
                    ("alog", nega), ("dcolf", dcolf), ("mask64", mask64), ("ident", identt)]:
        seq.dma("sync", lambda s, nm=nm, dst=dst: s.dma_start(
            out=dst[:], in_=nc.lookup_tensor(nm).ap() if False else {"nw1": nw1_in,
            "convw": convw_in, "convb": convb_in, "dtb": dtb_in, "alog": alog_in,
            "dcolf": dcolf_in, "mask64": mask64_in, "ident": ident_in}[nm]))

    def fnega(s):
        load_act_set(s, nc, SET_EXPLN)
        s.activation(nega[:], nega[:], AF.Exp)
    seq.c("scalar", fnega)
    seq.c("vector", lambda v: v.tensor_scalar(nega[:], nega[:], -1.0, None, AL.mult))

    for dch in range(8):
        for pc in range(3):
            c0, c1 = 512 * pc, min(512 * (pc + 1), NPC)
            seq.dma("sync", lambda s, dch=dch, c0=c0, c1=c1: s.dma_start(
                out=wstage.ap()[:, 0:c1 - c0],
                in_=inwT_in.tensor.ap()[128 * dch:128 * (dch + 1), c0:c1]))

            def fsp_(v, dch=dch, c0=c0, c1=c1):
                v.tensor_copy(WH.ap()[:, NPC * dch + c0:NPC * dch + c1],
                              wstage.ap()[:, 0:c1 - c0])
                v.tensor_tensor(WL.ap()[:, NPC * dch + c0:NPC * dch + c1],
                                wstage.ap()[:, 0:c1 - c0],
                                WH.ap()[:, NPC * dch + c0:NPC * dch + c1], AL.subtract)
            seq.c("vector", fsp_)

    for sp in range(NSP):
        toff = SPAN * sp

        def fxl(s, toff=toff):
            return [s.dma_start(out=xt[i][:],
                                in_=x_in.tensor.ap()[128 * i:128 * (i + 1), toff:toff + SPAN])
                    for i in range(8)]
        seq.dma("sync", fxl, n=8)

        if pre_ln:
            ctx0 = dict(xt=xt, ph=ph, pl=pl, sq=sqs.ap()[:, 0:512], pf=pf, stat=stat,
                        ones_col=ones_col, ones_row=ones_row, eps=1e-5, out_f32=True,
                        ps_stA=psb[2].ap()[0:1, 0:512], ps_stB=psb[3].ap()[0:1, 0:512],
                        ps_rep=psb[4])
            emit_ln_prep(seq, nc, ctx0, SPAN, with_mean=True)
            def flno(sy, toff=toff):
                return [sy.dma_start(out=ln_hbm.tensor.ap()[128 * i:128 * (i + 1),
                                                            toff:toff + SPAN],
                                     in_=xt[i][:]) for i in range(8)]
            seq.dma("gpsimd", flno, n=8)
        ctxd = dict(xt=xt, ph=ph, pl=pl, sq=sqs.ap()[:, 0:512], pf=pf, stat=stat,
                    ones_col=ones_col, ones_row=ones_row, eps=1e-5,
                    wcol=[nw1.ap()[:, i:i + 1] for i in range(8)],
                    ps_stA=psb[2].ap()[0:1, 0:512], ps_stB=psb[3].ap()[0:1, 0:512],
                    ps_rep=psb[4])
        emit_ln_prep(seq, nc, ctxd, SPAN, with_mean=False)

        if sp > 0:
            def fcar(v):
                for i in range(5):
                    v.tensor_copy(cvs[i].ap()[:, 0:3], cvs[i].ap()[:, SPAN:SPAN + 3])
            seq.c("vector", fcar)

        def fproj(ot, dst_ap, m=128):
            def fmm(t, ot=ot, m=m):
                for dch in range(8):
                    wsh = WH.ap()[:, NPC * dch + 128 * ot:NPC * dch + 128 * ot + m]
                    wsl = WL.ap()[:, NPC * dch + 128 * ot:NPC * dch + 128 * ot + m]
                    t.matmul(ps_proj.ap()[0:m, 0:512], wsh, ph[dch][:], start=(dch == 0), stop=False)
                    t.matmul(ps_proj.ap()[0:m, 0:512], wsl, ph[dch][:], start=False, stop=False)
                    t.matmul(ps_proj.ap()[0:m, 0:512], wsh, pl[dch][:], start=False, stop=(dch == 7))
            seq.c("tensor", fmm)
            seq.c("vector", lambda v, dst_ap=dst_ap: v.tensor_copy(dst_ap, ps_proj.ap()[0:dst_ap.ap[0][1], 0:512]))

        for i in range(4):
            fproj(i, xt[i][:])

        def fzo(sy, toff=toff):
            return [sy.dma_start(out=z_hbm.tensor.ap()[128 * i:128 * (i + 1),
                                                       toff:toff + SPAN],
                                 in_=xt[i][:]) for i in range(4)]
        seq.dma("gpsimd", fzo, n=4)
        for i in range(4):
            fproj(4 + i, cvs[i].ap()[:, 3:SPAN + 3])
        fproj(8, cvs[4].ap()[:, 3:SPAN + 3])
        def fmmdt(t):
            for dch in range(8):
                wsh = WH.ap()[:, NPC * dch + 1152:NPC * dch + 1216]
                wsl = WL.ap()[:, NPC * dch + 1152:NPC * dch + 1216]
                t.matmul(ps_proj.ap()[0:64, 0:512], wsh, ph[dch][:], start=(dch == 0), stop=False)
                t.matmul(ps_proj.ap()[0:64, 0:512], wsl, ph[dch][:], start=False, stop=False)
                t.matmul(ps_proj.ap()[0:64, 0:512], wsh, pl[dch][:], start=False, stop=(dch == 7))
        seq.c("tensor", fmmdt)
        seq.c("vector", lambda v: v.tensor_copy(dtraw[:], ps_proj.ap()[0:8, 0:512]))

        def fconv(v):
            for i in range(5):
                out = xfm[i][:] if i < 4 else bcfm[:]
                v.tensor_scalar(out, cvs[i].ap()[:, 3:SPAN + 3], convw.ap()[:, 4 * i + 3:4 * i + 4],
                                convb.ap()[:, i:i + 1], AL.mult, AL.add)
                for tap in range(3):
                    v.scalar_tensor_tensor(out, cvs[i].ap()[:, tap:tap + SPAN],
                                           convw.ap()[:, 4 * i + tap:4 * i + tap + 1],
                                           out, AL.mult, AL.add)
        seq.c("vector", fconv)

        def fsilu(s):
            load_act_set(s, nc, SET_SILU)
            for i in range(4):
                s.activation(xfm[i][:], xfm[i][:], AF.Silu)
            s.activation(bcfm[:], bcfm[:], AF.Silu)
        seq.c("scalar", fsilu)

        def fdt1(v):
            v.tensor_scalar(dtraw[:], dtraw[:], dtb[:], None, AL.add)
            v.tensor_scalar_max(rsc1[:], dtraw[:], 0.0)
        seq.c("vector", fdt1)

        def fdt2(s):
            load_act_set(s, nc, SET_EXPLN)
            s.activation(dtf[:], dtraw[:], AF.Abs)
            s.activation(dtf[:], dtf[:], AF.Exp, scale=-1.0)
            s.activation(dtf[:], dtf[:], AF.Ln, bias=1.0)
        seq.c("scalar", fdt2)

        def fdt3(v):
            v.tensor_tensor(dtf[:], dtf[:], rsc1[:], AL.add)
            v.tensor_scalar(csr[:], dtf[:], nega[:], None, AL.mult)
            v.tensor_scalar(rsc1[:], rsc1[:], 0.0, 1.0, AL.mult, AL.add)
            v.tensor_tensor_scan(rsc2[:], rsc1[:], csr[:], 0.0, AL.mult, AL.add)
            v.tensor_copy(csr[:], rsc2[:])
            v.memset(cpm.ap()[0:8, 0:1], 0.0)
            v.tensor_copy(cpm.ap()[0:8, 1:NCH], mk_ap(csr, 63, [[SPAN, 8], [64, NCH - 1]]))
            v.tensor_tensor(decdt[:], csr[:], mk_ap(csr, 63, [[SPAN, 8], [64, NCH], [0, 64]]),
                            AL.subtract)
            v.tensor_tensor(yod[:], csr[:], mk_ap(cpm, 0, [[NCH, 8], [1, NCH], [0, 64]]),
                            AL.subtract)
        seq.c("vector", fdt3)

        def fdt4(s):
            s.activation(decdt[:], decdt[:], AF.Exp, scale=-1.0)
            s.activation(yod[:], yod[:], AF.Exp)
        seq.c("scalar", fdt4)

        def fdt5(v):
            v.tensor_tensor(decdt[:], decdt[:], dtf[:], AL.mult)
            v.tensor_copy(dcfm[:], mk_ap(yod, 63, [[SPAN, 8], [64, NCH]]))
        seq.c("vector", fdt5)

        def fst45(sy):
            return [sy.dma_start(out=scr.ap()[4:5, 0:8 * SPAN], in_=csr[:]),
                    sy.dma_start(out=scr.ap()[5:6, 0:8 * NCH], in_=dcfm[:])]
        seq.dma("gpsimd", fst45, n=2)

        def fcsdc(sy):
            out = []
            for i in range(4):
                for hh in range(2):
                    sap = mk_ap(scr, 4 * 32768 + (2 * i + hh) * SPAN, [[0, 64], [1, SPAN]])
                    dap = mk_ap(csrep[i], hh * 64 * SPAN, [[SPAN, 64], [1, SPAN]])
                    out.append(sy.dma_start(out=dap, in_=sap))
                    sap = mk_ap(scr, 5 * 32768 + (2 * i + hh) * NCH, [[0, 64], [1, NCH]])
                    dap = mk_ap(dccm[i], hh * 64 * NCH, [[NCH, 64], [1, NCH]])
                    out.append(sy.dma_start(out=dap, in_=sap))
            return out
        seq.dma("gpsimd", fcsdc, n=16)

        def fcc(sy):
            sap = mk_ap(bcfm, 64 * SPAN, [[SPAN, 64], [1, SPAN]])
            return [sy.dma_start(out=mk_ap(cC, 0, [[SPAN, 64], [1, SPAN]]), in_=sap),
                    sy.dma_start(out=mk_ap(cC, 64 * SPAN, [[SPAN, 64], [1, SPAN]]), in_=sap)]
        seq.dma("gpsimd", fcc, n=2)

        tmk = [0]

        def emit_tmstage(src_tile, full=False):
            LAGM = None if full else {"gpsimd": 12}
            stg = sqs if tmk[0] % 2 == 0 else pf
            tmk[0] += 1

            def ftr(t, src_tile=src_tile):
                for blk in range(4):
                    t.transpose(ps_T.ap()[:, 128 * blk:128 * (blk + 1)],
                                src_tile.ap()[:, 128 * blk:128 * (blk + 1)], identt[:])
            seq.c("tensor", ftr, dma_lag=LAGM)
            seq.c("vector", lambda v, stg=stg: v.tensor_copy(stg.ap()[:, 0:512],
                                                             ps_T.ap()[:, 0:512]),
                  dma_lag=LAGM)

            def fst(sy, stg=stg):
                return [sy.dma_start(out=scr.ap()[blk:blk + 1, 0:16384],
                                     in_=stg.ap()[:, 128 * blk:128 * (blk + 1)])
                        for blk in range(4)]
            seq.dma("gpsimd", fst, n=4)

        for i in range(4):
            emit_tmstage(xfm[i], full=(i == 0))

            def fg(sy, i=i):
                out = []
                for blk in range(4):
                    for slot in range(2):
                        sap = mk_ap(scr, blk * 32768 + slot * 64,
                                    [[128, 64], [64 * 128, 2], [1, 64]])
                        dap = mk_ap(xch[i], slot * 64 * SPAN + blk * 128,
                                    [[SPAN, 64], [64, 2], [1, 64]])
                        out.append(sy.dma_start(out=dap, in_=sap))
                return out
            seq.dma("gpsimd", fg, n=8)

        emit_tmstage(bcfm)

        def fgb(sy):
            out = []
            for blk in range(4):
                for dup in range(2):
                    sap = mk_ap(scr, blk * 32768, [[128, 64], [64 * 128, 2], [1, 64]])
                    dap = mk_ap(btm2, dup * 64 * SPAN + blk * 128,
                                [[SPAN, 64], [64, 2], [1, 64]])
                    out.append(sy.dma_start(out=dap, in_=sap))
            return out
        seq.dma("gpsimd", fgb, n=8)

        def frc(sy):
            out = []
            for i in range(4):
                for q, src_fm in enumerate([csr, decdt, dtf, yod]):
                    for h in range(2):
                        sap = mk_ap(src_fm, (2 * i + h) * SPAN, [[SPAN, 1], [1, SPAN]])
                        dap = mk_ap(xt[i], (64 * h + q) * SPAN, [[SPAN, 1], [1, SPAN]])
                        out.append(sy.dma_start(out=dap, in_=sap))
            return out
        seq.dma("gpsimd", frc, n=32)
        for i in range(4):
            emit_tmstage(xt[i], full=(i == 0))

            def fgc(sy, i=i):
                out = []
                for blk in range(4):
                    for slot in range(2):
                        sap = mk_ap(scr, blk * 32768 + slot * 64,
                                    [[128, 64], [64 * 128, 2], [1, 8]])
                        dap = mk_ap(colm[i], slot * 64 * (8 * NCH) + blk * 16,
                                    [[8 * NCH, 64], [8, 2], [1, 8]])
                        out.append(sy.dma_start(out=dap, in_=sap))
                return out
            seq.dma("gpsimd", fgc, n=8)

        for c in range(NCH):
            def fGt(t, c=c):
                t.matmul(ps_G.ap()[0:64, 0:64], bcfm.ap()[0:64, 64 * c:64 * (c + 1)],
                         cC.ap()[0:64, 64 * c:64 * (c + 1)], start=True, stop=True)
            seq.c("tensor", fGt)
            seq.c("vector", lambda v, c=c: v.tensor_copy(gts2.ap()[0:64, 64 * c:64 * (c + 1)],
                                                         ps_G.ap()[0:64, 0:64]))

            def fgdup(sy, c=c):
                sap = mk_ap(gts2, 64 * c, [[64 * NCH, 64], [1, 64]])
                dap = mk_ap(gts2, 64 * 64 * NCH + 64 * c, [[64 * NCH, 64], [1, 64]])
                return sy.dma_start(out=dap, in_=sap)
            seq.dma("gpsimd", fgdup)

            def fL(v, c=c):
                for i in range(4):
                    v.scalar_tensor_tensor(dscm[i][:], csrep[i].ap()[:, 64 * c:64 * (c + 1)],
                                           colm[i].ap()[:, 8 * c:8 * c + 1], mask64[:],
                                           AL.subtract, AL.add)
            seq.c("vector", fL)

            def fEx(s, c=c):
                for i in range(4):
                    s.activation(escm[i][:], dscm[i][:], AF.Exp)
            seq.c("scalar", fEx)

            def fM(v, c=c):
                for i in range(4):
                    v.scalar_tensor_tensor(mt[i][:], gts2.ap()[:, 64 * c:64 * (c + 1)],
                                           colm[i].ap()[:, 8 * c + 2:8 * c + 3], escm[i][:],
                                           AL.mult, AL.mult)
                    v.tensor_scalar(xsc[i][:], xch[i].ap()[:, 64 * c:64 * (c + 1)],
                                    colm[i].ap()[:, 8 * c + 1:8 * c + 2], None, AL.mult)
            seq.c("vector", fM)

            def fmms(t, c=c):
                for i in range(4):
                    co = 64 * i
                    for sl in range(2):
                        b2 = 64 * sl
                        t.matmul(ps_Yd.ap()[b2:b2 + 64, co:co + 64], mt[i].ap()[b2:b2 + 64, 0:64],
                                 xch[i].ap()[b2:b2 + 64, 64 * c:64 * (c + 1)],
                                 start=True, stop=True, tile_position=(b2, b2))
                        t.matmul(ps_Yo.ap()[b2:b2 + 64, co:co + 64],
                                 cC.ap()[b2:b2 + 64, 64 * c:64 * (c + 1)],
                                 s_st[i].ap()[b2:b2 + 64, 0:64],
                                 start=True, stop=True, tile_position=(b2, b2))
                        t.matmul(ps_S.ap()[b2:b2 + 64, co:co + 64],
                                 btm2.ap()[b2:b2 + 64, 64 * c:64 * (c + 1)],
                                 xsc[i].ap()[b2:b2 + 64, 0:64],
                                 start=True, stop=True, tile_position=(b2, b2))
            seq.c("tensor", fmms)

            def fyev(v, c=c):
                for i in range(4):
                    co = 64 * i
                    v.tensor_copy(ytp[i][:], ps_Yd.ap()[:, co:co + 64])
                    v.scalar_tensor_tensor(ytp[i][:], ps_Yo.ap()[:, co:co + 64],
                                           colm[i].ap()[:, 8 * c + 3:8 * c + 4],
                                           ytp[i][:], AL.mult, AL.add)
                    v.scalar_tensor_tensor(ytp[i][:], xch[i].ap()[:, 64 * c:64 * (c + 1)],
                                           dcolf.ap()[:, i:i + 1], ytp[i][:], AL.mult, AL.add)
                    v.scalar_tensor_tensor(s_st[i][:], s_st[i][:],
                                           dccm[i].ap()[:, c:c + 1],
                                           ps_S.ap()[:, co:co + 64], AL.mult, AL.add)
            seq.c("vector", fyev)

            def fytr(t, c=c):
                for i in range(4):
                    co = 64 * i
                    for sl in range(2):
                        b2 = 64 * sl
                        t.matmul(ps_T.ap()[b2:b2 + 64, co:co + 64], ytp[i].ap()[b2:b2 + 64, 0:64],
                                 identt.ap()[b2:b2 + 64, b2:b2 + 64],
                                 start=True, stop=True, tile_position=(b2, b2))
            seq.c("tensor", fytr)

            def fyfm(v, c=c):
                for i in range(4):
                    co = 64 * i
                    for sl in range(2):
                        b2 = 64 * sl
                        v.tensor_copy(yfm[i].ap()[b2:b2 + 64, 64 * c:64 * (c + 1)],
                                      ps_T.ap()[b2:b2 + 64, co:co + 64])
            seq.c("vector", fyfm)

        def fyo(sy, toff=toff):
            return [sy.dma_start(out=y_hbm.tensor.ap()[128 * i:128 * (i + 1),
                                                       toff:toff + SPAN],
                                 in_=yfm[i][:]) for i in range(4)]
        seq.dma("gpsimd", fyo, n=4)

    with (
        nc.semaphore("c_sem") as c_sem,
        nc.semaphore("d_sync") as d_sync,
        nc.semaphore("d_gp") as d_gp,
        nc.Block() as block,
    ):
        d_sems = {"sync": d_sync, "gpsimd": d_gp}
        n_items, n_dma = seq.build(block, c_sem, d_sems)

        @block.gpsimd
        def _(g):
            g.wait_ge(c_sem, n_items)
            for k, s in d_sems.items():
                g.wait_ge(s, 16 * n_dma[k])
    return nc



def build_mamba_out_phase(TQ=1024, add_outer=False):
    SPAN = 512
    NSP = TQ // SPAN
    nc = bass.Bass("TRN2", target_bir_lowering=False, detect_race_conditions=False)
    x_in = nc.dram_tensor("x", [D, TQ], F32, kind="ExternalInput").ap()
    y_in = nc.dram_tensor("yf", [2048, TQ], F32, kind="ExternalInput").ap()
    z_in = nc.dram_tensor("zf", [2048, TQ], F32, kind="ExternalInput").ap()
    gw_in = nc.dram_tensor("gwc", [128, 16], F32, kind="ExternalInput").ap()
    owT_in = nc.dram_tensor("owT", [2048, 1024], F32, kind="ExternalInput").ap()
    xo_in = nc.dram_tensor("xo", [D, TQ], F32, kind="ExternalInput").ap() if add_outer else None
    out_hbm = nc.dram_tensor("out", [D, TQ], F32, kind="ExternalOutput").ap()

    A = nc.alloc_sbuf_tensor
    WH = A("how", [128, 16 * 1024], BF16)
    WL = A("low", [128, 16 * 1024], BF16)
    gt = [A(f"gt{i}", [128, SPAN], F32) for i in range(16)]
    wstage = gt[0]
    zst = A("zst", [128, SPAN], F32)
    xt = [A(f"xt{i}", [128, SPAN], F32) for i in range(8)]
    gh = [A(f"gh{i}", [128, SPAN], BF16) for i in range(16)]
    gl = [A(f"gl{i}", [128, SPAN], BF16) for i in range(16)]
    pf = A("pf", [128, SPAN], F32)
    sqs = A("sqs", [128, 512], F32)
    stat = A("stat", [1, 2 * SPAN], F32)
    gwc = A("gwc_s", [128, 16], F32)
    ones_col = A("ones_col", [128, 1], F32)
    ones_row = A("ones_row", [1, 128], F32)

    psb = [nc.alloc_psum_tensor(f"psO{i}", [128, 512], F32) for i in range(4)]
    ps_mm, ps_st, ps_rep = psb[0], psb[1], psb[2]

    seq = Seq(nc)
    seq.c("vector", lambda v: (v.memset(ones_col[:], 1.0), v.memset(ones_row[:], 1.0)))
    seq.dma("sync", lambda s: s.dma_start(out=gwc[:], in_=gw_in))
    for dch in range(16):
        for pc in range(2):
            c0 = 512 * pc
            seq.dma("sync", lambda s, dch=dch, c0=c0: s.dma_start(
                out=wstage.ap()[:, 0:512],
                in_=owT_in.tensor.ap()[128 * dch:128 * (dch + 1), c0:c0 + 512]))

            def fsp_(v, dch=dch, c0=c0):
                sl = slice(1024 * dch + c0, 1024 * dch + c0 + 512)
                v.tensor_copy(WH.ap()[:, sl], wstage.ap()[:, 0:512])
                v.tensor_tensor(WL.ap()[:, sl], wstage.ap()[:, 0:512], WH.ap()[:, sl],
                                AL.subtract)
            seq.c("vector", fsp_)

    for sp in range(NSP):
        toff = SPAN * sp

        def fxl(s, toff=toff):
            out = [s.dma_start(out=xt[i][:],
                               in_=x_in.tensor.ap()[128 * i:128 * (i + 1), toff:toff + SPAN])
                   for i in range(8)]
            out += [s.dma_start(out=gt[i][:],
                                in_=y_in.tensor.ap()[128 * i:128 * (i + 1), toff:toff + SPAN])
                    for i in range(16)]
            return out
        seq.dma("sync", fxl, n=24)
        for i in range(16):
            seq.dma("sync", lambda s, i=i, toff=toff: s.dma_start(
                out=zst[:], in_=z_in.tensor.ap()[128 * i:128 * (i + 1), toff:toff + SPAN]))

            def fg(sv, i=i):
                load_act_set(sv, nc, SET_SILU)
                sv.activation(zst[:], zst[:], AF.Silu)
            seq.c("scalar", fg)
            seq.c("vector", lambda v, i=i: v.tensor_tensor(gt[i][:], gt[i][:], zst[:], AL.mult))

        seq.c("scalar", lambda s: load_act_set(s, nc, SET_EXPLN))
        for i in range(16):
            seq.c("scalar", lambda s, i=i: s.activation(sqs.ap()[:, 0:512], gt[i][:], AF.Square))
            seq.c("tensor", lambda t, i=i: t.matmul(ps_st.ap()[0:1, 0:512], ones_col[:],
                                                    sqs.ap()[:, 0:512],
                                                    start=(i == 0), stop=(i == 15)))

        def fvar(v):
            v.tensor_copy(stat.ap()[0:1, 0:SPAN], ps_st.ap()[0:1, 0:512])
            v.tensor_scalar(stat.ap()[0:1, 0:SPAN], stat.ap()[0:1, 0:SPAN],
                            1.0 / 2048.0, 1e-5, AL.mult, AL.add)
        seq.c("vector", fvar)

        def frs(s):
            s.activation(stat.ap()[0:1, SPAN:2 * SPAN], stat.ap()[0:1, 0:SPAN], AF.Ln)
            s.activation(stat.ap()[0:1, SPAN:2 * SPAN], stat.ap()[0:1, SPAN:2 * SPAN],
                         AF.Exp, scale=-0.5)
        seq.c("scalar", frs)

        def fnw(v):
            v.tensor_tensor(pf.ap()[0:1, 0:SPAN], stat.ap()[0:1, SPAN:2 * SPAN],
                            stat.ap()[0:1, SPAN:2 * SPAN], AL.mult)
            v.tensor_tensor(pf.ap()[0:1, 0:SPAN], pf.ap()[0:1, 0:SPAN],
                            stat.ap()[0:1, 0:SPAN], AL.mult)
            v.tensor_scalar(pf.ap()[0:1, 0:SPAN], pf.ap()[0:1, 0:SPAN], -0.5, 1.5,
                            AL.mult, AL.add)
            v.tensor_tensor(stat.ap()[0:1, 0:SPAN], stat.ap()[0:1, SPAN:2 * SPAN],
                            pf.ap()[0:1, 0:SPAN], AL.mult)
        seq.c("vector", fnw)
        seq.c("tensor", lambda t: t.matmul(ps_rep[:], ones_row[:], stat.ap()[0:1, 0:SPAN],
                                           start=True, stop=True))
        for i in range(16):
            def fsc(v, i=i):
                v.tensor_tensor(gt[i][:], gt[i][:], ps_rep.ap()[:], AL.mult)
                v.tensor_scalar(gt[i][:], gt[i][:], gwc.ap()[:, i:i + 1], None, AL.mult)
                v.tensor_copy(gh[i][:], gt[i][:])
                v.tensor_tensor(gl[i][:], gt[i][:], gh[i][:], AL.subtract)
            seq.c("vector", fsc)

        for ot in range(8):
            def fmm(t, ot=ot):
                for dch in range(16):
                    wsh = WH.ap()[:, 1024 * dch + 128 * ot:1024 * dch + 128 * (ot + 1)]
                    wsl = WL.ap()[:, 1024 * dch + 128 * ot:1024 * dch + 128 * (ot + 1)]
                    t.matmul(ps_mm[:], wsh, gh[dch][:], start=(dch == 0), stop=False)
                    t.matmul(ps_mm[:], wsl, gh[dch][:], start=False, stop=False)
                    t.matmul(ps_mm[:], wsh, gl[dch][:], start=False, stop=(dch == 15))
            seq.c("tensor", fmm)
            seq.c("vector", lambda v, ot=ot: v.tensor_tensor(xt[ot][:], xt[ot][:],
                                                             ps_mm.ap()[:], AL.add))
            if add_outer:
                seq.dma("sync", lambda s, ot=ot, toff=toff: s.dma_start(
                    out=zst[:], in_=xo_in.tensor.ap()[128 * ot:128 * (ot + 1),
                                                      toff:toff + SPAN]))
                seq.c("vector", lambda v, ot=ot: v.tensor_tensor(xt[ot][:], xt[ot][:],
                                                                 zst[:], AL.add))
            seq.dma("gpsimd", lambda sy, ot=ot, toff=toff: sy.dma_start(
                out=out_hbm.tensor.ap()[128 * ot:128 * (ot + 1), toff:toff + SPAN],
                in_=xt[ot][:]))

    with (
        nc.semaphore("c_sem") as c_sem,
        nc.semaphore("d_sync") as d_sync,
        nc.semaphore("d_gp") as d_gp,
        nc.Block() as block,
    ):
        d_sems = {"sync": d_sync, "gpsimd": d_gp}
        n_items, n_dma = seq.build(block, c_sem, d_sems)

        @block.gpsimd
        def _(g):
            g.wait_ge(c_sem, n_items)
            for k, s in d_sems.items():
                g.wait_ge(s, 16 * n_dma[k])
    return nc


# version 08
# speedup vs baseline: 1.6069x; 1.0098x over previous
import numpy as np
import ml_dtypes

import concourse.bass as bass
import concourse.mybir as mybir
from concourse.bass_utils import run_bass_kernel_spmd

F32 = mybir.dt.float32
BF16 = mybir.dt.bfloat16
AL = mybir.AluOpType
AF = mybir.ActivationFunctionType

SET_EXPLN = 6
SET_SIGMOID = 2
SET_GELU = 10
SET_SILU = 18

D = 1024
ENGINES = ("tensor", "vector", "scalar", "gpsimd", "sync")
FEV_MODE = "stt"


def load_act_set(scalar, nc, idx):
    scalar.add_instruction(mybir.InstLoadActFuncSet(
        name=nc.get_next_instruction_name(), act_func_set_id=idx, ins=[], outs=[]))
    scalar.drain()


class _EngProxy:

    def __init__(self, eng):
        object.__setattr__(self, "_eng", eng)
        object.__setattr__(self, "last", None)

    def __getattr__(self, name):
        attr = getattr(self._eng, name)
        if not callable(attr):
            return attr

        def wrapper(*a, **k):
            r = attr(*a, **k)
            if r is not None and hasattr(r, "then_inc"):
                object.__setattr__(self, "last", r)
            return r
        return wrapper


class Seq:

    def __init__(self, nc):
        self.nc = nc
        self.items = []

    def c(self, eng, fn, dma_lag=None):
        self.items.append((eng, fn, ("c2", dma_lag) if dma_lag else "c"))

    def dma(self, eng, fn, n=1):
        self.items.append((eng, fn, ("d", n)))

    def raw(self, eng_fns):
        base = len(self.items)
        for eng, fn in eng_fns.items():
            self.items.append((eng, fn, ("r", base)))

    def build(self, block, c_sem, d_sems):
        counts = {k: 0 for k in d_sems}
        ndma_at = []
        for (e, _, kind) in self.items:
            ndma_at.append(dict(counts))
            if isinstance(kind, tuple) and kind[0] == "d":
                counts[e] += kind[1]

        def make_runner(name):
            def runner(eng):
                last_dwait = {k: 0 for k in d_sems}
                prev_idx = -2
                for i, (e, fn, kind) in enumerate(self.items):
                    if e != name:
                        continue
                    if isinstance(kind, tuple) and kind[0] == "r":
                        eng.wait_ge(c_sem, kind[1])
                        for k, n in ndma_at[i].items():
                            if n > last_dwait[k]:
                                eng.wait_ge(d_sems[k], 16 * n)
                                last_dwait[k] = n
                        proxy = _EngProxy(eng)
                        fn(proxy)
                        assert proxy.last is not None
                        proxy.last.then_inc(c_sem, 1)
                        prev_idx = i
                        continue
                    eng.wait_ge(c_sem, i)
                    lag = kind[1] if isinstance(kind, tuple) and kind[0] == "c2" else {}
                    for k, n in ndma_at[i].items():
                        n_eff = max(0, n - lag.get(k, 0))
                        if n_eff > last_dwait[k]:
                            eng.wait_ge(d_sems[k], 16 * n_eff)
                            last_dwait[k] = n_eff
                    if isinstance(kind, tuple) and kind[0] == "d":
                        ins = fn(eng)
                        if not isinstance(ins, (list, tuple)):
                            ins = [ins]
                        assert len(ins) == kind[1], f"item {i}: {len(ins)} != n={kind[1]}"
                        for q in ins:
                            q.then_inc(d_sems[e], 16)
                        eng.nop().then_inc(c_sem, 1)
                    else:
                        proxy = _EngProxy(eng)
                        fn(proxy)
                        assert proxy.last is not None, f"item {i} emitted no instruction"
                        proxy.last.then_inc(c_sem, 1)
                    prev_idx = i
            return runner

        for name in ENGINES:
            getattr(block, name)(make_runner(name))
        return len(self.items), dict(counts)



def emit_ln_prep(seq, nc, ctx, T, with_mean):
    xt, ph, pl = ctx["xt"], ctx["ph"], ctx["pl"]
    sq, pf, stat = ctx["sq"], ctx["pf"], ctx["stat"]
    ones_col, ones_row = ctx["ones_col"], ctx["ones_row"]
    ps_stA, ps_stB, ps_rep = ctx["ps_stA"], ctx["ps_stB"], ctx["ps_rep"]
    eps_t = ctx.get("eps_t")
    nts = T // 512

    seq.c("scalar", lambda s: load_act_set(s, nc, SET_EXPLN))
    for ts in range(nts):
        sl = slice(512 * ts, 512 * (ts + 1))
        for i in range(8):
            def fsq(s, i=i, sl=sl):
                s.activation(sq, xt[i].ap()[:, sl], AF.Square)
            seq.c("scalar", fsq)

            def fmm(t, i=i, sl=sl):
                t.matmul(ps_stB, ones_col[:], sq,
                         start=(i == 0), stop=(i == 7))
                t.matmul(ps_stA, ones_col[:], xt[i].ap()[:, sl],
                         start=(i == 0), stop=(i == 7))
            seq.c("tensor", fmm)

        def fst(v, ts=ts):
            v.tensor_copy(stat.ap()[0:1, 512 * ts:512 * (ts + 1)], ps_stA)
            v.tensor_copy(stat.ap()[0:1, T + 512 * ts:T + 512 * (ts + 1)], ps_stB)
        seq.c("vector", fst)

    def fmu(v):
        if with_mean:
            v.tensor_scalar(stat.ap()[0:1, 0:T], stat.ap()[0:1, 0:T],
                            1.0 / 1024.0, None, AL.mult)
        else:
            v.memset(stat.ap()[0:1, 0:T], 0.0)
        v.tensor_scalar(stat.ap()[0:1, T:2 * T], stat.ap()[0:1, T:2 * T],
                        1.0 / 1024.0, None, AL.mult)
        v.tensor_tensor(pf.ap()[0:1, 0:T], stat.ap()[0:1, 0:T],
                        stat.ap()[0:1, 0:T], AL.mult)
        v.tensor_tensor(stat.ap()[0:1, T:2 * T], stat.ap()[0:1, T:2 * T],
                        pf.ap()[0:1, 0:T], AL.subtract)
        v.tensor_scalar(stat.ap()[0:1, T:2 * T], stat.ap()[0:1, T:2 * T],
                        float(ctx.get("eps", 1e-5)), None, AL.add)
    seq.c("vector", fmu)

    def frs(s):
        s.activation(stat.ap()[0:1, 2 * T:3 * T], stat.ap()[0:1, T:2 * T], AF.Ln)
        s.activation(stat.ap()[0:1, 2 * T:3 * T], stat.ap()[0:1, 2 * T:3 * T], AF.Exp, scale=-0.5)
    seq.c("scalar", frs)

    def fnewton(v):
        v.tensor_tensor(pf.ap()[0:1, 0:T], stat.ap()[0:1, 2 * T:3 * T],
                        stat.ap()[0:1, 2 * T:3 * T], AL.mult)
        v.tensor_tensor(pf.ap()[0:1, 0:T], pf.ap()[0:1, 0:T],
                        stat.ap()[0:1, T:2 * T], AL.mult)
        v.tensor_scalar(pf.ap()[0:1, 0:T], pf.ap()[0:1, 0:T],
                        -0.5, 1.5, AL.mult, AL.add)
        v.tensor_tensor(stat.ap()[0:1, T:2 * T], stat.ap()[0:1, 2 * T:3 * T],
                        pf.ap()[0:1, 0:T], AL.mult)
    seq.c("vector", fnewton)

    ps_rep2 = ctx.get("ps_rep2")
    for i in range(8):
        for ts in range(nts):
            sl = slice(512 * ts, 512 * (ts + 1))
            if ps_rep2 is not None:
                def frep2(t, ts=ts):
                    if with_mean:
                        t.matmul(ps_rep[:], ones_row[:],
                                 stat.ap()[0:1, 512 * ts:512 * (ts + 1)],
                                 start=True, stop=True)
                    t.matmul(ps_rep2[:], ones_row[:],
                             stat.ap()[0:1, T + 512 * ts:T + 512 * (ts + 1)],
                             start=True, stop=True)
                seq.c("tensor", frep2)

                def fpb(v, i=i, sl=sl, ts=ts):
                    if with_mean:
                        v.tensor_tensor(pf.ap()[:, sl], xt[i].ap()[:, sl],
                                        ps_rep.ap()[:], AL.subtract)
                        v.tensor_tensor(pf.ap()[:, sl], pf.ap()[:, sl],
                                        ps_rep2.ap()[:], AL.mult)
                    else:
                        v.tensor_tensor(pf.ap()[:, sl], xt[i].ap()[:, sl],
                                        ps_rep2.ap()[:], AL.mult)
                    if ctx.get("wcol") is not None:
                        v.tensor_scalar(pf.ap()[:, sl], pf.ap()[:, sl], ctx["wcol"][i],
                                        None, AL.mult)
                    if ctx.get("out_f32"):
                        v.tensor_copy(xt[i].ap()[:, sl], pf.ap()[:, sl])
                    else:
                        v.tensor_copy(ph[i].ap()[:, sl], pf.ap()[:, sl])
                        v.tensor_tensor(pl[i].ap()[:, sl], pf.ap()[:, sl],
                                        ph[i].ap()[:, sl], AL.subtract)
                seq.c("vector", fpb)
                continue
            if with_mean:
                def frepmu(t, ts=ts):
                    t.matmul(ps_rep[:], ones_row[:],
                             stat.ap()[0:1, 512 * ts:512 * (ts + 1)],
                             start=True, stop=True)
                seq.c("tensor", frepmu)

                def fsub(v, i=i, sl=sl):
                    v.tensor_tensor(pf.ap()[:, sl], xt[i].ap()[:, sl],
                                    ps_rep.ap()[:], AL.subtract)
                seq.c("vector", fsub)

            def freprs(t, ts=ts):
                t.matmul(ps_rep[:], ones_row[:],
                         stat.ap()[0:1, T + 512 * ts:T + 512 * (ts + 1)],
                         start=True, stop=True)
            seq.c("tensor", freprs)

            def fmul(v, i=i, sl=sl):
                src0 = pf.ap()[:, sl] if with_mean else xt[i].ap()[:, sl]
                v.tensor_tensor(pf.ap()[:, sl], src0, ps_rep.ap()[:], AL.mult)
                if ctx.get("wcol") is not None:
                    v.tensor_scalar(pf.ap()[:, sl], pf.ap()[:, sl], ctx["wcol"][i],
                                    None, AL.mult)
            seq.c("vector", fmul)

        if ps_rep2 is None:
            def fcast(v, i=i):
                if ctx.get("out_f32"):
                    v.tensor_copy(xt[i].ap()[:, 0:T], pf.ap()[:, 0:T])
                else:
                    v.tensor_copy(ph[i].ap()[:, 0:T], pf.ap()[:, 0:T])
                    v.tensor_tensor(pl[i].ap()[:, 0:T], pf.ap()[:, 0:T], ph[i].ap()[:, 0:T],
                                    AL.subtract)
            seq.c("vector", fcast)



def build_ffn_phase(with_wo, max_items=None):
    T = 2048
    NT = 256
    nc = bass.Bass("TRN2", target_bir_lowering=False, detect_race_conditions=False)
    x_in = nc.dram_tensor("x", [D, T], F32, kind="ExternalInput").ap()
    if with_wo:
        ctx_in = nc.dram_tensor("ctx", [D, T], F32, kind="ExternalInput").ap()
        woT_in = nc.dram_tensor("woT", [D, D], F32, kind="ExternalInput").ap()
        bo_in = nc.dram_tensor("bo", [128, 8], F32, kind="ExternalInput").ap()
    w1T_in = nc.dram_tensor("w1T", [D, 4096], F32, kind="ExternalInput").ap()
    b1_in = nc.dram_tensor("b1", [128, 32], F32, kind="ExternalInput").ap()
    w2T_in = nc.dram_tensor("w2T", [4096, D], F32, kind="ExternalInput").ap()
    b2_in = nc.dram_tensor("b2", [128, 8], F32, kind="ExternalInput").ap()
    out_hbm = nc.dram_tensor("out", [D, T], F32, kind="ExternalOutput").ap()
    dbg_hbm = nc.dram_tensor("dbg", [4, 2048], F32, kind="ExternalOutput").ap()
    s1_scr = nc.dram_tensor("s1_scr", [D, 4096], BF16, kind="Internal").ap()
    s2_scr = nc.dram_tensor("s2_scr", [4096, D], BF16, kind="Internal").ap()

    xt = [nc.alloc_sbuf_tensor(f"xt{i}", [128, T], F32) for i in range(8)]
    bp = [nc.alloc_sbuf_tensor(f"bp{i}", [128, T], BF16) for i in range(24)]
    ph, pl = bp[0:8], bp[8:16]
    hh, hl = bp[16:20], bp[20:24]
    pf = nc.alloc_sbuf_tensor("pf", [128, T], F32)
    wstage = nc.alloc_sbuf_tensor("wstage", [128, 1024], F32)
    wbh = nc.alloc_sbuf_tensor("wbh", [128, 2048], BF16)
    wbl = nc.alloc_sbuf_tensor("wbl", [128, 2048], BF16)
    ones_col = nc.alloc_sbuf_tensor("ones_col", [128, 1], F32)
    ones_row = nc.alloc_sbuf_tensor("ones_row", [1, 128], F32)
    stat = nc.alloc_sbuf_tensor("stat", [1, 3 * T], F32)
    eps_t = nc.alloc_sbuf_tensor("eps_t", [1, 1], F32)
    alpha1 = nc.alloc_sbuf_tensor("alpha1", [128, 32], F32)
    b1c = nc.alloc_sbuf_tensor("b1c", [128, 32], F32)
    alpha2 = nc.alloc_sbuf_tensor("alpha2", [128, 8], F32)
    b2c = nc.alloc_sbuf_tensor("b2c", [128, 8], F32)
    boc = nc.alloc_sbuf_tensor("boc", [128, 8], F32) if with_wo else None
    hsc = nc.alloc_sbuf_tensor("hsc", [128, 512], F32)
    sq = hsc.ap()[:, 0:512]
    asum2 = nc.alloc_sbuf_tensor("asum2", [128, 32], F32)
    s1t = wbh.ap()[:, 0:1024]
    s2t = wbh.ap()[:, 1024:2048]

    psb = [nc.alloc_psum_tensor(f"psb{i}", [128, 512], F32) for i in range(8)]
    ps_mm = psb[0]
    ps_stA = psb[1].ap()[0:1, 0:512]
    ps_stB = psb[2].ap()[0:1, 0:512]
    ps_rep = psb[3]

    ctx = dict(xt=xt, ph=ph, pl=pl, sq=sq, pf=pf, stat=stat, ones_col=ones_col,
               ones_row=ones_row, eps_t=eps_t, eps=1e-5,
               ps_stA=ps_stA, ps_stB=ps_stB, ps_rep=ps_rep, ps_rep2=psb[4])

    seq = Seq(nc)

    for i in range(8):
        seq.dma("sync", lambda s, i=i: s.dma_start(
            out=xt[i][:], in_=x_in.tensor.ap()[128 * i:128 * (i + 1), :]))

    def fconsts(v):
        v.memset(ones_col[:], 1.0)
        v.memset(ones_row[:], 1.0)
        v.memset(eps_t[:], 1e-5)
    seq.c("vector", fconsts)

    if with_wo:
        seq.dma("sync", lambda s: s.dma_start(out=boc[:], in_=bo_in))
        cth, ctl = bp[0:8], bp[8:16]
        for i in range(8):
            seq.dma("sync", lambda s, i=i: s.dma_start(
                out=pf.ap()[:, 0:T], in_=ctx_in.tensor.ap()[128 * i:128 * (i + 1), :]))

            def fsplit(v, i=i):
                v.tensor_copy(cth[i].ap()[:, 0:T], pf.ap()[:, 0:T])
                v.tensor_tensor(ctl[i].ap()[:, 0:T], pf.ap()[:, 0:T],
                                cth[i].ap()[:, 0:T], AL.subtract)
            seq.c("vector", fsplit)
        for dt in range(8):
            for dch in range(8):
                seq.dma("sync", lambda s, dt=dt, dch=dch: s.dma_start(
                    out=wstage.ap()[:, 128 * dch:128 * (dch + 1)],
                    in_=woT_in.tensor.ap()[128 * dch:128 * (dch + 1),
                                           128 * dt:128 * (dt + 1)]))

            def fws(v, dt=dt):
                v.tensor_copy(wbh.ap()[:, 0:1024], wstage.ap()[:, 0:1024])
                v.tensor_tensor(wbl.ap()[:, 0:1024], wstage.ap()[:, 0:1024],
                                wbh.ap()[:, 0:1024], AL.subtract)
            seq.c("vector", fws)
            for tt in range(4):
                def fmm(t, dt=dt, tt=tt):
                    for dch in range(8):
                        wsh = wbh.ap()[:, 128 * dch:128 * (dch + 1)]
                        wsl = wbl.ap()[:, 128 * dch:128 * (dch + 1)]
                        rh = cth[dch].ap()[:, 512 * tt:512 * (tt + 1)]
                        rl = ctl[dch].ap()[:, 512 * tt:512 * (tt + 1)]
                        t.matmul(ps_mm[:], wsh, rh, start=(dch == 0), stop=False)
                        t.matmul(ps_mm[:], wsl, rh, start=False, stop=False)
                        t.matmul(ps_mm[:], wsh, rl, start=False, stop=(dch == 7))
                seq.c("tensor", fmm)

                def fev(v, dt=dt, tt=tt):
                    if FEV_MODE == "stt":
                        v.scalar_tensor_tensor(xt[dt].ap()[:, 512 * tt:512 * (tt + 1)],
                                               ps_mm.ap()[:], boc.ap()[:, dt:dt + 1],
                                               xt[dt].ap()[:, 512 * tt:512 * (tt + 1)],
                                               AL.add, AL.add)
                    elif FEV_MODE == "copy":
                        v.tensor_copy(pf.ap()[:, 0:512], ps_mm.ap()[:])
                    elif FEV_MODE == "sbufonly":
                        v.tensor_copy(pf.ap()[:, 0:512], xt[0].ap()[:, 0:512])
                    elif FEV_MODE == "scalarcopy":
                        pass
                    elif FEV_MODE == "tt":
                        v.tensor_tensor(xt[dt].ap()[:, 512 * tt:512 * (tt + 1)],
                                        ps_mm.ap()[:],
                                        xt[dt].ap()[:, 512 * tt:512 * (tt + 1)], AL.add)
                        v.tensor_scalar(xt[dt].ap()[:, 512 * tt:512 * (tt + 1)],
                                        xt[dt].ap()[:, 512 * tt:512 * (tt + 1)],
                                        boc.ap()[:, dt:dt + 1], None, AL.add)
                if FEV_MODE == "scalarcopy":
                    def fevs(s, dt=dt, tt=tt):
                        s.copy(pf.ap()[:, 0:512], ps_mm.ap()[:])
                    seq.c("scalar", fevs)
                else:
                    seq.c("vector", fev)

    emit_ln_prep(seq, nc, ctx, T, with_mean=True)

    seq.c("scalar", lambda s: load_act_set(s, nc, SET_EXPLN))

    def fz(v):
        v.memset(asum2[:], 0.0)
        v.memset(alpha2[:], 0.0)
    seq.c("vector", fz)
    for dch in range(8):
        for qtr in range(4):
            seq.dma("sync", lambda s, dch=dch, qtr=qtr: s.dma_start(
                out=wstage[:], in_=w1T_in.tensor.ap()[128 * dch:128 * (dch + 1),
                                                      1024 * qtr:1024 * (qtr + 1)]))

            def fsgn(s):
                s.activation(wbh.ap()[:, 0:1024], wstage[:], AF.Sign)
            seq.c("scalar", fsgn)
            seq.dma("sync", lambda s, dch=dch, qtr=qtr: s.dma_start(
                out=s1_scr.tensor.ap()[128 * dch:128 * (dch + 1),
                                       1024 * qtr:1024 * (qtr + 1)], in_=wbh.ap()[:, 0:1024]))

            def fabs(s):
                s.activation(wstage[:], wstage[:], AF.Abs)
            seq.c("scalar", fabs)

            def fcol(t, qtr=qtr):
                for j8 in range(8):
                    t.matmul(ps_rep.ap()[:, j8:j8 + 1],
                             wstage.ap()[:, 128 * j8:128 * (j8 + 1)], ones_col[:],
                             start=True, stop=True)
            seq.c("tensor", fcol)

            def fadd(v, qtr=qtr):
                v.tensor_tensor(asum2.ap()[:, 8 * qtr:8 * (qtr + 1)],
                                asum2.ap()[:, 8 * qtr:8 * (qtr + 1)],
                                ps_rep.ap()[:, 0:8], AL.add)
            seq.c("vector", fadd)

    def fsc(v):
        v.tensor_scalar(alpha1[:], asum2[:], 1.0 / 1024.0, None, AL.mult)
    seq.c("vector", fsc)
    seq.dma("sync", lambda s: s.dma_start(out=b1c[:], in_=b1_in))
    seq.dma("sync", lambda s: s.dma_start(out=b2c[:], in_=b2_in))

    for jch in range(32):
        seq.dma("sync", lambda s, jch=jch: s.dma_start(
            out=wstage.ap()[:, 0:1024],
            in_=w2T_in.tensor.ap()[128 * jch:128 * (jch + 1), :]))

        def fsgn2(s):
            s.activation(wbh.ap()[:, 0:1024], wstage.ap()[:, 0:1024], AF.Sign)
        seq.c("scalar", fsgn2)
        seq.dma("sync", lambda s, jch=jch: s.dma_start(
            out=s2_scr.tensor.ap()[128 * jch:128 * (jch + 1), :],
            in_=wbh.ap()[:, 0:1024]))

        def fabs2(s):
            s.activation(wstage.ap()[:, 0:1024], wstage.ap()[:, 0:1024], AF.Abs)
        seq.c("scalar", fabs2)

        def fcol2(t):
            for d8 in range(8):
                t.matmul(ps_rep.ap()[:, d8:d8 + 1],
                         wstage.ap()[:, 128 * d8:128 * (d8 + 1)], ones_col[:],
                         start=True, stop=True)
        seq.c("tensor", fcol2)

        def fadd2(v):
            v.tensor_tensor(alpha2[:], alpha2[:], ps_rep.ap()[:, 0:8], AL.add)
        seq.c("vector", fadd2)

    def fsc2(v):
        v.tensor_scalar(alpha2[:], alpha2[:], 1.0 / 4096.0, None, AL.mult)
    seq.c("vector", fsc2)

    seq.c("sync", lambda s: s.nop())
    seq.c("scalar", lambda s: load_act_set(s, nc, SET_GELU))
    bufs4 = [s1t, s2t, wbl.ap()[:, 0:1024], wbl.ap()[:, 1024:2048]]
    LAG1 = {"sync": 16}
    LAG2 = {"sync": 2}
    for tt in range(T // NT):
        tsl = slice(NT * tt, NT * (tt + 1))

        def femit_ld1(jp):
            pb = (bufs4[0], bufs4[1]) if jp % 2 == 0 else (bufs4[2], bufs4[3])

            def fld(s, jp=jp, pb=pb):
                out = []
                for q in range(2):
                    jt = 2 * jp + q
                    for dch in range(8):
                        out.append(s.dma_start(
                            out=pb[q][:, 128 * dch:128 * (dch + 1)],
                            in_=s1_scr.tensor.ap()[128 * dch:128 * (dch + 1),
                                                   128 * jt:128 * (jt + 1)]))
                return out
            seq.dma("sync", fld, n=16)

        femit_ld1(0)
        for jp in range(16):
            if jp + 1 < 16:
                femit_ld1(jp + 1)
            pb = (bufs4[0], bufs4[1]) if jp % 2 == 0 else (bufs4[2], bufs4[3])
            jts = (2 * jp, 2 * jp + 1)
            hscs = (hsc, pf)

            def fmm1(t, tsl=tsl, pb=pb):
                for q in range(2):
                    for dch in range(8):
                        w = pb[q][:, 128 * dch:128 * (dch + 1)]
                        t.matmul(psb[q].ap()[:, 0:NT], w, ph[dch].ap()[:, tsl],
                                 start=(dch == 0), stop=False)
                        t.matmul(psb[q].ap()[:, 0:NT], w, pl[dch].ap()[:, tsl],
                                 start=False, stop=(dch == 7))
            seq.c("tensor", fmm1, dma_lag=LAG1)

            def fev1(v, jts=jts, hscs=hscs):
                for q, jt in enumerate(jts):
                    v.tensor_scalar(hscs[q].ap()[:, 0:NT], psb[q].ap()[:, 0:NT],
                                    alpha1.ap()[:, jt:jt + 1], None, AL.mult)
            seq.c("vector", fev1, dma_lag=LAG1)

            def fge(s, jts=jts, hscs=hscs):
                for q, jt in enumerate(jts):
                    s.activation(hscs[q].ap()[:, 256:256 + NT], hscs[q].ap()[:, 0:NT],
                                 AF.Gelu, bias=b1c.ap()[:, jt:jt + 1])
            seq.c("scalar", fge, dma_lag=LAG1)

            def fsp(v, jts=jts, hscs=hscs):
                for q, jt in enumerate(jts):
                    hhd = hh[jt // 8].ap()[:, NT * (jt % 8):NT * (jt % 8 + 1)]
                    hld = hl[jt // 8].ap()[:, NT * (jt % 8):NT * (jt % 8 + 1)]
                    v.tensor_copy(hhd, hscs[q].ap()[:, 256:256 + NT])
                    v.tensor_tensor(hld, hscs[q].ap()[:, 256:256 + NT], hhd, AL.subtract)
            seq.c("vector", fsp, dma_lag=LAG1)

        def femit_ld2(jp):
            pb = (bufs4[0], bufs4[1]) if jp % 2 == 0 else (bufs4[2], bufs4[3])

            def fld2(s, jp=jp, pb=pb):
                return [s.dma_start(out=pb[q][:, 0:1024],
                                    in_=s2_scr.tensor.ap()[128 * (2 * jp + q):128 * (2 * jp + q + 1), :])
                        for q in range(2)]
            seq.dma("sync", fld2, n=2)

        femit_ld2(0)
        for jp in range(16):
            if jp + 1 < 16:
                femit_ld2(jp + 1)
            pb = (bufs4[0], bufs4[1]) if jp % 2 == 0 else (bufs4[2], bufs4[3])

            def fmm2(t, jp=jp, pb=pb):
                for q in range(2):
                    jch = 2 * jp + q
                    rh = hh[jch // 8].ap()[:, NT * (jch % 8):NT * (jch % 8 + 1)]
                    rl = hl[jch // 8].ap()[:, NT * (jch % 8):NT * (jch % 8 + 1)]
                    for dt in range(8):
                        w = pb[q][:, 128 * dt:128 * (dt + 1)]
                        t.matmul(psb[dt].ap()[:, 256:256 + NT], w, rh,
                                 start=(jch == 0), stop=False)
                        t.matmul(psb[dt].ap()[:, 256:256 + NT], w, rl,
                                 start=False, stop=(jch == 31))
            seq.c("tensor", fmm2, dma_lag=LAG2)

        def fev2(v, tsl=tsl):
            for dt in range(8):
                v.scalar_tensor_tensor(xt[dt].ap()[:, tsl], psb[dt].ap()[:, 256:256 + NT],
                                       alpha2.ap()[:, dt:dt + 1], xt[dt].ap()[:, tsl],
                                       AL.mult, AL.add)
                v.tensor_scalar(xt[dt].ap()[:, tsl], xt[dt].ap()[:, tsl],
                                b2c.ap()[:, dt:dt + 1], None, AL.add)
        seq.c("vector", fev2)

    dbgt = pf
    def ftap(v):
        v.tensor_copy(dbgt.ap()[0:1, 0:T], stat.ap()[0:1, T:2 * T])
    seq.c("vector", ftap)
    seq.dma("gpsimd", lambda g: g.dma_start(out=dbg_hbm.tensor.ap()[0:1, :], in_=dbgt.ap()[0:1, 0:T]))
    def ftap2(v):
        v.memset(dbgt.ap()[0:1, 0:2048], 0.0)
        v.tensor_copy(dbgt.ap()[0:1, 0:32], alpha1.ap()[0:1, 0:32])
        v.tensor_copy(dbgt.ap()[0:1, 32:40], alpha2.ap()[0:1, 0:8])
    seq.c("vector", ftap2)
    seq.dma("gpsimd", lambda g: g.dma_start(out=dbg_hbm.tensor.ap()[1:2, :], in_=dbgt.ap()[0:1, 0:2048]))
    def ftap3(v):
        v.tensor_copy(dbgt.ap()[0:1, 0:2048], ph[0].ap()[0:1, 0:T])
    seq.c("vector", ftap3)
    seq.dma("gpsimd", lambda g: g.dma_start(out=dbg_hbm.tensor.ap()[2:3, :], in_=dbgt.ap()[0:1, 0:T]))
    def ftap4(v):
        v.tensor_copy(dbgt.ap()[0:1, 0:2048], hh[0].ap()[0:1, 0:T])
    seq.c("vector", ftap4)
    seq.dma("gpsimd", lambda g: g.dma_start(out=dbg_hbm.tensor.ap()[3:4, :], in_=dbgt.ap()[0:1, 0:T]))

    for i in range(8):
        seq.dma("gpsimd", lambda g, i=i: g.dma_start(
            out=out_hbm.tensor.ap()[128 * i:128 * (i + 1), :], in_=xt[i][:]))

    if max_items is not None:
        seq.items = seq.items[:max_items]
    with (
        nc.semaphore("c_sem") as c_sem,
        nc.semaphore("d_sync") as d_sync,
        nc.semaphore("d_gp") as d_gp,
        nc.Block() as block,
    ):
        d_sems = {"sync": d_sync, "gpsimd": d_gp}
        n_items, n_dma = seq.build(block, c_sem, d_sems)

        @block.gpsimd
        def _(g):
            g.wait_ge(c_sem, n_items)
            for k, s in d_sems.items():
                g.wait_ge(s, 16 * n_dma[k])
    return nc



def _cols128(v):
    v = np.asarray(v, dtype=np.float32)
    return np.ascontiguousarray(v.reshape(-1, 128).T)


def hT(a):
    return np.ascontiguousarray(np.asarray(a, dtype=np.float32).T)


def run_ffn_phase(x_fm_per_core, ffn_w, wo=None):
    nc = build_ffn_phase(with_wo=wo is not None)
    w1T = hT(ffn_w["w1"])
    w2T = hT(ffn_w["w2"])
    b1 = _cols128(ffn_w["b1"])
    b2 = _cols128(ffn_w["b2"])
    in_maps = []
    for c in range(8):
        m = {"x": np.ascontiguousarray(x_fm_per_core[c]),
             "w1T": w1T, "b1": b1, "w2T": w2T, "b2": b2}
        if wo is not None:
            m["ctx"] = np.ascontiguousarray(wo["ctx_fm"][c])
            m["woT"] = hT(wo["woT_pre"]) if "woT_pre" in wo else np.ascontiguousarray(wo["woT"])
            m["bo"] = _cols128(wo["bo"])
        in_maps.append(m)
    res = run_bass_kernel_spmd(nc, in_maps, core_ids=list(range(8)))
    return [res.results[c]["out"] for c in range(8)], [res.results[c]["dbg"] for c in range(8)]


def kernel(**inputs):
    T = 4096
    x0 = np.asarray(inputs["L0_attn_a_x"], np.float32)

    nc1 = build_attn_a_phase(T)
    pp = np.arange(128)[:, None] % 32
    cc = np.arange(32)[None, :]
    maskS = np.where(pp < cc, 0.0, -1e30).astype(np.float32)
    maskI = np.where(pp <= cc, 0.0, -1e30).astype(np.float32)
    selq = np.zeros((128, 256), np.float32)
    for tile in range(4):
        selq[0:64, 64 * tile + 2 * tile] = 1.0
        selq[64:128, 64 * tile + 2 * tile + 1] = 1.0
    ident = np.eye(128, dtype=np.float32)
    Wq, Wk = inputs["L0_attn_a_Wq"], inputs["L0_attn_a_Wk"]
    Wv, Wg, Wb = inputs["L0_attn_a_Wv"], inputs["L0_attn_a_Wg"], inputs["L0_attn_a_Wb"]
    Wa = np.asarray(inputs["L0_attn_a_Wa"], np.float32)
    dtb_a = np.asarray(inputs["L0_attn_a_dt_bias"], np.float32)
    alog_a = np.asarray(inputs["L0_attn_a_A_log"], np.float32)
    nw_a = np.asarray(inputs["L0_attn_a_norm_w"], np.float32)
    in1 = []
    for c in range(8):
        seq_i, h0 = c // 2, 8 * (c % 2)
        sl = slice(h0 * 64, (h0 + 8) * 64)
        wap = np.zeros((1024, 64), np.float32)
        wap[:, 0:8] = hT(Wa[h0:h0 + 8])
        in1.append({
            "x": hT(x0[seq_i]),
            "wqT": hT(np.asarray(Wq)[sl]), "wkT": hT(np.asarray(Wk)[sl]),
            "wvT": hT(np.asarray(Wv)[sl]), "wgT": hT(np.asarray(Wg)[sl]),
            "wbT": hT(np.asarray(Wb)[sl]), "waT": wap, "selq": selq,
            "dtb": dtb_a[h0:h0 + 8, None].copy(), "alog": alog_a[h0:h0 + 8, None].copy(),
            "nwfm": np.ascontiguousarray(np.tile(nw_a, 2)[:, None]).astype(np.float32),
            "maskS": maskS, "maskI": maskI, "ident": ident,
        })
    r1 = run_bass_kernel_spmd(nc1, in1, core_ids=list(range(8)))
    ctx_flat = np.concatenate([
        np.concatenate([r1.results[2 * s]["ctx"], r1.results[2 * s + 1]["ctx"]], 0).T
        for s in range(4)], 0)

    xflat = x0.reshape(4 * T, 1024)
    NTOK = 4 * T // 8
    ffn_w = {k: inputs["L0_ffn_" + k] for k in ("w1", "b1", "w2", "b2")}
    xs = [hT(xflat[NTOK * c:NTOK * (c + 1)]) for c in range(8)]
    ctxs = [hT(ctx_flat[NTOK * c:NTOK * (c + 1)]) for c in range(8)]
    outs, _ = run_ffn_phase(xs, ffn_w, wo=dict(
        ctx_fm=ctxs, woT=hT(inputs["L0_attn_a_Wo"]), bo=inputs["L0_attn_a_bo"]))
    x3flat = np.concatenate([o.T for o in outs], 0)

    in_w = np.asarray(inputs["L0_attn_b_in_proj_w"], np.float32)
    conv_w = np.asarray(inputs["L0_attn_b_conv_w"], np.float32)
    conv_b = np.asarray(inputs["L0_attn_b_conv_b"], np.float32)
    dtb_b = np.asarray(inputs["L0_attn_b_dt_bias"], np.float32)
    alog_b = np.asarray(inputs["L0_attn_b_A_log"], np.float32)
    Dsk = np.asarray(inputs["L0_attn_b_D"], np.float32)
    gnw = np.asarray(inputs["L0_attn_b_gnorm_w"], np.float32)
    opw = np.asarray(inputs["L0_attn_b_out_proj_w"], np.float32)
    nw_b = np.asarray(inputs["L0_attn_b_norm_w"], np.float32)
    pp64 = np.arange(128)[:, None] % 64
    cc64 = np.arange(64)[None, :]
    mask64 = np.where(pp64 <= cc64, 0.0, -1e30).astype(np.float32)

    def run_mamba_layer(xflat_in, li, pre_ln):
        nca = build_mamba_a_phase(T, pre_ln=pre_ln)
        xseq = [hT(xflat_in[T * s:T * (s + 1)]) for s in range(4)]
        y_rows = [None] * 4
        z_rows = [None] * 4
        ln_seq = [None] * 4
        for lh in range(2):
            ina = []
            for c in range(8):
                seq_i, h0 = c // 2, 16 * lh + 8 * (c % 2)
                rows = np.concatenate([
                    np.arange(h0 * 64, (h0 + 8) * 64),
                    2048 + np.arange(h0 * 64, (h0 + 8) * 64),
                    np.arange(4096, 4160), np.arange(4160, 4224),
                    4224 + h0 + np.arange(8)])
                inwT = np.zeros((1024, 1216), np.float32)
                inwT[:, 0:1160] = in_w[li][rows].T
                crow = np.concatenate([np.arange(h0 * 64, (h0 + 8) * 64),
                                       np.arange(2048, 2112), np.arange(2112, 2176)])
                cw = np.zeros((128, 20), np.float32)
                cb = np.zeros((128, 5), np.float32)
                for i in range(5):
                    cw[:, 4 * i:4 * i + 4] = conv_w[li][crow[128 * i:128 * (i + 1)]]
                    cb[:, i] = conv_b[li][crow[128 * i:128 * (i + 1)]]
                dc = np.zeros((128, 4), np.float32)
                for i in range(4):
                    dc[0:64, i] = Dsk[li][h0 + 2 * i]
                    dc[64:128, i] = Dsk[li][h0 + 2 * i + 1]
                ina.append({
                    "x": xseq[seq_i], "inwT": inwT, "nw1": _cols128(nw_b[li]),
                    "convw": cw, "convb": cb,
                    "dtb": dtb_b[li][h0:h0 + 8, None].copy(),
                    "alog": alog_b[li][h0:h0 + 8, None].copy(),
                    "dcolf": dc, "mask64": mask64, "ident": ident,
                })
            ra = run_bass_kernel_spmd(nca, ina, core_ids=list(range(8)))
            for c in range(8):
                seq_i, h0 = c // 2, 16 * lh + 8 * (c % 2)
                if y_rows[seq_i] is None:
                    y_rows[seq_i] = np.zeros((2048, T), np.float32)
                    z_rows[seq_i] = np.zeros((2048, T), np.float32)
                y_rows[seq_i][h0 * 64:(h0 + 8) * 64] = ra.results[c]["y"]
                z_rows[seq_i][h0 * 64:(h0 + 8) * 64] = ra.results[c]["z"]
                if pre_ln and lh == 0 and c % 2 == 0:
                    ln_seq[seq_i] = ra.results[c]["ln0"]
        yflat = np.concatenate([y.T for y in y_rows], 0)
        zflat = np.concatenate([z.T for z in z_rows], 0)
        lnflat = (np.concatenate([l.T for l in ln_seq], 0) if pre_ln else None)
        return yflat, zflat, lnflat

    y0, z0, ln0flat = run_mamba_layer(x3flat, 0, pre_ln=True)
    ncb0 = build_mamba_out_phase(NTOK, add_outer=False)
    inb = []
    for c in range(8):
        sl = slice(NTOK * c, NTOK * (c + 1))
        inb.append({"x": hT(ln0flat[sl]), "yf": hT(y0[sl]), "zf": hT(z0[sl]),
                    "gwc": _cols128(gnw[0]), "owT": hT(opw[0])})
    rb0 = run_bass_kernel_spmd(ncb0, inb, core_ids=list(range(8)))
    r1flat = np.concatenate([rb0.results[c]["out"].T for c in range(8)], 0)

    y1, z1, _ = run_mamba_layer(r1flat, 1, pre_ln=False)
    ncb1 = build_mamba_out_phase(NTOK, add_outer=True)
    inb = []
    for c in range(8):
        sl = slice(NTOK * c, NTOK * (c + 1))
        inb.append({"x": hT(r1flat[sl]), "yf": hT(y1[sl]), "zf": hT(z1[sl]),
                    "gwc": _cols128(gnw[1]), "owT": hT(opw[1]),
                    "xo": hT(x3flat[sl])})
    rb1 = run_bass_kernel_spmd(ncb1, inb, core_ids=list(range(8)))
    x4flat = np.concatenate([rb1.results[c]["out"].T for c in range(8)], 0)

    xs = [hT(x4flat[NTOK * c:NTOK * (c + 1)]) for c in range(8)]
    outs, _ = run_ffn_phase(xs, ffn_w, wo=None)
    x5flat = np.concatenate([o.T for o in outs], 0)
    return np.ascontiguousarray(x5flat.reshape(4, T, 1024))



def mk_ap(t, offset, dims):
    return bass.AP(tensor=t, offset=offset, ap=[list(d) for d in dims])


LAST_MARKS = {}


def build_attn_a_phase(T_SEQ=4096, debug=False, trunc=None):
    SPAN = 512
    NSP = T_SEQ // SPAN
    NW = SPAN // 32
    nc = bass.Bass("TRN2", target_bir_lowering=False, detect_race_conditions=False)
    x_in = nc.dram_tensor("x", [D, T_SEQ], F32, kind="ExternalInput").ap()
    w_ins = {}
    for nm, ncol in [("wqT", 512), ("wkT", 512), ("wvT", 512), ("wgT", 512), ("wbT", 512), ("waT", 64)]:
        w_ins[nm] = nc.dram_tensor(nm, [D, ncol], F32, kind="ExternalInput").ap()
    dtb_in = nc.dram_tensor("dtb", [8, 1], F32, kind="ExternalInput").ap()
    alog_in = nc.dram_tensor("alog", [8, 1], F32, kind="ExternalInput").ap()
    nwfm_in = nc.dram_tensor("nwfm", [128, 1], F32, kind="ExternalInput").ap()
    maskS_in = nc.dram_tensor("maskS", [128, 32], F32, kind="ExternalInput").ap()
    maskI_in = nc.dram_tensor("maskI", [128, 32], F32, kind="ExternalInput").ap()
    ident_in = nc.dram_tensor("ident", [128, 128], F32, kind="ExternalInput").ap()
    selq_in = nc.dram_tensor("selq", [128, 256], F32, kind="ExternalInput").ap()
    ctx_hbm = nc.dram_tensor("ctx", [512, T_SEQ], F32, kind="ExternalOutput").ap()
    scr = nc.dram_tensor("scr_fold", [8, 32768], F32, kind="Internal")
    dbg = nc.dram_tensor("dbg", [128, 4096], F32, kind="ExternalOutput").ap() if debug else None

    A = nc.alloc_sbuf_tensor
    s_st = [A(f"s_st{i}", [128, 64], F32) for i in range(4)]
    u_t = [A(f"u_t{g}", [128, 64], F32) for g in range(2)]
    tmpc = [A(f"tmpc{g}", [128, 64], F32) for g in range(2)]
    wbd = [A(f"wbd{g}", [128, 128], F32) for g in range(2)]
    mbd = [A(f"mbd{g}", [128, 128], F32) for g in range(2)]
    ones_col = A("ones_col", [128, 1], F32)
    ones_row = A("ones_row", [1, 128], F32)
    nwfm = A("nwfm_s", [128, 1], F32)
    maskS = A("maskS_s", [128, 32], F32)
    maskI = A("maskI_s", [128, 32], F32)
    identt = A("identt", [128, 128], F32)
    sel = A("sel", [128, 256], F32)
    dtb = A("dtb_s", [8, 1], F32)
    nega = A("nega", [8, 1], F32)
    WH, WL = {}, {}
    for nm, ncol in [("wqT", 512), ("wkT", 512), ("wvT", 512), ("wgT", 512), ("wbT", 512), ("waT", 64)]:
        WH[nm] = A("h" + nm, [128, 8 * ncol], BF16)
        WL[nm] = A("l" + nm, [128, 8 * ncol], BF16)
    xt = [A(f"xt{i}", [128, SPAN], F32) for i in range(8)]
    wstage = xt[0]
    ph = [A(f"ph{i}", [128, SPAN], BF16) for i in range(8)]
    pl = [A(f"pl{i}", [128, SPAN], BF16) for i in range(8)]
    pf = A("pf", [128, SPAN], F32)
    sqs = A("sqs", [128, 512], F32)
    stat = A("stat", [1, 3 * SPAN], F32)
    qf = [A(f"qf{i}", [128, SPAN], F32) for i in range(4)]
    kf = [A(f"kf{i}", [128, SPAN], F32) for i in range(4)]
    vstage = xt[0:4]
    vch = [A(f"vch{g}", [128, NW * 64], F32) for g in range(2)]
    bch = [A(f"bch{g}", [128, NW * 64], F32) for g in range(2)]
    ktm = [A(f"ktm{g}", [128, NW * 64], F32) for g in range(2)]
    tmstg = sqs
    ych = [A(f"ych{g}", [128, NW * 64], F32) for g in range(2)]
    vrS = [A(f"vrS{g}", [128, 64], F32) for g in range(2)]
    t1S = [A(f"t1S{g}", [128, SPAN], F32) for g in range(2)]
    t1I = [A(f"t1I{g}", [128, SPAN], F32) for g in range(2)]
    colstk = [A(f"colstk{g}", [128, NW * 8], F32) for g in range(2)]
    dccol = [A(f"dccol{i}", [128, NW], F32) for i in range(4)]
    rstdw = [A(f"rstdw{g}", [128, NW], F32) for g in range(2)]
    caf = A("caf", [8, SPAN], F32)
    spf = A("spf", [8, SPAN], F32)
    spf2 = A("spf2", [8, SPAN], F32)
    knf = A("knf", [8, SPAN], F32)
    qnf = A("qnf", [8, SPAN], F32)
    cakf = A("cakf", [8, SPAN], F32)
    decf = A("decf", [8, SPAN], F32)
    d2f = A("d2f", [8, SPAN], F32)
    auxf = A("auxf", [8, SPAN], F32)
    dcf = A("dcf", [8, NW], F32)
    cprev = A("cprev", [8, NW], F32)
    dsS = [A(f"dsS{g}", [128, 32], F32) for g in range(2)]
    dsI = [A(f"dsI{g}", [128, 32], F32) for g in range(2)]
    esS = [A(f"esS{g}", [128, 32], F32) for g in range(2)]
    esI = [A(f"esI{g}", [128, 32], F32) for g in range(2)]
    ctxf = xt[4:8]

    psb = [nc.alloc_psum_tensor(f"psA{i}", [128, 512], F32) for i in range(8)]
    ps_chain = [psb[0], psb[1]]
    ps_G = [psb[2], psb[3]]
    ps_R = [psb[4], psb[5]]
    ps_Y = [psb[6], psb[7]]

    seq = Seq(nc)

    def fconst(v):
        v.memset(ones_col[:], 1.0)
        v.memset(ones_row[:], 1.0)
        v.memset(wbd[0][:], 0.0)
        v.memset(wbd[1][:], 0.0)
        v.memset(mbd[0][:], 0.0)
        v.memset(mbd[1][:], 0.0)
        v.memset(u_t[0][:], 0.0)
        v.memset(u_t[1][:], 0.0)
        for i in range(4):
            v.memset(s_st[i][:], 0.0)
    seq.c("vector", fconst)
    seq.dma("sync", lambda s: s.dma_start(out=nwfm[:], in_=nwfm_in))
    seq.dma("sync", lambda s: s.dma_start(out=maskS[:], in_=maskS_in))
    seq.dma("sync", lambda s: s.dma_start(out=maskI[:], in_=maskI_in))
    seq.dma("sync", lambda s: s.dma_start(out=identt[:], in_=ident_in))
    seq.dma("sync", lambda s: s.dma_start(out=sel[:], in_=selq_in))
    seq.dma("sync", lambda s: s.dma_start(out=dtb[:], in_=dtb_in))
    seq.dma("sync", lambda s: s.dma_start(out=nega[:], in_=alog_in))

    def fnega(s):
        load_act_set(s, nc, SET_EXPLN)
        s.activation(nega[:], nega[:], AF.Exp)
    seq.c("scalar", fnega)
    seq.c("vector", lambda v: v.tensor_scalar(nega[:], nega[:], -1.0, None, AL.mult))

    for nm, ncol in [("wqT", 512), ("wkT", 512), ("wvT", 512), ("wgT", 512), ("wbT", 512), ("waT", 64)]:
        for dch in range(8):
            seq.dma("sync", lambda s, nm=nm, dch=dch, ncol=ncol: s.dma_start(
                out=wstage.ap()[:, 0:ncol],
                in_=w_ins[nm].tensor.ap()[128 * dch:128 * (dch + 1), :]))

            def fsp_(v, nm=nm, dch=dch, ncol=ncol):
                v.tensor_copy(WH[nm].ap()[:, ncol * dch:ncol * (dch + 1)], wstage.ap()[:, 0:ncol])
                v.tensor_tensor(WL[nm].ap()[:, ncol * dch:ncol * (dch + 1)], wstage.ap()[:, 0:ncol],
                                WH[nm].ap()[:, ncol * dch:ncol * (dch + 1)], AL.subtract)
            seq.c("vector", fsp_)

    ch = {"pe": 0, "dv": 0}
    ch_pe = nc.semaphore("ch_pe").__enter__()
    ch_dv = nc.semaphore("ch_dv").__enter__()

    for sp in range(NSP):
        toff = SPAN * sp

        def fxl(s, toff=toff):
            return [s.dma_start(out=xt[i][:],
                                in_=x_in.tensor.ap()[128 * i:128 * (i + 1), toff:toff + SPAN])
                    for i in range(8)]
        seq.dma("sync", fxl, n=8)

        ctxd = dict(xt=xt, ph=ph, pl=pl, sq=sqs.ap()[:, 0:512], pf=pf, stat=stat,
                    ones_col=ones_col, ones_row=ones_row, eps=1e-5,
                    ps_stA=psb[2].ap()[0:1, 0:512], ps_stB=psb[3].ap()[0:1, 0:512],
                    ps_rep=psb[4], ps_rep2=psb[5])
        emit_ln_prep(seq, nc, ctxd, SPAN, with_mean=True)
        if debug and sp == 0:
            seq.c("vector", lambda v: v.tensor_copy(pf[:], ph[0][:]))
            seq.dma("gpsimd", lambda sy: sy.dma_start(out=dbg.tensor.ap()[:, 2960:3472], in_=pf[:]))
            seq.dma("gpsimd", lambda sy: sy.dma_start(out=dbg.tensor.ap()[0:1, 2448:2960],
                                                      in_=stat.ap()[0:1, 512:1024]))
            LAST_MARKS["ln_dump"] = len(seq.items)

        def proj_fm(nm, outs):
            for ot in range(4):
                def fmm(t, nm=nm, ot=ot):
                    for dch in range(8):
                        wsh = WH[nm].ap()[:, 512 * dch + 128 * ot:512 * dch + 128 * (ot + 1)]
                        wsl = WL[nm].ap()[:, 512 * dch + 128 * ot:512 * dch + 128 * (ot + 1)]
                        t.matmul(psb[2][:], wsh, ph[dch][:], start=(dch == 0), stop=False)
                        t.matmul(psb[2][:], wsl, ph[dch][:], start=False, stop=False)
                        t.matmul(psb[2][:], wsh, pl[dch][:], start=False, stop=(dch == 7))
                seq.c("tensor", fmm)

                def fev(v, outs=outs, ot=ot):
                    v.tensor_copy(outs[ot][:], psb[2].ap()[:])
                seq.c("vector", fev)

        def tm_fold(src_tiles, fold_fn, e_cnt=64):
            LAGF = {"gpsimd": 5}
            for g in range(2):
                for blk in range(4):
                    slot = 4 * g + blk
                    stg = sqs.ap()[:, 0:256] if slot % 2 == 0 else sqs.ap()[:, 256:512]
                    lag0 = None if slot == 0 else LAGF

                    def ftr_(t, g=g, blk=blk):
                        for half in range(2):
                            t.transpose(psb[5].ap()[:, 128 * half:128 * (half + 1)],
                                        src_tiles[2 * g + half].ap()[:, 128 * blk:128 * (blk + 1)],
                                        identt[:])
                    seq.c("tensor", ftr_, dma_lag=lag0)

                    def fte_(v, stg=stg):
                        v.tensor_copy(stg, psb[5].ap()[:, 0:256])
                    seq.c("vector", fte_, dma_lag=lag0)

                    def fst_(sy, slot=slot, stg=stg):
                        return sy.dma_start(out=scr.ap()[slot:slot + 1, :], in_=stg)
                    seq.dma("gpsimd", fst_)

                    def ff_(sy, g=g, blk=blk, slot=slot, e_cnt=e_cnt):
                        out = []
                        for hh in range(4):
                            sap = mk_ap(scr, slot * 32768 + 64 * hh,
                                        [[256, 32], [32 * 256, 4], [1, e_cnt]])
                            out.append(sy.dma_start(out=fold_fn(g, blk, hh), in_=sap))
                        return out
                    seq.dma("gpsimd", ff_, n=4)

        def chain_dims(dst, blk, hh):
            return mk_ap(dst, 4 * blk * 64 + hh * 32 * NW * 64,
                         [[NW * 64, 32], [64, 4], [1, 64]])

        LAST_MARKS[f"pre_qk_{sp}"] = len(seq.items)
        proj_fm("wqT", qf)
        if debug and sp == 0:
            seq.dma("gpsimd", lambda sy: sy.dma_start(out=dbg.tensor.ap()[:, 3472:3984], in_=qf[0][:]))
            LAST_MARKS["q_dump"] = len(seq.items)
        proj_fm("wkT", kf)
        if debug and sp == 0:
            seq.dma("gpsimd", lambda sy: sy.dma_start(out=dbg.tensor.ap()[:, 1664:2176], in_=kf[0][:]))
            LAST_MARKS["k_dump"] = len(seq.items)
        LAST_MARKS[f"post_qk_{sp}"] = len(seq.items)

        def fmma(t):
            for dch in range(8):
                t.matmul(psb[5].ap()[0:64, 0:512], WH["waT"].ap()[:, 64 * dch:64 * (dch + 1)],
                         ph[dch][:], start=(dch == 0), stop=False)
                t.matmul(psb[5].ap()[0:64, 0:512], WL["waT"].ap()[:, 64 * dch:64 * (dch + 1)],
                         ph[dch][:], start=False, stop=False)
                t.matmul(psb[5].ap()[0:64, 0:512], WH["waT"].ap()[:, 64 * dch:64 * (dch + 1)],
                         pl[dch][:], start=False, stop=(dch == 7))
        seq.c("tensor", fmma)
        seq.c("vector", lambda v: v.tensor_copy(spf[:], psb[5].ap()[0:8, 0:512]))
        LAST_MARKS[f"a_copy_{sp}"] = len(seq.items)
        if debug and sp == 0:
            seq.dma("gpsimd", lambda sy: sy.dma_start(out=dbg.tensor.ap()[8:16, 0:512], in_=spf[:]))

        def fsp1(v):
            v.tensor_scalar(spf[:], spf[:], dtb[:], None, AL.add)
            v.tensor_scalar_max(spf2[:], spf[:], 0.0)
        seq.c("vector", fsp1)
        if debug and sp == 0:
            seq.dma("gpsimd", lambda sy: sy.dma_start(out=dbg.tensor.ap()[16:24, 0:512], in_=spf[:]))
            seq.dma("gpsimd", lambda sy: sy.dma_start(out=dbg.tensor.ap()[24:32, 0:512], in_=spf2[:]))

        def fsp2(s):
            load_act_set(s, nc, SET_EXPLN)
            s.activation(caf[:], spf[:], AF.Abs)
            s.activation(caf[:], caf[:], AF.Exp, scale=-1.0)
            s.activation(caf[:], caf[:], AF.Ln, bias=1.0)
        seq.c("scalar", fsp2)
        if debug and sp == 0:
            seq.dma("gpsimd", lambda sy: sy.dma_start(out=dbg.tensor.ap()[32:40, 0:512], in_=caf[:]))

        def fsp3(v):
            v.tensor_tensor(caf[:], caf[:], spf2[:], AL.add)
            v.tensor_scalar(caf[:], caf[:], nega[:], None, AL.mult)
            v.tensor_scalar(spf[:], spf[:], 0.0, 1.0, AL.mult, AL.add)
            v.tensor_tensor_scan(spf2[:], spf[:], caf[:], 0.0, AL.mult, AL.add)
            v.tensor_copy(caf[:], spf2[:])
            v.memset(cprev.ap()[0:8, 0:1], 0.0)
            v.tensor_copy(cprev.ap()[0:8, 1:NW], mk_ap(caf, 31, [[SPAN, 8], [32, NW - 1]]))
        seq.c("vector", fsp3)

        for nm, chd, do_sig in [("wvT", vch, False), ("wbT", bch, True)]:
            proj_fm(nm, vstage)
            if do_sig:
                def fsig(s):
                    load_act_set(s, nc, SET_SIGMOID)
                    for ot in range(4):
                        s.activation(vstage[ot][:], vstage[ot][:], AF.Sigmoid)
                seq.c("scalar", fsig)
            tm_fold(vstage, lambda g, blk, hh, chd=chd: chain_dims(chd[g], blk, hh))

        for src_t, fmdst in [(kf, knf), (qf, qnf)]:
            for tile in range(4):
                def fsq(s, src_t=src_t, tile=tile):
                    s.activation(pf[:], src_t[tile][:], AF.Square)
                seq.c("scalar", fsq)

                def fnm(t, tile=tile):
                    t.matmul(psb[2].ap()[0:64, 0:512], sel.ap()[:, 64 * tile:64 * (tile + 1)],
                             pf[:], start=(tile == 0), stop=(tile == 3))
                seq.c("tensor", fnm)

            def fcp(v, fmdst=fmdst):
                v.tensor_copy(fmdst[:], psb[2].ap()[0:8, 0:512])
            seq.c("vector", fcp)

            def fns(s, fmdst=fmdst):
                load_act_set(s, nc, SET_EXPLN)
                s.activation(spf2[:], fmdst[:], AF.Ln)
                s.activation(spf2[:], spf2[:], AF.Exp, scale=-0.5)
            seq.c("scalar", fns)

            def fnw(v, fmdst=fmdst):
                v.tensor_tensor(pf.ap()[0:8, 0:512], spf2[:], spf2[:], AL.mult)
                v.tensor_tensor(pf.ap()[0:8, 0:512], pf.ap()[0:8, 0:512], fmdst[:], AL.mult)
                v.tensor_scalar(pf.ap()[0:8, 0:512], pf.ap()[0:8, 0:512], -0.5, 1.5,
                                AL.mult, AL.add)
                v.tensor_tensor(fmdst[:], spf2[:], pf.ap()[0:8, 0:512], AL.mult)
            seq.c("vector", fnw)

        tm_fold(kf, lambda g, blk, hh: chain_dims(ktm[g], blk, hh))

        def fcak(s):
            s.activation(cakf[:], knf[:], AF.Ln)
            s.activation(auxf[:], qnf[:], AF.Ln)
        seq.c("scalar", fcak)

        def fcak2(v):
            v.tensor_tensor(cakf[:], cakf[:], caf[:], AL.add)
            v.tensor_tensor(auxf[:], auxf[:], caf[:], AL.add)
            v.tensor_scalar(auxf[:], auxf[:], -2.0794415416798357, None, AL.add)
        seq.c("vector", fcak2)

        def fdec(v):
            v.tensor_tensor(decf[:], caf[:], mk_ap(cprev, 0, [[NW, 8], [1, NW], [0, 32]]),
                            AL.subtract)
            v.tensor_tensor(d2f[:], caf[:], mk_ap(caf, 31, [[SPAN, 8], [32, NW], [0, 32]]),
                            AL.subtract)
        seq.c("vector", fdec)

        def fdece(s):
            s.activation(decf[:], decf[:], AF.Exp)
            s.activation(d2f[:], d2f[:], AF.Exp, scale=-1.0)
        seq.c("scalar", fdece)

        def fdc(v):
            v.tensor_copy(dcf[:], mk_ap(decf, 31, [[SPAN, 8], [32, NW]]))
            v.tensor_tensor(spf[:], decf[:], knf[:], AL.mult)
            v.tensor_scalar(spf[:], spf[:], -1.0, None, AL.mult)
            v.tensor_tensor(spf2[:], d2f[:], knf[:], AL.mult)
            v.tensor_tensor(decf[:], decf[:], qnf[:], AL.mult)
            v.tensor_scalar(decf[:], decf[:], 0.125, None, AL.mult)
            v.tensor_scalar(d2f[:], knf[:], -1.0, None, AL.mult)
        seq.c("vector", fdc)

        for i in range(4):
            for q, src_fm in enumerate([caf, knf, spf, spf2, decf, d2f]):
                for h in range(2):
                    def frc(sy, i=i, q=q, h=h, src_fm=src_fm):
                        sap = mk_ap(src_fm, (2 * i + h) * SPAN, [[SPAN, 1], [1, SPAN]])
                        dap = mk_ap(vstage[i], (64 * h + q) * SPAN, [[SPAN, 1], [1, SPAN]])
                        return sy.dma_start(out=dap, in_=sap)
                    seq.dma("gpsimd", frc)
        tm_fold(vstage, lambda g, blk, hh: mk_ap(colstk[g], 4 * blk * 8 + hh * 32 * NW * 8,
                                                 [[NW * 8, 32], [8, 4], [1, 8]]),
                e_cnt=8)

        def fstt(sy):
            return [sy.dma_start(out=scr.ap()[0:1, 0:8 * SPAN], in_=cakf[:]),
                    sy.dma_start(out=scr.ap()[1:2, 0:8 * SPAN], in_=auxf[:])]
        seq.dma("gpsimd", fstt, n=2)

        def ft12(sy):
            out = []
            for g in range(2):
                for hp in range(4):
                    sap = mk_ap(scr, (4 * g + hp) * SPAN, [[0, 32], [1, SPAN]])
                    dap = mk_ap(t1S[g], 32 * hp * 512, [[512, 32], [1, SPAN]])
                    out.append(sy.dma_start(out=dap, in_=sap))
                    sap = mk_ap(scr, 32768 + (4 * g + hp) * SPAN, [[0, 32], [1, SPAN]])
                    dap = mk_ap(t1I[g], 32 * hp * 512, [[512, 32], [1, SPAN]])
                    out.append(sy.dma_start(out=dap, in_=sap))
            return out
        seq.dma("gpsimd", ft12, n=16)

        seq.dma("gpsimd", lambda sy: sy.dma_start(out=scr.ap()[2:3, 0:8 * NW], in_=dcf[:]))

        def fdcc(sy):
            out = []
            for pr in range(4):
                for h in range(2):
                    sap = mk_ap(scr, 2 * 32768 + (2 * pr + h) * NW, [[0, 64], [1, NW]])
                    dap = mk_ap(dccol[pr], 64 * h * NW, [[NW, 64], [1, NW]])
                    out.append(sy.dma_start(out=dap, in_=sap))
            return out
        seq.dma("gpsimd", fdcc, n=8)

        for w in range(NW):
            def fGall(t, w=w):
                for g in range(2):
                    for hp in range(4):
                        ktile = kf[2 * g + hp // 2]
                        qtile = qf[2 * g + hp // 2]
                        base = (hp % 2) * 64
                        t.matmul(psb[2 + 2 * g].ap()[32 * hp:32 * (hp + 1), 0:32],
                                 ktile.ap()[base:base + 64, 32 * w:32 * (w + 1)],
                                 ktile.ap()[base:base + 64, 32 * w:32 * (w + 1)],
                                 start=True, stop=True, tile_position=(base, 32 * hp))
                        t.matmul(psb[3 + 2 * g].ap()[32 * hp:32 * (hp + 1), 0:32],
                                 ktile.ap()[base:base + 64, 32 * w:32 * (w + 1)],
                                 qtile.ap()[base:base + 64, 32 * w:32 * (w + 1)],
                                 start=True, stop=True, tile_position=(base, 32 * hp))
            seq.c("tensor", fGall)

            def fDall(v, w=w):
                for g in range(2):
                    v.scalar_tensor_tensor(dsS[g][:], t1S[g].ap()[:, 32 * w:32 * (w + 1)],
                                           colstk[g].ap()[:, 8 * w:8 * w + 1], maskS[:],
                                           AL.subtract, AL.add)
                    v.scalar_tensor_tensor(dsI[g][:], t1I[g].ap()[:, 32 * w:32 * (w + 1)],
                                           colstk[g].ap()[:, 8 * w:8 * w + 1], maskI[:],
                                           AL.subtract, AL.add)
            seq.c("vector", fDall)

            def fEall(s, w=w):
                for g in range(2):
                    s.activation(esS[g][:], dsS[g][:], AF.Exp)
                    s.activation(esI[g][:], dsI[g][:], AF.Exp)
            seq.c("scalar", fEall)

            def fWMev(v, w=w):
                for g in range(2):
                    for hp in range(4):
                        v.scalar_tensor_tensor(
                            wbd[g].ap()[32 * hp:32 * (hp + 1), 32 * hp:32 * (hp + 1)],
                            psb[2 + 2 * g].ap()[32 * hp:32 * (hp + 1), 0:32],
                            colstk[g].ap()[32 * hp:32 * (hp + 1), 8 * w + 5:8 * w + 6],
                            esS[g].ap()[32 * hp:32 * (hp + 1), 0:32], AL.mult, AL.mult)
                        v.scalar_tensor_tensor(
                            mbd[g].ap()[32 * hp:32 * (hp + 1), 32 * hp:32 * (hp + 1)],
                            psb[3 + 2 * g].ap()[32 * hp:32 * (hp + 1), 0:32],
                            colstk[g].ap()[32 * hp:32 * (hp + 1), 8 * w + 1:8 * w + 2],
                            esI[g].ap()[32 * hp:32 * (hp + 1), 0:32], AL.mult, AL.mult)
            seq.c("vector", fWMev)

            def fRQS(t, w=w):
                for g in range(2):
                    for hp in range(4):
                        ktile = kf[2 * g + hp // 2]
                        qtile = qf[2 * g + hp // 2]
                        base = (hp % 2) * 64
                        st = s_st[2 * g + hp // 2]
                        t.matmul(psb[2 + 2 * g].ap()[32 * hp:32 * (hp + 1), 0:64],
                                 ktile.ap()[base:base + 64, 32 * w:32 * (w + 1)],
                                 st.ap()[base:base + 64, 0:64],
                                 start=True, stop=True, tile_position=(base, 32 * hp))
                        t.matmul(psb[3 + 2 * g].ap()[32 * hp:32 * (hp + 1), 0:64],
                                 qtile.ap()[base:base + 64, 32 * w:32 * (w + 1)],
                                 st.ap()[base:base + 64, 0:64],
                                 start=True, stop=True, tile_position=(base, 32 * hp))
            seq.c("tensor", fRQS)

            def fvryt(v, w=w):
                for g in range(2):
                    v.scalar_tensor_tensor(vrS[g].ap()[:, 0:64],
                                           psb[2 + 2 * g].ap()[:, 0:64],
                                           colstk[g].ap()[:, 8 * w + 2:8 * w + 3],
                                           vch[g].ap()[:, 64 * w:64 * (w + 1)],
                                           AL.mult, AL.add)
                    v.tensor_scalar(ych[g].ap()[:, 64 * w:64 * (w + 1)],
                                    psb[3 + 2 * g].ap()[:, 0:64],
                                    colstk[g].ap()[:, 8 * w + 4:8 * w + 5], None, AL.mult)
            seq.c("vector", fvryt)

            ch["pe_base"] = ch["pe"]
            ch["dv_base"] = ch["dv"]

            def chain_T2(t, w=w, pe0=ch["pe"], dv0=ch["dv"]):
                for st_i in range(32):
                    for g in range(2):
                        t.wait_ge(ch_dv, dv0 + 2 * (st_i - 1) + g + 1 if st_i > 0 else dv0)
                        mm = t.matmul(ps_chain[g].ap()[:, 0:64], wbd[g][:], u_t[g][:],
                                      start=True, stop=True)
                        mm.then_inc(ch_pe, 1)
                t.matmul(psb[6].ap()[0:1, 64:65], ones_col.ap()[0:1, 0:1],
                         ones_col.ap()[0:1, 0:1], start=True, stop=True)

            def chain_V2(v, w=w, pe0=ch["pe"], dv0=ch["dv"]):
                for st_i in range(32):
                    for g in range(2):
                        v.wait_ge(ch_pe, pe0 + 2 * st_i + g + 1)
                        v.tensor_tensor(tmpc[g].ap()[:, 0:64], ps_chain[g].ap()[:, 0:64],
                                        vrS[g].ap()[:, 0:64], AL.add)
                        uop = v.tensor_tensor(u_t[g][:], bch[g].ap()[:, 64 * w:64 * (w + 1)],
                                              tmpc[g].ap()[:, 0:64], AL.mult)
                        uop.then_inc(ch_dv, 1)
                v.tensor_copy(dsS[0].ap()[0:1, 0:1], dsS[0].ap()[0:1, 0:1])

            ch["pe"] += 64
            ch["dv"] += 64
            seq.raw({"tensor": chain_T2, "vector": chain_V2})

            def fMU2(t, w=w):
                for g in range(2):
                    t.matmul(psb[6 + g].ap()[:, 0:64], mbd[g][:], u_t[g][:],
                             start=True, stop=True)
            seq.c("tensor", fMU2)

            def fyaU(v, w=w):
                for g in range(2):
                    v.tensor_tensor(ych[g].ap()[:, 64 * w:64 * (w + 1)],
                                    ych[g].ap()[:, 64 * w:64 * (w + 1)],
                                    psb[6 + g].ap()[:, 0:64], AL.add)
                    v.tensor_scalar(tmpc[g].ap()[:, 0:64], u_t[g].ap()[:, 0:64],
                                    colstk[g].ap()[:, 8 * w + 3:8 * w + 4], None, AL.mult)
            seq.c("vector", fyaU)

            def fKU2(t, w=w):
                for g in range(2):
                    for hp in range(4):
                        dstps = psb[2 + 2 * g] if hp < 2 else psb[3 + 2 * g]
                        t.matmul(dstps.ap()[64 * (hp % 2):64 * (hp % 2) + 64, 0:64],
                                 ktm[g].ap()[32 * hp:32 * (hp + 1), 64 * w:64 * (w + 1)],
                                 tmpc[g].ap()[32 * hp:32 * (hp + 1), 0:64],
                                 start=True, stop=True,
                                 tile_position=(32 * hp, 64 * (hp % 2)))
            seq.c("tensor", fKU2)

            def fSu2(v, w=w):
                for g in range(2):
                    v.scalar_tensor_tensor(s_st[2 * g][:], s_st[2 * g][:],
                                           dccol[2 * g].ap()[:, w:w + 1],
                                           psb[2 + 2 * g].ap()[:, 0:64], AL.mult, AL.add)
                    v.scalar_tensor_tensor(s_st[2 * g + 1][:], s_st[2 * g + 1][:],
                                           dccol[2 * g + 1].ap()[:, w:w + 1],
                                           psb[3 + 2 * g].ap()[:, 0:64], AL.mult, AL.add)
            seq.c("vector", fSu2)

        if debug and sp == 0:
            seq.dma("gpsimd", lambda sy: sy.dma_start(out=dbg.tensor.ap()[:, 2448:3472],
                                                      in_=ych[0][:]))

        for g in range(2):
            for w in range(NW):
                def fct(t, g=g, w=w):
                    t.transpose(psb[5].ap()[0:64, 0:128], ych[g].ap()[:, 64 * w:64 * (w + 1)],
                                identt[:])
                seq.c("tensor", fct)

                def fce(v, g=g, w=w):
                    for hp in range(4):
                        h = 4 * g + hp
                        v.tensor_copy(ctxf[h // 2].ap()[(h % 2) * 64:(h % 2) * 64 + 64,
                                                        32 * w:32 * (w + 1)],
                                      psb[5].ap()[0:64, 32 * hp:32 * (hp + 1)])
                seq.c("vector", fce)

        proj_fm("wgT", vstage)

        def fsilu(s):
            load_act_set(s, nc, SET_SILU)
            for ot in range(4):
                s.activation(vstage[ot][:], vstage[ot][:], AF.Silu)
        seq.c("scalar", fsilu)

        for tile in range(4):
            def fsqy(s, tile=tile):
                load_act_set(s, nc, SET_EXPLN)
                s.activation(pf[:], ctxf[tile][:], AF.Square)
            seq.c("scalar", fsqy)

            def fnmy(t, tile=tile):
                t.matmul(psb[2].ap()[0:64, 0:512], sel.ap()[:, 64 * tile:64 * (tile + 1)],
                         pf[:], start=(tile == 0), stop=(tile == 3))
            seq.c("tensor", fnmy)

        def fyrs(v):
            v.tensor_copy(spf2[:], psb[2].ap()[0:8, 0:512])
            v.tensor_scalar(spf2[:], spf2[:], 1.0 / 64.0, 1e-6, AL.mult, AL.add)
        seq.c("vector", fyrs)

        def fyr0(s):
            s.activation(spf[:], spf2[:], AF.Ln)
            s.activation(spf[:], spf[:], AF.Exp, scale=-0.5)
        seq.c("scalar", fyr0)

        def fyrn(v):
            v.tensor_tensor(pf.ap()[0:8, 0:512], spf[:], spf[:], AL.mult)
            v.tensor_tensor(pf.ap()[0:8, 0:512], pf.ap()[0:8, 0:512], spf2[:], AL.mult)
            v.tensor_scalar(pf.ap()[0:8, 0:512], pf.ap()[0:8, 0:512], -0.5, 1.5, AL.mult, AL.add)
            v.tensor_tensor(spf2[:], spf[:], pf.ap()[0:8, 0:512], AL.mult)
        seq.c("vector", fyrn)

        seq.dma("gpsimd", lambda sy: sy.dma_start(out=scr.ap()[3:4, 0:8 * SPAN], in_=spf2[:]))

        def frr(sy):
            out = []
            for i in range(4):
                for hh in range(2):
                    dst_t = bch[i // 2]
                    sap = mk_ap(scr, 3 * 32768 + (2 * i + hh) * SPAN, [[0, 64], [1, SPAN]])
                    dap = mk_ap(dst_t, (i % 2) * 512 + hh * 64 * 1024, [[1024, 64], [1, SPAN]])
                    out.append(sy.dma_start(out=dap, in_=sap))
            return out
        seq.dma("gpsimd", frr, n=8)

        def fapply(v):
            for i in range(4):
                rep = bch[i // 2].ap()
                rep = mk_ap(bch[i // 2], (i % 2) * 512, [[1024, 128], [1, 512]])
                v.tensor_tensor(ctxf[i][:], ctxf[i][:], rep, AL.mult)
                v.tensor_tensor(ctxf[i][:], ctxf[i][:], vstage[i][:], AL.mult)
                v.tensor_scalar(ctxf[i][:], ctxf[i][:], nwfm[:], None, AL.mult)
        seq.c("vector", fapply)

        def fcout(sy, toff=toff):
            return [sy.dma_start(out=ctx_hbm.tensor.ap()[128 * i:128 * (i + 1),
                                                         toff:toff + SPAN],
                                 in_=ctxf[i][:]) for i in range(4)]
        seq.dma("gpsimd", fcout, n=4)

        if debug and sp == 0:
            dumps = [(caf[:], 0, 8, 0, 512), (knf[:], 0, 8, 3584, 4096),
                     (colstk[0][:], 0, 128, 512, 640), (t1S[0][:], 0, 128, 640, 1152),
                     (wbd[0][:], 0, 128, 1152, 1280), (u_t[0][:], 0, 128, 1280, 1344),
                     (ych[0][:], 0, 128, 1344, 2368), (s_st[0][:], 0, 128, 2368, 2432),
                     (vrS[0][:], 0, 128, 3472, 3536), (dccol[0][:], 0, 128, 3536, 3552)]
            for src_ap, p0, p1, c0, c1 in dumps:
                seq.dma("gpsimd", lambda sy, src_ap=src_ap, p0=p0, p1=p1, c0=c0, c1=c1:
                        sy.dma_start(out=dbg.tensor.ap()[p0:p1, c0:c1], in_=src_ap))

    LAST_MARKS["end"] = len(seq.items)
    if trunc is not None:
        seq.items = seq.items[:trunc]
    with (
        nc.semaphore("c_sem") as c_sem,
        nc.semaphore("d_sync") as d_sync,
        nc.semaphore("d_gp") as d_gp,
        nc.Block() as block,
    ):
        d_sems = {"sync": d_sync, "gpsimd": d_gp}
        n_items, n_dma = seq.build(block, c_sem, d_sems)

        @block.gpsimd
        def _(g):
            g.wait_ge(c_sem, n_items)
            for k, s in d_sems.items():
                g.wait_ge(s, 16 * n_dma[k])
    return nc



def build_mamba_a_phase(T_SEQ=4096, pre_ln=False):
    SPAN = 512
    NSP = T_SEQ // SPAN
    NCH = SPAN // 64
    NPC = 1216
    nc = bass.Bass("TRN2", target_bir_lowering=False, detect_race_conditions=False)
    x_in = nc.dram_tensor("x", [D, T_SEQ], F32, kind="ExternalInput").ap()
    inwT_in = nc.dram_tensor("inwT", [D, NPC], F32, kind="ExternalInput").ap()
    nw1_in = nc.dram_tensor("nw1", [128, 8], F32, kind="ExternalInput").ap()
    convw_in = nc.dram_tensor("convw", [128, 20], F32, kind="ExternalInput").ap()
    convb_in = nc.dram_tensor("convb", [128, 5], F32, kind="ExternalInput").ap()
    dtb_in = nc.dram_tensor("dtb", [8, 1], F32, kind="ExternalInput").ap()
    alog_in = nc.dram_tensor("alog", [8, 1], F32, kind="ExternalInput").ap()
    dcolf_in = nc.dram_tensor("dcolf", [128, 4], F32, kind="ExternalInput").ap()
    mask64_in = nc.dram_tensor("mask64", [128, 64], F32, kind="ExternalInput").ap()
    ident_in = nc.dram_tensor("ident", [128, 128], F32, kind="ExternalInput").ap()
    y_hbm = nc.dram_tensor("y", [512, T_SEQ], F32, kind="ExternalOutput").ap()
    z_hbm = nc.dram_tensor("z", [512, T_SEQ], F32, kind="ExternalOutput").ap()
    ln_hbm = nc.dram_tensor("ln0", [D, T_SEQ], F32, kind="ExternalOutput").ap() if pre_ln else None
    scr = nc.dram_tensor("scr_m", [8, 32768], F32, kind="Internal")

    A = nc.alloc_sbuf_tensor
    WH = A("hw", [128, 8 * NPC], BF16)
    WL = A("lw", [128, 8 * NPC], BF16)
    xt = [A(f"xt{i}", [128, SPAN], F32) for i in range(8)]
    wstage = xt[0]
    ph = [A(f"ph{i}", [128, SPAN], BF16) for i in range(8)]
    pl = [A(f"pl{i}", [128, SPAN], BF16) for i in range(8)]
    pf = A("pf", [128, SPAN], F32)
    sqs = A("sqs", [128, 512], F32)
    stat = A("stat", [1, 3 * SPAN], F32)
    nw1 = A("nw1c", [128, 8], F32)
    convw = A("convw_s", [128, 20], F32)
    convb = A("convb_s", [128, 5], F32)
    dtb = A("dtb_s", [8, 1], F32)
    nega = A("nega_s", [8, 1], F32)
    dcolf = A("dcolf_s", [128, 4], F32)
    mask64 = A("mask64_s", [128, 64], F32)
    identt = A("identt", [128, 128], F32)
    ones_col = A("ones_col", [128, 1], F32)
    ones_row = A("ones_row", [1, 128], F32)
    cvs = [A(f"cvs{i}", [128, SPAN + 3], F32) for i in range(5)]
    xfm = [A(f"xfm{i}", [128, SPAN], F32) for i in range(4)]
    bcfm = A("bcfm", [128, SPAN], F32)
    cC = A("cC", [128, SPAN], F32)
    xch = [A(f"xch{i}", [128, SPAN], F32) for i in range(4)]
    btm2 = A("btm2", [128, SPAN], F32)
    csrep = [A(f"csrep{i}", [128, SPAN], F32) for i in range(4)]
    colm = [A(f"colm{i}", [128, 8 * NCH], F32) for i in range(4)]
    dccm = [A(f"dccm{i}", [128, NCH], F32) for i in range(4)]
    yfm = [A(f"yfm{i}", [128, SPAN], F32) for i in range(4)]
    gts2 = A("gts2", [128, 64 * NCH], F32)
    mt = [A(f"mt{i}", [128, 64], F32) for i in range(4)]
    dscm = [A(f"dscm{i}", [128, 64], F32) for i in range(4)]
    escm = [A(f"escm{i}", [128, 64], F32) for i in range(4)]
    xsc = [A(f"xsc{i}", [128, 64], F32) for i in range(4)]
    ytp = [A(f"ytp{i}", [128, 64], F32) for i in range(4)]
    s_st = [A(f"sst{i}", [128, 64], F32) for i in range(4)]
    dtraw = A("dtraw", [8, SPAN], F32)
    dtf = A("dtf", [8, SPAN], F32)
    csr = A("csr", [8, SPAN], F32)
    decdt = A("decdt", [8, SPAN], F32)
    yod = A("yod", [8, SPAN], F32)
    rsc1 = A("rsc1", [8, SPAN], F32)
    rsc2 = A("rsc2", [8, SPAN], F32)
    dcfm = A("dcfm", [8, NCH], F32)
    cpm = A("cpm", [8, NCH], F32)

    psb = [nc.alloc_psum_tensor(f"psM{i}", [128, 512], F32) for i in range(6)]
    ps_Yd, ps_Yo, ps_proj, ps_G, ps_S, ps_T = psb[0], psb[1], psb[2], psb[3], psb[4], psb[5]

    seq = Seq(nc)

    def fconst(v):
        v.memset(ones_col[:], 1.0)
        v.memset(ones_row[:], 1.0)
        for i in range(4):
            v.memset(s_st[i][:], 0.0)
        for i in range(5):
            v.memset(cvs[i].ap()[:, 0:3], 0.0)
    seq.c("vector", fconst)
    for nm, dst in [("nw1", nw1), ("convw", convw), ("convb", convb), ("dtb", dtb),
                    ("alog", nega), ("dcolf", dcolf), ("mask64", mask64), ("ident", identt)]:
        seq.dma("sync", lambda s, nm=nm, dst=dst: s.dma_start(
            out=dst[:], in_=nc.lookup_tensor(nm).ap() if False else {"nw1": nw1_in,
            "convw": convw_in, "convb": convb_in, "dtb": dtb_in, "alog": alog_in,
            "dcolf": dcolf_in, "mask64": mask64_in, "ident": ident_in}[nm]))

    def fnega(s):
        load_act_set(s, nc, SET_EXPLN)
        s.activation(nega[:], nega[:], AF.Exp)
    seq.c("scalar", fnega)
    seq.c("vector", lambda v: v.tensor_scalar(nega[:], nega[:], -1.0, None, AL.mult))

    for dch in range(8):
        for pc in range(3):
            c0, c1 = 512 * pc, min(512 * (pc + 1), NPC)
            seq.dma("sync", lambda s, dch=dch, c0=c0, c1=c1: s.dma_start(
                out=wstage.ap()[:, 0:c1 - c0],
                in_=inwT_in.tensor.ap()[128 * dch:128 * (dch + 1), c0:c1]))

            def fsp_(v, dch=dch, c0=c0, c1=c1):
                v.tensor_copy(WH.ap()[:, NPC * dch + c0:NPC * dch + c1],
                              wstage.ap()[:, 0:c1 - c0])
                v.tensor_tensor(WL.ap()[:, NPC * dch + c0:NPC * dch + c1],
                                wstage.ap()[:, 0:c1 - c0],
                                WH.ap()[:, NPC * dch + c0:NPC * dch + c1], AL.subtract)
            seq.c("vector", fsp_)

    for sp in range(NSP):
        toff = SPAN * sp

        def fxl(s, toff=toff):
            return [s.dma_start(out=xt[i][:],
                                in_=x_in.tensor.ap()[128 * i:128 * (i + 1), toff:toff + SPAN])
                    for i in range(8)]
        seq.dma("sync", fxl, n=8)

        if pre_ln:
            ctx0 = dict(xt=xt, ph=ph, pl=pl, sq=sqs.ap()[:, 0:512], pf=pf, stat=stat,
                        ones_col=ones_col, ones_row=ones_row, eps=1e-5, out_f32=True,
                        ps_stA=psb[2].ap()[0:1, 0:512], ps_stB=psb[3].ap()[0:1, 0:512],
                        ps_rep=psb[4], ps_rep2=psb[5])
            emit_ln_prep(seq, nc, ctx0, SPAN, with_mean=True)
            def flno(sy, toff=toff):
                return [sy.dma_start(out=ln_hbm.tensor.ap()[128 * i:128 * (i + 1),
                                                            toff:toff + SPAN],
                                     in_=xt[i][:]) for i in range(8)]
            seq.dma("gpsimd", flno, n=8)
        ctxd = dict(xt=xt, ph=ph, pl=pl, sq=sqs.ap()[:, 0:512], pf=pf, stat=stat,
                    ones_col=ones_col, ones_row=ones_row, eps=1e-5,
                    wcol=[nw1.ap()[:, i:i + 1] for i in range(8)],
                    ps_stA=psb[2].ap()[0:1, 0:512], ps_stB=psb[3].ap()[0:1, 0:512],
                    ps_rep=psb[4], ps_rep2=psb[5])
        emit_ln_prep(seq, nc, ctxd, SPAN, with_mean=False)

        if sp > 0:
            def fcar(v):
                for i in range(5):
                    v.tensor_copy(cvs[i].ap()[:, 0:3], cvs[i].ap()[:, SPAN:SPAN + 3])
            seq.c("vector", fcar)

        def fproj(ot, dst_ap, m=128):
            def fmm(t, ot=ot, m=m):
                for dch in range(8):
                    wsh = WH.ap()[:, NPC * dch + 128 * ot:NPC * dch + 128 * ot + m]
                    wsl = WL.ap()[:, NPC * dch + 128 * ot:NPC * dch + 128 * ot + m]
                    t.matmul(ps_proj.ap()[0:m, 0:512], wsh, ph[dch][:], start=(dch == 0), stop=False)
                    t.matmul(ps_proj.ap()[0:m, 0:512], wsl, ph[dch][:], start=False, stop=False)
                    t.matmul(ps_proj.ap()[0:m, 0:512], wsh, pl[dch][:], start=False, stop=(dch == 7))
            seq.c("tensor", fmm)
            seq.c("vector", lambda v, dst_ap=dst_ap: v.tensor_copy(dst_ap, ps_proj.ap()[0:dst_ap.ap[0][1], 0:512]))

        for i in range(4):
            fproj(i, xt[i][:])

        def fzo(sy, toff=toff):
            return [sy.dma_start(out=z_hbm.tensor.ap()[128 * i:128 * (i + 1),
                                                       toff:toff + SPAN],
                                 in_=xt[i][:]) for i in range(4)]
        seq.dma("gpsimd", fzo, n=4)
        for i in range(4):
            fproj(4 + i, cvs[i].ap()[:, 3:SPAN + 3])
        fproj(8, cvs[4].ap()[:, 3:SPAN + 3])
        def fmmdt(t):
            for dch in range(8):
                wsh = WH.ap()[:, NPC * dch + 1152:NPC * dch + 1216]
                wsl = WL.ap()[:, NPC * dch + 1152:NPC * dch + 1216]
                t.matmul(ps_proj.ap()[0:64, 0:512], wsh, ph[dch][:], start=(dch == 0), stop=False)
                t.matmul(ps_proj.ap()[0:64, 0:512], wsl, ph[dch][:], start=False, stop=False)
                t.matmul(ps_proj.ap()[0:64, 0:512], wsh, pl[dch][:], start=False, stop=(dch == 7))
        seq.c("tensor", fmmdt)
        seq.c("vector", lambda v: v.tensor_copy(dtraw[:], ps_proj.ap()[0:8, 0:512]))

        def fconv(v):
            for i in range(5):
                out = xfm[i][:] if i < 4 else bcfm[:]
                v.tensor_scalar(out, cvs[i].ap()[:, 3:SPAN + 3], convw.ap()[:, 4 * i + 3:4 * i + 4],
                                convb.ap()[:, i:i + 1], AL.mult, AL.add)
                for tap in range(3):
                    v.scalar_tensor_tensor(out, cvs[i].ap()[:, tap:tap + SPAN],
                                           convw.ap()[:, 4 * i + tap:4 * i + tap + 1],
                                           out, AL.mult, AL.add)
        seq.c("vector", fconv)

        def fsilu(s):
            load_act_set(s, nc, SET_SILU)
            for i in range(4):
                s.activation(xfm[i][:], xfm[i][:], AF.Silu)
            s.activation(bcfm[:], bcfm[:], AF.Silu)
        seq.c("scalar", fsilu)

        def fdt1(v):
            v.tensor_scalar(dtraw[:], dtraw[:], dtb[:], None, AL.add)
            v.tensor_scalar_max(rsc1[:], dtraw[:], 0.0)
        seq.c("vector", fdt1)

        def fdt2(s):
            load_act_set(s, nc, SET_EXPLN)
            s.activation(dtf[:], dtraw[:], AF.Abs)
            s.activation(dtf[:], dtf[:], AF.Exp, scale=-1.0)
            s.activation(dtf[:], dtf[:], AF.Ln, bias=1.0)
        seq.c("scalar", fdt2)

        def fdt3(v):
            v.tensor_tensor(dtf[:], dtf[:], rsc1[:], AL.add)
            v.tensor_scalar(csr[:], dtf[:], nega[:], None, AL.mult)
            v.tensor_scalar(rsc1[:], rsc1[:], 0.0, 1.0, AL.mult, AL.add)
            v.tensor_tensor_scan(rsc2[:], rsc1[:], csr[:], 0.0, AL.mult, AL.add)
            v.tensor_copy(csr[:], rsc2[:])
            v.memset(cpm.ap()[0:8, 0:1], 0.0)
            v.tensor_copy(cpm.ap()[0:8, 1:NCH], mk_ap(csr, 63, [[SPAN, 8], [64, NCH - 1]]))
            v.tensor_tensor(decdt[:], csr[:], mk_ap(csr, 63, [[SPAN, 8], [64, NCH], [0, 64]]),
                            AL.subtract)
            v.tensor_tensor(yod[:], csr[:], mk_ap(cpm, 0, [[NCH, 8], [1, NCH], [0, 64]]),
                            AL.subtract)
        seq.c("vector", fdt3)

        def fdt4(s):
            s.activation(decdt[:], decdt[:], AF.Exp, scale=-1.0)
            s.activation(yod[:], yod[:], AF.Exp)
        seq.c("scalar", fdt4)

        def fdt5(v):
            v.tensor_tensor(decdt[:], decdt[:], dtf[:], AL.mult)
            v.tensor_copy(dcfm[:], mk_ap(yod, 63, [[SPAN, 8], [64, NCH]]))
        seq.c("vector", fdt5)

        def fst45(sy):
            return [sy.dma_start(out=scr.ap()[4:5, 0:8 * SPAN], in_=csr[:]),
                    sy.dma_start(out=scr.ap()[5:6, 0:8 * NCH], in_=dcfm[:])]
        seq.dma("gpsimd", fst45, n=2)

        def fcsdc(sy):
            out = []
            for i in range(4):
                for hh in range(2):
                    sap = mk_ap(scr, 4 * 32768 + (2 * i + hh) * SPAN, [[0, 64], [1, SPAN]])
                    dap = mk_ap(csrep[i], hh * 64 * SPAN, [[SPAN, 64], [1, SPAN]])
                    out.append(sy.dma_start(out=dap, in_=sap))
                    sap = mk_ap(scr, 5 * 32768 + (2 * i + hh) * NCH, [[0, 64], [1, NCH]])
                    dap = mk_ap(dccm[i], hh * 64 * NCH, [[NCH, 64], [1, NCH]])
                    out.append(sy.dma_start(out=dap, in_=sap))
            return out
        seq.dma("gpsimd", fcsdc, n=16)

        def fcc(sy):
            sap = mk_ap(bcfm, 64 * SPAN, [[SPAN, 64], [1, SPAN]])
            return [sy.dma_start(out=mk_ap(cC, 0, [[SPAN, 64], [1, SPAN]]), in_=sap),
                    sy.dma_start(out=mk_ap(cC, 64 * SPAN, [[SPAN, 64], [1, SPAN]]), in_=sap)]
        seq.dma("gpsimd", fcc, n=2)

        tmk = [0]

        def emit_tmstage(src_tile, full=False):
            LAGM = None if full else {"gpsimd": 12}
            stg = sqs if tmk[0] % 2 == 0 else pf
            tmk[0] += 1

            def ftr(t, src_tile=src_tile):
                for blk in range(4):
                    t.transpose(ps_T.ap()[:, 128 * blk:128 * (blk + 1)],
                                src_tile.ap()[:, 128 * blk:128 * (blk + 1)], identt[:])
            seq.c("tensor", ftr, dma_lag=LAGM)
            seq.c("vector", lambda v, stg=stg: v.tensor_copy(stg.ap()[:, 0:512],
                                                             ps_T.ap()[:, 0:512]),
                  dma_lag=LAGM)

            def fst(sy, stg=stg):
                return [sy.dma_start(out=scr.ap()[blk:blk + 1, 0:16384],
                                     in_=stg.ap()[:, 128 * blk:128 * (blk + 1)])
                        for blk in range(4)]
            seq.dma("gpsimd", fst, n=4)

        for i in range(4):
            emit_tmstage(xfm[i], full=(i == 0))

            def fg(sy, i=i):
                out = []
                for blk in range(4):
                    for slot in range(2):
                        sap = mk_ap(scr, blk * 32768 + slot * 64,
                                    [[128, 64], [64 * 128, 2], [1, 64]])
                        dap = mk_ap(xch[i], slot * 64 * SPAN + blk * 128,
                                    [[SPAN, 64], [64, 2], [1, 64]])
                        out.append(sy.dma_start(out=dap, in_=sap))
                return out
            seq.dma("gpsimd", fg, n=8)

        emit_tmstage(bcfm)

        def fgb(sy):
            out = []
            for blk in range(4):
                for dup in range(2):
                    sap = mk_ap(scr, blk * 32768, [[128, 64], [64 * 128, 2], [1, 64]])
                    dap = mk_ap(btm2, dup * 64 * SPAN + blk * 128,
                                [[SPAN, 64], [64, 2], [1, 64]])
                    out.append(sy.dma_start(out=dap, in_=sap))
            return out
        seq.dma("gpsimd", fgb, n=8)

        def frc(sy):
            out = []
            for i in range(4):
                for q, src_fm in enumerate([csr, decdt, dtf, yod]):
                    for h in range(2):
                        sap = mk_ap(src_fm, (2 * i + h) * SPAN, [[SPAN, 1], [1, SPAN]])
                        dap = mk_ap(xt[i], (64 * h + q) * SPAN, [[SPAN, 1], [1, SPAN]])
                        out.append(sy.dma_start(out=dap, in_=sap))
            return out
        seq.dma("gpsimd", frc, n=32)
        for i in range(4):
            emit_tmstage(xt[i], full=(i == 0))

            def fgc(sy, i=i):
                out = []
                for blk in range(4):
                    for slot in range(2):
                        sap = mk_ap(scr, blk * 32768 + slot * 64,
                                    [[128, 64], [64 * 128, 2], [1, 8]])
                        dap = mk_ap(colm[i], slot * 64 * (8 * NCH) + blk * 16,
                                    [[8 * NCH, 64], [8, 2], [1, 8]])
                        out.append(sy.dma_start(out=dap, in_=sap))
                return out
            seq.dma("gpsimd", fgc, n=8)

        for c in range(NCH):
            def fGt(t, c=c):
                t.matmul(ps_G.ap()[0:64, 0:64], bcfm.ap()[0:64, 64 * c:64 * (c + 1)],
                         cC.ap()[0:64, 64 * c:64 * (c + 1)], start=True, stop=True)
            seq.c("tensor", fGt)
            seq.c("vector", lambda v, c=c: v.tensor_copy(gts2.ap()[0:64, 64 * c:64 * (c + 1)],
                                                         ps_G.ap()[0:64, 0:64]))

            def fgdup(sy, c=c):
                sap = mk_ap(gts2, 64 * c, [[64 * NCH, 64], [1, 64]])
                dap = mk_ap(gts2, 64 * 64 * NCH + 64 * c, [[64 * NCH, 64], [1, 64]])
                return sy.dma_start(out=dap, in_=sap)
            seq.dma("gpsimd", fgdup)

            def fL(v, c=c):
                for i in range(4):
                    v.scalar_tensor_tensor(dscm[i][:], csrep[i].ap()[:, 64 * c:64 * (c + 1)],
                                           colm[i].ap()[:, 8 * c:8 * c + 1], mask64[:],
                                           AL.subtract, AL.add)
            seq.c("vector", fL)

            def fEx(s, c=c):
                for i in range(4):
                    s.activation(escm[i][:], dscm[i][:], AF.Exp)
            seq.c("scalar", fEx)

            def fM(v, c=c):
                for i in range(4):
                    v.scalar_tensor_tensor(mt[i][:], gts2.ap()[:, 64 * c:64 * (c + 1)],
                                           colm[i].ap()[:, 8 * c + 2:8 * c + 3], escm[i][:],
                                           AL.mult, AL.mult)
                    v.tensor_scalar(xsc[i][:], xch[i].ap()[:, 64 * c:64 * (c + 1)],
                                    colm[i].ap()[:, 8 * c + 1:8 * c + 2], None, AL.mult)
            seq.c("vector", fM)

            def fmms(t, c=c):
                for i in range(4):
                    co = 64 * i
                    for sl in range(2):
                        b2 = 64 * sl
                        t.matmul(ps_Yd.ap()[b2:b2 + 64, co:co + 64], mt[i].ap()[b2:b2 + 64, 0:64],
                                 xch[i].ap()[b2:b2 + 64, 64 * c:64 * (c + 1)],
                                 start=True, stop=True, tile_position=(b2, b2))
                        t.matmul(ps_Yo.ap()[b2:b2 + 64, co:co + 64],
                                 cC.ap()[b2:b2 + 64, 64 * c:64 * (c + 1)],
                                 s_st[i].ap()[b2:b2 + 64, 0:64],
                                 start=True, stop=True, tile_position=(b2, b2))
                        t.matmul(ps_S.ap()[b2:b2 + 64, co:co + 64],
                                 btm2.ap()[b2:b2 + 64, 64 * c:64 * (c + 1)],
                                 xsc[i].ap()[b2:b2 + 64, 0:64],
                                 start=True, stop=True, tile_position=(b2, b2))
            seq.c("tensor", fmms)

            def fyev(v, c=c):
                for i in range(4):
                    co = 64 * i
                    v.tensor_copy(ytp[i][:], ps_Yd.ap()[:, co:co + 64])
                    v.scalar_tensor_tensor(ytp[i][:], ps_Yo.ap()[:, co:co + 64],
                                           colm[i].ap()[:, 8 * c + 3:8 * c + 4],
                                           ytp[i][:], AL.mult, AL.add)
                    v.scalar_tensor_tensor(ytp[i][:], xch[i].ap()[:, 64 * c:64 * (c + 1)],
                                           dcolf.ap()[:, i:i + 1], ytp[i][:], AL.mult, AL.add)
                    v.scalar_tensor_tensor(s_st[i][:], s_st[i][:],
                                           dccm[i].ap()[:, c:c + 1],
                                           ps_S.ap()[:, co:co + 64], AL.mult, AL.add)
            seq.c("vector", fyev)

            def fytr(t, c=c):
                for i in range(4):
                    co = 64 * i
                    for sl in range(2):
                        b2 = 64 * sl
                        t.matmul(ps_T.ap()[b2:b2 + 64, co:co + 64], ytp[i].ap()[b2:b2 + 64, 0:64],
                                 identt.ap()[b2:b2 + 64, b2:b2 + 64],
                                 start=True, stop=True, tile_position=(b2, b2))
            seq.c("tensor", fytr)

            def fyfm(v, c=c):
                for i in range(4):
                    co = 64 * i
                    for sl in range(2):
                        b2 = 64 * sl
                        v.tensor_copy(yfm[i].ap()[b2:b2 + 64, 64 * c:64 * (c + 1)],
                                      ps_T.ap()[b2:b2 + 64, co:co + 64])
            seq.c("vector", fyfm)

        def fyo(sy, toff=toff):
            return [sy.dma_start(out=y_hbm.tensor.ap()[128 * i:128 * (i + 1),
                                                       toff:toff + SPAN],
                                 in_=yfm[i][:]) for i in range(4)]
        seq.dma("gpsimd", fyo, n=4)

    with (
        nc.semaphore("c_sem") as c_sem,
        nc.semaphore("d_sync") as d_sync,
        nc.semaphore("d_gp") as d_gp,
        nc.Block() as block,
    ):
        d_sems = {"sync": d_sync, "gpsimd": d_gp}
        n_items, n_dma = seq.build(block, c_sem, d_sems)

        @block.gpsimd
        def _(g):
            g.wait_ge(c_sem, n_items)
            for k, s in d_sems.items():
                g.wait_ge(s, 16 * n_dma[k])
    return nc



def build_mamba_out_phase(TQ=1024, add_outer=False):
    SPAN = 512
    NSP = TQ // SPAN
    nc = bass.Bass("TRN2", target_bir_lowering=False, detect_race_conditions=False)
    x_in = nc.dram_tensor("x", [D, TQ], F32, kind="ExternalInput").ap()
    y_in = nc.dram_tensor("yf", [2048, TQ], F32, kind="ExternalInput").ap()
    z_in = nc.dram_tensor("zf", [2048, TQ], F32, kind="ExternalInput").ap()
    gw_in = nc.dram_tensor("gwc", [128, 16], F32, kind="ExternalInput").ap()
    owT_in = nc.dram_tensor("owT", [2048, 1024], F32, kind="ExternalInput").ap()
    xo_in = nc.dram_tensor("xo", [D, TQ], F32, kind="ExternalInput").ap() if add_outer else None
    out_hbm = nc.dram_tensor("out", [D, TQ], F32, kind="ExternalOutput").ap()

    A = nc.alloc_sbuf_tensor
    WH = A("how", [128, 16 * 1024], BF16)
    WL = A("low", [128, 16 * 1024], BF16)
    gt = [A(f"gt{i}", [128, SPAN], F32) for i in range(16)]
    wstage = gt[0]
    zst = A("zst", [128, SPAN], F32)
    xt = [A(f"xt{i}", [128, SPAN], F32) for i in range(8)]
    gh = [A(f"gh{i}", [128, SPAN], BF16) for i in range(16)]
    gl = [A(f"gl{i}", [128, SPAN], BF16) for i in range(16)]
    pf = A("pf", [128, SPAN], F32)
    sqs = A("sqs", [128, 512], F32)
    stat = A("stat", [1, 2 * SPAN], F32)
    gwc = A("gwc_s", [128, 16], F32)
    ones_col = A("ones_col", [128, 1], F32)
    ones_row = A("ones_row", [1, 128], F32)

    psb = [nc.alloc_psum_tensor(f"psO{i}", [128, 512], F32) for i in range(4)]
    ps_mm, ps_st, ps_rep = psb[0], psb[1], psb[2]

    seq = Seq(nc)
    seq.c("vector", lambda v: (v.memset(ones_col[:], 1.0), v.memset(ones_row[:], 1.0)))
    seq.dma("sync", lambda s: s.dma_start(out=gwc[:], in_=gw_in))
    for dch in range(16):
        for pc in range(2):
            c0 = 512 * pc
            seq.dma("sync", lambda s, dch=dch, c0=c0: s.dma_start(
                out=wstage.ap()[:, 0:512],
                in_=owT_in.tensor.ap()[128 * dch:128 * (dch + 1), c0:c0 + 512]))

            def fsp_(v, dch=dch, c0=c0):
                sl = slice(1024 * dch + c0, 1024 * dch + c0 + 512)
                v.tensor_copy(WH.ap()[:, sl], wstage.ap()[:, 0:512])
                v.tensor_tensor(WL.ap()[:, sl], wstage.ap()[:, 0:512], WH.ap()[:, sl],
                                AL.subtract)
            seq.c("vector", fsp_)

    for sp in range(NSP):
        toff = SPAN * sp

        def fxl(s, toff=toff):
            out = [s.dma_start(out=xt[i][:],
                               in_=x_in.tensor.ap()[128 * i:128 * (i + 1), toff:toff + SPAN])
                   for i in range(8)]
            out += [s.dma_start(out=gt[i][:],
                                in_=y_in.tensor.ap()[128 * i:128 * (i + 1), toff:toff + SPAN])
                    for i in range(16)]
            return out
        seq.dma("sync", fxl, n=24)
        for i in range(16):
            seq.dma("sync", lambda s, i=i, toff=toff: s.dma_start(
                out=zst[:], in_=z_in.tensor.ap()[128 * i:128 * (i + 1), toff:toff + SPAN]))

            def fg(sv, i=i):
                load_act_set(sv, nc, SET_SILU)
                sv.activation(zst[:], zst[:], AF.Silu)
            seq.c("scalar", fg)
            seq.c("vector", lambda v, i=i: v.tensor_tensor(gt[i][:], gt[i][:], zst[:], AL.mult))

        seq.c("scalar", lambda s: load_act_set(s, nc, SET_EXPLN))
        for i in range(16):
            seq.c("scalar", lambda s, i=i: s.activation(sqs.ap()[:, 0:512], gt[i][:], AF.Square))
            seq.c("tensor", lambda t, i=i: t.matmul(ps_st.ap()[0:1, 0:512], ones_col[:],
                                                    sqs.ap()[:, 0:512],
                                                    start=(i == 0), stop=(i == 15)))

        def fvar(v):
            v.tensor_copy(stat.ap()[0:1, 0:SPAN], ps_st.ap()[0:1, 0:512])
            v.tensor_scalar(stat.ap()[0:1, 0:SPAN], stat.ap()[0:1, 0:SPAN],
                            1.0 / 2048.0, 1e-5, AL.mult, AL.add)
        seq.c("vector", fvar)

        def frs(s):
            s.activation(stat.ap()[0:1, SPAN:2 * SPAN], stat.ap()[0:1, 0:SPAN], AF.Ln)
            s.activation(stat.ap()[0:1, SPAN:2 * SPAN], stat.ap()[0:1, SPAN:2 * SPAN],
                         AF.Exp, scale=-0.5)
        seq.c("scalar", frs)

        def fnw(v):
            v.tensor_tensor(pf.ap()[0:1, 0:SPAN], stat.ap()[0:1, SPAN:2 * SPAN],
                            stat.ap()[0:1, SPAN:2 * SPAN], AL.mult)
            v.tensor_tensor(pf.ap()[0:1, 0:SPAN], pf.ap()[0:1, 0:SPAN],
                            stat.ap()[0:1, 0:SPAN], AL.mult)
            v.tensor_scalar(pf.ap()[0:1, 0:SPAN], pf.ap()[0:1, 0:SPAN], -0.5, 1.5,
                            AL.mult, AL.add)
            v.tensor_tensor(stat.ap()[0:1, 0:SPAN], stat.ap()[0:1, SPAN:2 * SPAN],
                            pf.ap()[0:1, 0:SPAN], AL.mult)
        seq.c("vector", fnw)
        seq.c("tensor", lambda t: t.matmul(ps_rep[:], ones_row[:], stat.ap()[0:1, 0:SPAN],
                                           start=True, stop=True))
        for i in range(16):
            def fsc(v, i=i):
                v.tensor_tensor(gt[i][:], gt[i][:], ps_rep.ap()[:], AL.mult)
                v.tensor_scalar(gt[i][:], gt[i][:], gwc.ap()[:, i:i + 1], None, AL.mult)
                v.tensor_copy(gh[i][:], gt[i][:])
                v.tensor_tensor(gl[i][:], gt[i][:], gh[i][:], AL.subtract)
            seq.c("vector", fsc)

        for ot in range(8):
            def fmm(t, ot=ot):
                for dch in range(16):
                    wsh = WH.ap()[:, 1024 * dch + 128 * ot:1024 * dch + 128 * (ot + 1)]
                    wsl = WL.ap()[:, 1024 * dch + 128 * ot:1024 * dch + 128 * (ot + 1)]
                    t.matmul(ps_mm[:], wsh, gh[dch][:], start=(dch == 0), stop=False)
                    t.matmul(ps_mm[:], wsl, gh[dch][:], start=False, stop=False)
                    t.matmul(ps_mm[:], wsh, gl[dch][:], start=False, stop=(dch == 15))
            seq.c("tensor", fmm)
            seq.c("vector", lambda v, ot=ot: v.tensor_tensor(xt[ot][:], xt[ot][:],
                                                             ps_mm.ap()[:], AL.add))
            if add_outer:
                seq.dma("sync", lambda s, ot=ot, toff=toff: s.dma_start(
                    out=zst[:], in_=xo_in.tensor.ap()[128 * ot:128 * (ot + 1),
                                                      toff:toff + SPAN]))
                seq.c("vector", lambda v, ot=ot: v.tensor_tensor(xt[ot][:], xt[ot][:],
                                                                 zst[:], AL.add))
            seq.dma("gpsimd", lambda sy, ot=ot, toff=toff: sy.dma_start(
                out=out_hbm.tensor.ap()[128 * ot:128 * (ot + 1), toff:toff + SPAN],
                in_=xt[ot][:]))

    with (
        nc.semaphore("c_sem") as c_sem,
        nc.semaphore("d_sync") as d_sync,
        nc.semaphore("d_gp") as d_gp,
        nc.Block() as block,
    ):
        d_sems = {"sync": d_sync, "gpsimd": d_gp}
        n_items, n_dma = seq.build(block, c_sem, d_sems)

        @block.gpsimd
        def _(g):
            g.wait_ge(c_sem, n_items)
            for k, s in d_sems.items():
                g.wait_ge(s, 16 * n_dma[k])
    return nc


# version 09
# speedup vs baseline: 1.6258x; 1.0117x over previous
import numpy as np
import ml_dtypes

import concourse.bass as bass
import concourse.mybir as mybir
from concourse.bass_utils import run_bass_kernel_spmd

F32 = mybir.dt.float32
BF16 = mybir.dt.bfloat16
AL = mybir.AluOpType
AF = mybir.ActivationFunctionType

SET_EXPLN = 6
SET_SIGMOID = 2
SET_GELU = 10
SET_SILU = 18

D = 1024
ENGINES = ("tensor", "vector", "scalar", "gpsimd", "sync")
FEV_MODE = "stt"


def load_act_set(scalar, nc, idx):
    scalar.add_instruction(mybir.InstLoadActFuncSet(
        name=nc.get_next_instruction_name(), act_func_set_id=idx, ins=[], outs=[]))
    scalar.drain()


class _EngProxy:

    def __init__(self, eng):
        object.__setattr__(self, "_eng", eng)
        object.__setattr__(self, "last", None)

    def __getattr__(self, name):
        attr = getattr(self._eng, name)
        if not callable(attr):
            return attr

        def wrapper(*a, **k):
            r = attr(*a, **k)
            if r is not None and hasattr(r, "then_inc"):
                object.__setattr__(self, "last", r)
            return r
        return wrapper


class Seq:

    def __init__(self, nc):
        self.nc = nc
        self.items = []

    def c(self, eng, fn, dma_lag=None):
        self.items.append((eng, fn, ("c2", dma_lag) if dma_lag else "c"))

    def dma(self, eng, fn, n=1):
        self.items.append((eng, fn, ("d", n)))

    def raw(self, eng_fns):
        base = len(self.items)
        for eng, fn in eng_fns.items():
            self.items.append((eng, fn, ("r", base)))

    def build(self, block, c_sem, d_sems):
        counts = {k: 0 for k in d_sems}
        ndma_at = []
        for (e, _, kind) in self.items:
            ndma_at.append(dict(counts))
            if isinstance(kind, tuple) and kind[0] == "d":
                counts[e] += kind[1]

        def make_runner(name):
            def runner(eng):
                last_dwait = {k: 0 for k in d_sems}
                prev_idx = -2
                for i, (e, fn, kind) in enumerate(self.items):
                    if e != name:
                        continue
                    if isinstance(kind, tuple) and kind[0] == "r":
                        eng.wait_ge(c_sem, kind[1])
                        for k, n in ndma_at[i].items():
                            if n > last_dwait[k]:
                                eng.wait_ge(d_sems[k], 16 * n)
                                last_dwait[k] = n
                        proxy = _EngProxy(eng)
                        fn(proxy)
                        assert proxy.last is not None
                        proxy.last.then_inc(c_sem, 1)
                        prev_idx = i
                        continue
                    eng.wait_ge(c_sem, i)
                    lag = kind[1] if isinstance(kind, tuple) and kind[0] == "c2" else {}
                    for k, n in ndma_at[i].items():
                        n_eff = max(0, n - lag.get(k, 0))
                        if n_eff > last_dwait[k]:
                            eng.wait_ge(d_sems[k], 16 * n_eff)
                            last_dwait[k] = n_eff
                    if isinstance(kind, tuple) and kind[0] == "d":
                        ins = fn(eng)
                        if not isinstance(ins, (list, tuple)):
                            ins = [ins]
                        assert len(ins) == kind[1], f"item {i}: {len(ins)} != n={kind[1]}"
                        for q in ins:
                            q.then_inc(d_sems[e], 16)
                        eng.nop().then_inc(c_sem, 1)
                    else:
                        proxy = _EngProxy(eng)
                        fn(proxy)
                        assert proxy.last is not None, f"item {i} emitted no instruction"
                        proxy.last.then_inc(c_sem, 1)
                    prev_idx = i
            return runner

        for name in ENGINES:
            getattr(block, name)(make_runner(name))
        return len(self.items), dict(counts)



def emit_ln_prep(seq, nc, ctx, T, with_mean):
    xt, ph, pl = ctx["xt"], ctx["ph"], ctx["pl"]
    sq, pf, stat = ctx["sq"], ctx["pf"], ctx["stat"]
    ones_col, ones_row = ctx["ones_col"], ctx["ones_row"]
    ps_stA, ps_stB, ps_rep = ctx["ps_stA"], ctx["ps_stB"], ctx["ps_rep"]
    eps_t = ctx.get("eps_t")
    nts = T // 512

    seq.c("scalar", lambda s: load_act_set(s, nc, SET_EXPLN))
    for ts in range(nts):
        sl = slice(512 * ts, 512 * (ts + 1))
        for i in range(8):
            def fsq(s, i=i, sl=sl):
                s.activation(sq, xt[i].ap()[:, sl], AF.Square)
            seq.c("scalar", fsq)

            def fmm(t, i=i, sl=sl):
                t.matmul(ps_stB, ones_col[:], sq,
                         start=(i == 0), stop=(i == 7))
                t.matmul(ps_stA, ones_col[:], xt[i].ap()[:, sl],
                         start=(i == 0), stop=(i == 7))
            seq.c("tensor", fmm)

        def fst(v, ts=ts):
            v.tensor_copy(stat.ap()[0:1, 512 * ts:512 * (ts + 1)], ps_stA)
            v.tensor_copy(stat.ap()[0:1, T + 512 * ts:T + 512 * (ts + 1)], ps_stB)
        seq.c("vector", fst)

    def fmu(v):
        if with_mean:
            v.tensor_scalar(stat.ap()[0:1, 0:T], stat.ap()[0:1, 0:T],
                            1.0 / 1024.0, None, AL.mult)
        else:
            v.memset(stat.ap()[0:1, 0:T], 0.0)
        v.tensor_scalar(stat.ap()[0:1, T:2 * T], stat.ap()[0:1, T:2 * T],
                        1.0 / 1024.0, None, AL.mult)
        v.tensor_tensor(pf.ap()[0:1, 0:T], stat.ap()[0:1, 0:T],
                        stat.ap()[0:1, 0:T], AL.mult)
        v.tensor_tensor(stat.ap()[0:1, T:2 * T], stat.ap()[0:1, T:2 * T],
                        pf.ap()[0:1, 0:T], AL.subtract)
        v.tensor_scalar(stat.ap()[0:1, T:2 * T], stat.ap()[0:1, T:2 * T],
                        float(ctx.get("eps", 1e-5)), None, AL.add)
    seq.c("vector", fmu)

    def frs(s):
        s.activation(stat.ap()[0:1, 2 * T:3 * T], stat.ap()[0:1, T:2 * T], AF.Ln)
        s.activation(stat.ap()[0:1, 2 * T:3 * T], stat.ap()[0:1, 2 * T:3 * T], AF.Exp, scale=-0.5)
    seq.c("scalar", frs)

    def fnewton(v):
        v.tensor_tensor(pf.ap()[0:1, 0:T], stat.ap()[0:1, 2 * T:3 * T],
                        stat.ap()[0:1, 2 * T:3 * T], AL.mult)
        v.tensor_tensor(pf.ap()[0:1, 0:T], pf.ap()[0:1, 0:T],
                        stat.ap()[0:1, T:2 * T], AL.mult)
        v.tensor_scalar(pf.ap()[0:1, 0:T], pf.ap()[0:1, 0:T],
                        -0.5, 1.5, AL.mult, AL.add)
        v.tensor_tensor(stat.ap()[0:1, T:2 * T], stat.ap()[0:1, 2 * T:3 * T],
                        pf.ap()[0:1, 0:T], AL.mult)
    seq.c("vector", fnewton)

    ps_rep2 = ctx.get("ps_rep2")
    for i in range(8):
        for ts in range(nts):
            sl = slice(512 * ts, 512 * (ts + 1))
            if ps_rep2 is not None:
                def frep2(t, ts=ts):
                    if with_mean:
                        t.matmul(ps_rep[:], ones_row[:],
                                 stat.ap()[0:1, 512 * ts:512 * (ts + 1)],
                                 start=True, stop=True)
                    t.matmul(ps_rep2[:], ones_row[:],
                             stat.ap()[0:1, T + 512 * ts:T + 512 * (ts + 1)],
                             start=True, stop=True)
                seq.c("tensor", frep2)

                def fpb(v, i=i, sl=sl, ts=ts):
                    if with_mean:
                        v.tensor_tensor(pf.ap()[:, sl], xt[i].ap()[:, sl],
                                        ps_rep.ap()[:], AL.subtract)
                        v.tensor_tensor(pf.ap()[:, sl], pf.ap()[:, sl],
                                        ps_rep2.ap()[:], AL.mult)
                    else:
                        v.tensor_tensor(pf.ap()[:, sl], xt[i].ap()[:, sl],
                                        ps_rep2.ap()[:], AL.mult)
                    if ctx.get("wcol") is not None:
                        v.tensor_scalar(pf.ap()[:, sl], pf.ap()[:, sl], ctx["wcol"][i],
                                        None, AL.mult)
                    if ctx.get("out_f32"):
                        v.tensor_copy(xt[i].ap()[:, sl], pf.ap()[:, sl])
                    else:
                        v.tensor_copy(ph[i].ap()[:, sl], pf.ap()[:, sl])
                        v.tensor_tensor(pl[i].ap()[:, sl], pf.ap()[:, sl],
                                        ph[i].ap()[:, sl], AL.subtract)
                seq.c("vector", fpb)
                continue
            if with_mean:
                def frepmu(t, ts=ts):
                    t.matmul(ps_rep[:], ones_row[:],
                             stat.ap()[0:1, 512 * ts:512 * (ts + 1)],
                             start=True, stop=True)
                seq.c("tensor", frepmu)

                def fsub(v, i=i, sl=sl):
                    v.tensor_tensor(pf.ap()[:, sl], xt[i].ap()[:, sl],
                                    ps_rep.ap()[:], AL.subtract)
                seq.c("vector", fsub)

            def freprs(t, ts=ts):
                t.matmul(ps_rep[:], ones_row[:],
                         stat.ap()[0:1, T + 512 * ts:T + 512 * (ts + 1)],
                         start=True, stop=True)
            seq.c("tensor", freprs)

            def fmul(v, i=i, sl=sl):
                src0 = pf.ap()[:, sl] if with_mean else xt[i].ap()[:, sl]
                v.tensor_tensor(pf.ap()[:, sl], src0, ps_rep.ap()[:], AL.mult)
                if ctx.get("wcol") is not None:
                    v.tensor_scalar(pf.ap()[:, sl], pf.ap()[:, sl], ctx["wcol"][i],
                                    None, AL.mult)
            seq.c("vector", fmul)

        if ps_rep2 is None:
            def fcast(v, i=i):
                if ctx.get("out_f32"):
                    v.tensor_copy(xt[i].ap()[:, 0:T], pf.ap()[:, 0:T])
                else:
                    v.tensor_copy(ph[i].ap()[:, 0:T], pf.ap()[:, 0:T])
                    v.tensor_tensor(pl[i].ap()[:, 0:T], pf.ap()[:, 0:T], ph[i].ap()[:, 0:T],
                                    AL.subtract)
            seq.c("vector", fcast)



def build_ffn_phase(with_wo, max_items=None):
    T = 2048
    NT = 256
    nc = bass.Bass("TRN2", target_bir_lowering=False, detect_race_conditions=False)
    x_in = nc.dram_tensor("x", [D, T], F32, kind="ExternalInput").ap()
    if with_wo:
        ctx_in = nc.dram_tensor("ctx", [D, T], F32, kind="ExternalInput").ap()
        woT_in = nc.dram_tensor("woT", [D, D], F32, kind="ExternalInput").ap()
        bo_in = nc.dram_tensor("bo", [128, 8], F32, kind="ExternalInput").ap()
    w1T_in = nc.dram_tensor("w1T", [D, 4096], F32, kind="ExternalInput").ap()
    b1_in = nc.dram_tensor("b1", [128, 32], F32, kind="ExternalInput").ap()
    w2T_in = nc.dram_tensor("w2T", [4096, D], F32, kind="ExternalInput").ap()
    b2_in = nc.dram_tensor("b2", [128, 8], F32, kind="ExternalInput").ap()
    out_hbm = nc.dram_tensor("out", [D, T], F32, kind="ExternalOutput").ap()
    dbg_hbm = nc.dram_tensor("dbg", [4, 2048], F32, kind="ExternalOutput").ap()
    s1_scr = nc.dram_tensor("s1_scr", [D, 4096], BF16, kind="Internal").ap()
    s2_scr = nc.dram_tensor("s2_scr", [4096, D], BF16, kind="Internal").ap()

    xt = [nc.alloc_sbuf_tensor(f"xt{i}", [128, T], F32) for i in range(8)]
    bp = [nc.alloc_sbuf_tensor(f"bp{i}", [128, T], BF16) for i in range(24)]
    ph, pl = bp[0:8], bp[8:16]
    hh, hl = bp[16:20], bp[20:24]
    pf = nc.alloc_sbuf_tensor("pf", [128, T], F32)
    wstage = nc.alloc_sbuf_tensor("wstage", [128, 1024], F32)
    wbh = nc.alloc_sbuf_tensor("wbh", [128, 2048], BF16)
    wbl = nc.alloc_sbuf_tensor("wbl", [128, 2048], BF16)
    ones_col = nc.alloc_sbuf_tensor("ones_col", [128, 1], F32)
    ones_row = nc.alloc_sbuf_tensor("ones_row", [1, 128], F32)
    stat = nc.alloc_sbuf_tensor("stat", [1, 3 * T], F32)
    eps_t = nc.alloc_sbuf_tensor("eps_t", [1, 1], F32)
    alpha1 = nc.alloc_sbuf_tensor("alpha1", [128, 32], F32)
    b1c = nc.alloc_sbuf_tensor("b1c", [128, 32], F32)
    alpha2 = nc.alloc_sbuf_tensor("alpha2", [128, 8], F32)
    b2c = nc.alloc_sbuf_tensor("b2c", [128, 8], F32)
    boc = nc.alloc_sbuf_tensor("boc", [128, 8], F32) if with_wo else None
    hsc = nc.alloc_sbuf_tensor("hsc", [128, 512], F32)
    sq = hsc.ap()[:, 0:512]
    asum2 = nc.alloc_sbuf_tensor("asum2", [128, 32], F32)
    s1t = wbh.ap()[:, 0:1024]
    s2t = wbh.ap()[:, 1024:2048]

    psb = [nc.alloc_psum_tensor(f"psb{i}", [128, 512], F32) for i in range(8)]
    ps_mm = psb[0]
    ps_stA = psb[1].ap()[0:1, 0:512]
    ps_stB = psb[2].ap()[0:1, 0:512]
    ps_rep = psb[3]

    ctx = dict(xt=xt, ph=ph, pl=pl, sq=sq, pf=pf, stat=stat, ones_col=ones_col,
               ones_row=ones_row, eps_t=eps_t, eps=1e-5,
               ps_stA=ps_stA, ps_stB=ps_stB, ps_rep=ps_rep, ps_rep2=psb[4])

    seq = Seq(nc)

    for i in range(8):
        seq.dma("sync", lambda s, i=i: s.dma_start(
            out=xt[i][:], in_=x_in.tensor.ap()[128 * i:128 * (i + 1), :]))

    def fconsts(v):
        v.memset(ones_col[:], 1.0)
        v.memset(ones_row[:], 1.0)
        v.memset(eps_t[:], 1e-5)
    seq.c("vector", fconsts)

    if with_wo:
        seq.dma("sync", lambda s: s.dma_start(out=boc[:], in_=bo_in))
        cth, ctl = bp[0:8], bp[8:16]
        for i in range(8):
            seq.dma("sync", lambda s, i=i: s.dma_start(
                out=pf.ap()[:, 0:T], in_=ctx_in.tensor.ap()[128 * i:128 * (i + 1), :]))

            def fsplit(v, i=i):
                v.tensor_copy(cth[i].ap()[:, 0:T], pf.ap()[:, 0:T])
                v.tensor_tensor(ctl[i].ap()[:, 0:T], pf.ap()[:, 0:T],
                                cth[i].ap()[:, 0:T], AL.subtract)
            seq.c("vector", fsplit)
        for dt in range(8):
            for dch in range(8):
                seq.dma("sync", lambda s, dt=dt, dch=dch: s.dma_start(
                    out=wstage.ap()[:, 128 * dch:128 * (dch + 1)],
                    in_=woT_in.tensor.ap()[128 * dch:128 * (dch + 1),
                                           128 * dt:128 * (dt + 1)]))

            def fws(v, dt=dt):
                v.tensor_copy(wbh.ap()[:, 0:1024], wstage.ap()[:, 0:1024])
                v.tensor_tensor(wbl.ap()[:, 0:1024], wstage.ap()[:, 0:1024],
                                wbh.ap()[:, 0:1024], AL.subtract)
            seq.c("vector", fws)
            for tt in range(4):
                def fmm(t, dt=dt, tt=tt):
                    for dch in range(8):
                        wsh = wbh.ap()[:, 128 * dch:128 * (dch + 1)]
                        wsl = wbl.ap()[:, 128 * dch:128 * (dch + 1)]
                        rh = cth[dch].ap()[:, 512 * tt:512 * (tt + 1)]
                        rl = ctl[dch].ap()[:, 512 * tt:512 * (tt + 1)]
                        t.matmul(ps_mm[:], wsh, rh, start=(dch == 0), stop=False)
                        t.matmul(ps_mm[:], wsl, rh, start=False, stop=False)
                        t.matmul(ps_mm[:], wsh, rl, start=False, stop=(dch == 7))
                seq.c("tensor", fmm)

                def fev(v, dt=dt, tt=tt):
                    if FEV_MODE == "stt":
                        v.scalar_tensor_tensor(xt[dt].ap()[:, 512 * tt:512 * (tt + 1)],
                                               ps_mm.ap()[:], boc.ap()[:, dt:dt + 1],
                                               xt[dt].ap()[:, 512 * tt:512 * (tt + 1)],
                                               AL.add, AL.add)
                    elif FEV_MODE == "copy":
                        v.tensor_copy(pf.ap()[:, 0:512], ps_mm.ap()[:])
                    elif FEV_MODE == "sbufonly":
                        v.tensor_copy(pf.ap()[:, 0:512], xt[0].ap()[:, 0:512])
                    elif FEV_MODE == "scalarcopy":
                        pass
                    elif FEV_MODE == "tt":
                        v.tensor_tensor(xt[dt].ap()[:, 512 * tt:512 * (tt + 1)],
                                        ps_mm.ap()[:],
                                        xt[dt].ap()[:, 512 * tt:512 * (tt + 1)], AL.add)
                        v.tensor_scalar(xt[dt].ap()[:, 512 * tt:512 * (tt + 1)],
                                        xt[dt].ap()[:, 512 * tt:512 * (tt + 1)],
                                        boc.ap()[:, dt:dt + 1], None, AL.add)
                if FEV_MODE == "scalarcopy":
                    def fevs(s, dt=dt, tt=tt):
                        s.copy(pf.ap()[:, 0:512], ps_mm.ap()[:])
                    seq.c("scalar", fevs)
                else:
                    seq.c("vector", fev)

    emit_ln_prep(seq, nc, ctx, T, with_mean=True)

    seq.c("scalar", lambda s: load_act_set(s, nc, SET_EXPLN))

    def fz(v):
        v.memset(asum2[:], 0.0)
        v.memset(alpha2[:], 0.0)
    seq.c("vector", fz)
    for dch in range(8):
        for qtr in range(4):
            seq.dma("sync", lambda s, dch=dch, qtr=qtr: s.dma_start(
                out=wstage[:], in_=w1T_in.tensor.ap()[128 * dch:128 * (dch + 1),
                                                      1024 * qtr:1024 * (qtr + 1)]))

            def fsgn(s):
                s.activation(wbh.ap()[:, 0:1024], wstage[:], AF.Sign)
            seq.c("scalar", fsgn)
            seq.dma("sync", lambda s, dch=dch, qtr=qtr: s.dma_start(
                out=s1_scr.tensor.ap()[128 * dch:128 * (dch + 1),
                                       1024 * qtr:1024 * (qtr + 1)], in_=wbh.ap()[:, 0:1024]))

            def fabs(s):
                s.activation(wstage[:], wstage[:], AF.Abs)
            seq.c("scalar", fabs)

            def fcol(t, qtr=qtr):
                for j8 in range(8):
                    t.matmul(ps_rep.ap()[:, j8:j8 + 1],
                             wstage.ap()[:, 128 * j8:128 * (j8 + 1)], ones_col[:],
                             start=True, stop=True)
            seq.c("tensor", fcol)

            def fadd(v, qtr=qtr):
                v.tensor_tensor(asum2.ap()[:, 8 * qtr:8 * (qtr + 1)],
                                asum2.ap()[:, 8 * qtr:8 * (qtr + 1)],
                                ps_rep.ap()[:, 0:8], AL.add)
            seq.c("vector", fadd)

    def fsc(v):
        v.tensor_scalar(alpha1[:], asum2[:], 1.0 / 1024.0, None, AL.mult)
    seq.c("vector", fsc)
    seq.dma("sync", lambda s: s.dma_start(out=b1c[:], in_=b1_in))
    seq.dma("sync", lambda s: s.dma_start(out=b2c[:], in_=b2_in))

    for jch in range(32):
        seq.dma("sync", lambda s, jch=jch: s.dma_start(
            out=wstage.ap()[:, 0:1024],
            in_=w2T_in.tensor.ap()[128 * jch:128 * (jch + 1), :]))

        def fsgn2(s):
            s.activation(wbh.ap()[:, 0:1024], wstage.ap()[:, 0:1024], AF.Sign)
        seq.c("scalar", fsgn2)
        seq.dma("sync", lambda s, jch=jch: s.dma_start(
            out=s2_scr.tensor.ap()[128 * jch:128 * (jch + 1), :],
            in_=wbh.ap()[:, 0:1024]))

        def fabs2(s):
            s.activation(wstage.ap()[:, 0:1024], wstage.ap()[:, 0:1024], AF.Abs)
        seq.c("scalar", fabs2)

        def fcol2(t):
            for d8 in range(8):
                t.matmul(ps_rep.ap()[:, d8:d8 + 1],
                         wstage.ap()[:, 128 * d8:128 * (d8 + 1)], ones_col[:],
                         start=True, stop=True)
        seq.c("tensor", fcol2)

        def fadd2(v):
            v.tensor_tensor(alpha2[:], alpha2[:], ps_rep.ap()[:, 0:8], AL.add)
        seq.c("vector", fadd2)

    def fsc2(v):
        v.tensor_scalar(alpha2[:], alpha2[:], 1.0 / 4096.0, None, AL.mult)
    seq.c("vector", fsc2)

    seq.c("sync", lambda s: s.nop())
    seq.c("scalar", lambda s: load_act_set(s, nc, SET_GELU))
    bufs4 = [s1t, s2t, wbl.ap()[:, 0:1024], wbl.ap()[:, 1024:2048]]
    LAG1 = {"sync": 16}
    LAG2 = {"sync": 2}
    for tt in range(T // NT):
        tsl = slice(NT * tt, NT * (tt + 1))

        def femit_ld1(jp):
            pb = (bufs4[0], bufs4[1]) if jp % 2 == 0 else (bufs4[2], bufs4[3])

            def fld(s, jp=jp, pb=pb):
                out = []
                for q in range(2):
                    jt = 2 * jp + q
                    for dch in range(8):
                        out.append(s.dma_start(
                            out=pb[q][:, 128 * dch:128 * (dch + 1)],
                            in_=s1_scr.tensor.ap()[128 * dch:128 * (dch + 1),
                                                   128 * jt:128 * (jt + 1)]))
                return out
            seq.dma("sync", fld, n=16)

        femit_ld1(0)
        for jp in range(16):
            if jp + 1 < 16:
                femit_ld1(jp + 1)
            pb = (bufs4[0], bufs4[1]) if jp % 2 == 0 else (bufs4[2], bufs4[3])
            jts = (2 * jp, 2 * jp + 1)
            hscs = (hsc, pf)

            def fmm1(t, tsl=tsl, pb=pb):
                for q in range(2):
                    for dch in range(8):
                        w = pb[q][:, 128 * dch:128 * (dch + 1)]
                        t.matmul(psb[q].ap()[:, 0:NT], w, ph[dch].ap()[:, tsl],
                                 start=(dch == 0), stop=False)
                        t.matmul(psb[q].ap()[:, 0:NT], w, pl[dch].ap()[:, tsl],
                                 start=False, stop=(dch == 7))
            seq.c("tensor", fmm1, dma_lag=LAG1)

            def fev1(v, jts=jts, hscs=hscs):
                for q, jt in enumerate(jts):
                    v.tensor_scalar(hscs[q].ap()[:, 0:NT], psb[q].ap()[:, 0:NT],
                                    alpha1.ap()[:, jt:jt + 1], None, AL.mult)
            seq.c("vector", fev1, dma_lag=LAG1)

            def fge(s, jts=jts, hscs=hscs):
                for q, jt in enumerate(jts):
                    s.activation(hscs[q].ap()[:, 256:256 + NT], hscs[q].ap()[:, 0:NT],
                                 AF.Gelu, bias=b1c.ap()[:, jt:jt + 1])
            seq.c("scalar", fge, dma_lag=LAG1)

            def fsp(v, jts=jts, hscs=hscs):
                for q, jt in enumerate(jts):
                    hhd = hh[jt // 8].ap()[:, NT * (jt % 8):NT * (jt % 8 + 1)]
                    hld = hl[jt // 8].ap()[:, NT * (jt % 8):NT * (jt % 8 + 1)]
                    v.tensor_copy(hhd, hscs[q].ap()[:, 256:256 + NT])
                    v.tensor_tensor(hld, hscs[q].ap()[:, 256:256 + NT], hhd, AL.subtract)
            seq.c("vector", fsp, dma_lag=LAG1)

        def femit_ld2(jp):
            pb = (bufs4[0], bufs4[1]) if jp % 2 == 0 else (bufs4[2], bufs4[3])

            def fld2(s, jp=jp, pb=pb):
                return [s.dma_start(out=pb[q][:, 0:1024],
                                    in_=s2_scr.tensor.ap()[128 * (2 * jp + q):128 * (2 * jp + q + 1), :])
                        for q in range(2)]
            seq.dma("sync", fld2, n=2)

        femit_ld2(0)
        for jp in range(16):
            if jp + 1 < 16:
                femit_ld2(jp + 1)
            pb = (bufs4[0], bufs4[1]) if jp % 2 == 0 else (bufs4[2], bufs4[3])

            def fmm2(t, jp=jp, pb=pb):
                for q in range(2):
                    jch = 2 * jp + q
                    rh = hh[jch // 8].ap()[:, NT * (jch % 8):NT * (jch % 8 + 1)]
                    rl = hl[jch // 8].ap()[:, NT * (jch % 8):NT * (jch % 8 + 1)]
                    for dt in range(8):
                        w = pb[q][:, 128 * dt:128 * (dt + 1)]
                        t.matmul(psb[dt].ap()[:, 256:256 + NT], w, rh,
                                 start=(jch == 0), stop=False)
                        t.matmul(psb[dt].ap()[:, 256:256 + NT], w, rl,
                                 start=False, stop=(jch == 31))
            seq.c("tensor", fmm2, dma_lag=LAG2)

        def fev2(v, tsl=tsl):
            for dt in range(8):
                v.scalar_tensor_tensor(xt[dt].ap()[:, tsl], psb[dt].ap()[:, 256:256 + NT],
                                       alpha2.ap()[:, dt:dt + 1], xt[dt].ap()[:, tsl],
                                       AL.mult, AL.add)
                v.tensor_scalar(xt[dt].ap()[:, tsl], xt[dt].ap()[:, tsl],
                                b2c.ap()[:, dt:dt + 1], None, AL.add)
        seq.c("vector", fev2)

    dbgt = pf
    def ftap(v):
        v.tensor_copy(dbgt.ap()[0:1, 0:T], stat.ap()[0:1, T:2 * T])
    seq.c("vector", ftap)
    seq.dma("gpsimd", lambda g: g.dma_start(out=dbg_hbm.tensor.ap()[0:1, :], in_=dbgt.ap()[0:1, 0:T]))
    def ftap2(v):
        v.memset(dbgt.ap()[0:1, 0:2048], 0.0)
        v.tensor_copy(dbgt.ap()[0:1, 0:32], alpha1.ap()[0:1, 0:32])
        v.tensor_copy(dbgt.ap()[0:1, 32:40], alpha2.ap()[0:1, 0:8])
    seq.c("vector", ftap2)
    seq.dma("gpsimd", lambda g: g.dma_start(out=dbg_hbm.tensor.ap()[1:2, :], in_=dbgt.ap()[0:1, 0:2048]))
    def ftap3(v):
        v.tensor_copy(dbgt.ap()[0:1, 0:2048], ph[0].ap()[0:1, 0:T])
    seq.c("vector", ftap3)
    seq.dma("gpsimd", lambda g: g.dma_start(out=dbg_hbm.tensor.ap()[2:3, :], in_=dbgt.ap()[0:1, 0:T]))
    def ftap4(v):
        v.tensor_copy(dbgt.ap()[0:1, 0:2048], hh[0].ap()[0:1, 0:T])
    seq.c("vector", ftap4)
    seq.dma("gpsimd", lambda g: g.dma_start(out=dbg_hbm.tensor.ap()[3:4, :], in_=dbgt.ap()[0:1, 0:T]))

    for i in range(8):
        seq.dma("gpsimd", lambda g, i=i: g.dma_start(
            out=out_hbm.tensor.ap()[128 * i:128 * (i + 1), :], in_=xt[i][:]))

    if max_items is not None:
        seq.items = seq.items[:max_items]
    with (
        nc.semaphore("c_sem") as c_sem,
        nc.semaphore("d_sync") as d_sync,
        nc.semaphore("d_gp") as d_gp,
        nc.Block() as block,
    ):
        d_sems = {"sync": d_sync, "gpsimd": d_gp}
        n_items, n_dma = seq.build(block, c_sem, d_sems)

        @block.gpsimd
        def _(g):
            g.wait_ge(c_sem, n_items)
            for k, s in d_sems.items():
                g.wait_ge(s, 16 * n_dma[k])
    return nc



def _cols128(v):
    v = np.asarray(v, dtype=np.float32)
    return np.ascontiguousarray(v.reshape(-1, 128).T)


def hT(a):
    return np.ascontiguousarray(np.asarray(a, dtype=np.float32).T)


def run_ffn_phase(x_fm_per_core, ffn_w, wo=None):
    nc = build_ffn_phase(with_wo=wo is not None)
    w1T = hT(ffn_w["w1"])
    w2T = hT(ffn_w["w2"])
    b1 = _cols128(ffn_w["b1"])
    b2 = _cols128(ffn_w["b2"])
    in_maps = []
    for c in range(8):
        m = {"x": np.ascontiguousarray(x_fm_per_core[c]),
             "w1T": w1T, "b1": b1, "w2T": w2T, "b2": b2}
        if wo is not None:
            m["ctx"] = np.ascontiguousarray(wo["ctx_fm"][c])
            m["woT"] = hT(wo["woT_pre"]) if "woT_pre" in wo else np.ascontiguousarray(wo["woT"])
            m["bo"] = _cols128(wo["bo"])
        in_maps.append(m)
    res = run_bass_kernel_spmd(nc, in_maps, core_ids=list(range(8)))
    return [res.results[c]["out"] for c in range(8)], [res.results[c]["dbg"] for c in range(8)]


def kernel(**inputs):
    T = 4096
    x0 = np.asarray(inputs["L0_attn_a_x"], np.float32)

    nc1 = build_attn_a_phase(T)
    pp = np.arange(128)[:, None] % 32
    cc = np.arange(32)[None, :]
    maskS = np.where(pp < cc, 0.0, -1e30).astype(np.float32)
    maskI = np.where(pp <= cc, 0.0, -1e30).astype(np.float32)
    selq = np.zeros((128, 256), np.float32)
    for tile in range(4):
        selq[0:64, 64 * tile + 2 * tile] = 1.0
        selq[64:128, 64 * tile + 2 * tile + 1] = 1.0
    ident = np.eye(128, dtype=np.float32)
    Wq, Wk = inputs["L0_attn_a_Wq"], inputs["L0_attn_a_Wk"]
    Wv, Wg, Wb = inputs["L0_attn_a_Wv"], inputs["L0_attn_a_Wg"], inputs["L0_attn_a_Wb"]
    Wa = np.asarray(inputs["L0_attn_a_Wa"], np.float32)
    dtb_a = np.asarray(inputs["L0_attn_a_dt_bias"], np.float32)
    alog_a = np.asarray(inputs["L0_attn_a_A_log"], np.float32)
    nw_a = np.asarray(inputs["L0_attn_a_norm_w"], np.float32)
    in1 = []
    for c in range(8):
        seq_i, h0 = c // 2, 8 * (c % 2)
        sl = slice(h0 * 64, (h0 + 8) * 64)
        wap = np.zeros((1024, 64), np.float32)
        wap[:, 0:8] = hT(Wa[h0:h0 + 8])
        in1.append({
            "x": hT(x0[seq_i]),
            "wqT": hT(np.asarray(Wq)[sl]), "wkT": hT(np.asarray(Wk)[sl]),
            "wvT": hT(np.asarray(Wv)[sl]), "wgT": hT(np.asarray(Wg)[sl]),
            "wbT": hT(np.asarray(Wb)[sl]), "waT": wap, "selq": selq,
            "dtb": dtb_a[h0:h0 + 8, None].copy(), "alog": alog_a[h0:h0 + 8, None].copy(),
            "nwfm": np.ascontiguousarray(np.tile(nw_a, 2)[:, None]).astype(np.float32),
            "maskS": maskS, "maskI": maskI, "ident": ident,
        })
    r1 = run_bass_kernel_spmd(nc1, in1, core_ids=list(range(8)))
    ctx_flat = np.concatenate([
        np.concatenate([r1.results[2 * s]["ctx"], r1.results[2 * s + 1]["ctx"]], 0).T
        for s in range(4)], 0)

    xflat = x0.reshape(4 * T, 1024)
    NTOK = 4 * T // 8
    ffn_w = {k: inputs["L0_ffn_" + k] for k in ("w1", "b1", "w2", "b2")}
    xs = [hT(xflat[NTOK * c:NTOK * (c + 1)]) for c in range(8)]
    ctxs = [hT(ctx_flat[NTOK * c:NTOK * (c + 1)]) for c in range(8)]
    outs, _ = run_ffn_phase(xs, ffn_w, wo=dict(
        ctx_fm=ctxs, woT=hT(inputs["L0_attn_a_Wo"]), bo=inputs["L0_attn_a_bo"]))
    x3flat = np.concatenate([o.T for o in outs], 0)

    in_w = np.asarray(inputs["L0_attn_b_in_proj_w"], np.float32)
    conv_w = np.asarray(inputs["L0_attn_b_conv_w"], np.float32)
    conv_b = np.asarray(inputs["L0_attn_b_conv_b"], np.float32)
    dtb_b = np.asarray(inputs["L0_attn_b_dt_bias"], np.float32)
    alog_b = np.asarray(inputs["L0_attn_b_A_log"], np.float32)
    Dsk = np.asarray(inputs["L0_attn_b_D"], np.float32)
    gnw = np.asarray(inputs["L0_attn_b_gnorm_w"], np.float32)
    opw = np.asarray(inputs["L0_attn_b_out_proj_w"], np.float32)
    nw_b = np.asarray(inputs["L0_attn_b_norm_w"], np.float32)
    pp64 = np.arange(128)[:, None] % 64
    cc64 = np.arange(64)[None, :]
    mask64 = np.where(pp64 <= cc64, 0.0, -1e30).astype(np.float32)

    def run_mamba_layer(xflat_in, li, pre_ln):
        nca = build_mamba_a_phase(T, pre_ln=pre_ln)
        xseq = [hT(xflat_in[T * s:T * (s + 1)]) for s in range(4)]
        ina = []
        for c in range(8):
            seq_i = c // 2
            m = {"x": xseq[seq_i], "nw1": _cols128(nw_b[li]),
                 "mask64": mask64, "ident": ident}
            for sfx, lh in (("", 0), ("_b", 1)):
                h0 = 16 * lh + 8 * (c % 2)
                rows = np.concatenate([
                    np.arange(h0 * 64, (h0 + 8) * 64),
                    2048 + np.arange(h0 * 64, (h0 + 8) * 64),
                    np.arange(4096, 4160), np.arange(4160, 4224),
                    4224 + h0 + np.arange(8)])
                inwT = np.zeros((1024, 1216), np.float32)
                inwT[:, 0:1160] = in_w[li][rows].T
                crow = np.concatenate([np.arange(h0 * 64, (h0 + 8) * 64),
                                       np.arange(2048, 2112), np.arange(2112, 2176)])
                cw = np.zeros((128, 20), np.float32)
                cb = np.zeros((128, 5), np.float32)
                for i in range(5):
                    cw[:, 4 * i:4 * i + 4] = conv_w[li][crow[128 * i:128 * (i + 1)]]
                    cb[:, i] = conv_b[li][crow[128 * i:128 * (i + 1)]]
                dc = np.zeros((128, 4), np.float32)
                for i in range(4):
                    dc[0:64, i] = Dsk[li][h0 + 2 * i]
                    dc[64:128, i] = Dsk[li][h0 + 2 * i + 1]
                m["inwT" + sfx] = inwT
                m["convw" + sfx] = cw
                m["convb" + sfx] = cb
                m["dtb" + sfx] = dtb_b[li][h0:h0 + 8, None].copy()
                m["alog" + sfx] = alog_b[li][h0:h0 + 8, None].copy()
                m["dcolf" + sfx] = dc
            ina.append(m)
        ra = run_bass_kernel_spmd(nca, ina, core_ids=list(range(8)))
        y_rows = [np.zeros((2048, T), np.float32) for _ in range(4)]
        z_rows = [np.zeros((2048, T), np.float32) for _ in range(4)]
        ln_seq = [None] * 4
        for c in range(8):
            seq_i = c // 2
            for sfx, lh in (("", 0), ("_b", 1)):
                h0 = 16 * lh + 8 * (c % 2)
                y_rows[seq_i][h0 * 64:(h0 + 8) * 64] = ra.results[c]["y" + sfx]
                z_rows[seq_i][h0 * 64:(h0 + 8) * 64] = ra.results[c]["z" + sfx]
            if pre_ln and c % 2 == 0:
                ln_seq[seq_i] = ra.results[c]["ln0"]
        yflat = np.concatenate([y.T for y in y_rows], 0)
        zflat = np.concatenate([z.T for z in z_rows], 0)
        lnflat = (np.concatenate([l.T for l in ln_seq], 0) if pre_ln else None)
        return yflat, zflat, lnflat

    y0, z0, ln0flat = run_mamba_layer(x3flat, 0, pre_ln=True)
    ncb0 = build_mamba_out_phase(NTOK, add_outer=False)
    inb = []
    for c in range(8):
        sl = slice(NTOK * c, NTOK * (c + 1))
        inb.append({"x": hT(ln0flat[sl]), "yf": hT(y0[sl]), "zf": hT(z0[sl]),
                    "gwc": _cols128(gnw[0]), "owT": hT(opw[0])})
    rb0 = run_bass_kernel_spmd(ncb0, inb, core_ids=list(range(8)))
    r1flat = np.concatenate([rb0.results[c]["out"].T for c in range(8)], 0)

    y1, z1, _ = run_mamba_layer(r1flat, 1, pre_ln=False)
    ncb1 = build_mamba_out_phase(NTOK, add_outer=True)
    inb = []
    for c in range(8):
        sl = slice(NTOK * c, NTOK * (c + 1))
        inb.append({"x": hT(r1flat[sl]), "yf": hT(y1[sl]), "zf": hT(z1[sl]),
                    "gwc": _cols128(gnw[1]), "owT": hT(opw[1]),
                    "xo": hT(x3flat[sl])})
    rb1 = run_bass_kernel_spmd(ncb1, inb, core_ids=list(range(8)))
    x4flat = np.concatenate([rb1.results[c]["out"].T for c in range(8)], 0)

    xs = [hT(x4flat[NTOK * c:NTOK * (c + 1)]) for c in range(8)]
    outs, _ = run_ffn_phase(xs, ffn_w, wo=None)
    x5flat = np.concatenate([o.T for o in outs], 0)
    return np.ascontiguousarray(x5flat.reshape(4, T, 1024))



def mk_ap(t, offset, dims):
    return bass.AP(tensor=t, offset=offset, ap=[list(d) for d in dims])


LAST_MARKS = {}


def build_attn_a_phase(T_SEQ=4096, debug=False, trunc=None):
    SPAN = 512
    NSP = T_SEQ // SPAN
    NW = SPAN // 32
    nc = bass.Bass("TRN2", target_bir_lowering=False, detect_race_conditions=False)
    x_in = nc.dram_tensor("x", [D, T_SEQ], F32, kind="ExternalInput").ap()
    w_ins = {}
    for nm, ncol in [("wqT", 512), ("wkT", 512), ("wvT", 512), ("wgT", 512), ("wbT", 512), ("waT", 64)]:
        w_ins[nm] = nc.dram_tensor(nm, [D, ncol], F32, kind="ExternalInput").ap()
    dtb_in = nc.dram_tensor("dtb", [8, 1], F32, kind="ExternalInput").ap()
    alog_in = nc.dram_tensor("alog", [8, 1], F32, kind="ExternalInput").ap()
    nwfm_in = nc.dram_tensor("nwfm", [128, 1], F32, kind="ExternalInput").ap()
    maskS_in = nc.dram_tensor("maskS", [128, 32], F32, kind="ExternalInput").ap()
    maskI_in = nc.dram_tensor("maskI", [128, 32], F32, kind="ExternalInput").ap()
    ident_in = nc.dram_tensor("ident", [128, 128], F32, kind="ExternalInput").ap()
    selq_in = nc.dram_tensor("selq", [128, 256], F32, kind="ExternalInput").ap()
    ctx_hbm = nc.dram_tensor("ctx", [512, T_SEQ], F32, kind="ExternalOutput").ap()
    scr = nc.dram_tensor("scr_fold", [8, 32768], F32, kind="Internal")
    dbg = nc.dram_tensor("dbg", [128, 4096], F32, kind="ExternalOutput").ap() if debug else None

    A = nc.alloc_sbuf_tensor
    s_st = [A(f"s_st{i}", [128, 64], F32) for i in range(4)]
    u_t = [A(f"u_t{g}", [128, 64], F32) for g in range(2)]
    tmpc = [A(f"tmpc{g}", [128, 64], F32) for g in range(2)]
    wbd = [A(f"wbd{g}", [128, 128], F32) for g in range(2)]
    mbd = [A(f"mbd{g}", [128, 128], F32) for g in range(2)]
    ones_col = A("ones_col", [128, 1], F32)
    ones_row = A("ones_row", [1, 128], F32)
    nwfm = A("nwfm_s", [128, 1], F32)
    maskS = A("maskS_s", [128, 32], F32)
    maskI = A("maskI_s", [128, 32], F32)
    identt = A("identt", [128, 128], F32)
    sel = A("sel", [128, 256], F32)
    dtb = A("dtb_s", [8, 1], F32)
    nega = A("nega", [8, 1], F32)
    WH, WL = {}, {}
    for nm, ncol in [("wqT", 512), ("wkT", 512), ("wvT", 512), ("wgT", 512), ("wbT", 512), ("waT", 64)]:
        WH[nm] = A("h" + nm, [128, 8 * ncol], BF16)
        WL[nm] = A("l" + nm, [128, 8 * ncol], BF16)
    xt = [A(f"xt{i}", [128, SPAN], F32) for i in range(8)]
    wstage = xt[0]
    ph = [A(f"ph{i}", [128, SPAN], BF16) for i in range(8)]
    pl = [A(f"pl{i}", [128, SPAN], BF16) for i in range(8)]
    pf = A("pf", [128, SPAN], F32)
    sqs = A("sqs", [128, 512], F32)
    stat = A("stat", [1, 3 * SPAN], F32)
    qf = [A(f"qf{i}", [128, SPAN], F32) for i in range(4)]
    kf = [A(f"kf{i}", [128, SPAN], F32) for i in range(4)]
    vstage = xt[0:4]
    vch = [A(f"vch{g}", [128, NW * 64], F32) for g in range(2)]
    bch = [A(f"bch{g}", [128, NW * 64], F32) for g in range(2)]
    ktm = [A(f"ktm{g}", [128, NW * 64], F32) for g in range(2)]
    tmstg = sqs
    ych = [A(f"ych{g}", [128, NW * 64], F32) for g in range(2)]
    vrS = [A(f"vrS{g}", [128, 64], F32) for g in range(2)]
    t1S = [A(f"t1S{g}", [128, SPAN], F32) for g in range(2)]
    t1I = [A(f"t1I{g}", [128, SPAN], F32) for g in range(2)]
    colstk = [A(f"colstk{g}", [128, NW * 8], F32) for g in range(2)]
    dccol = [A(f"dccol{i}", [128, NW], F32) for i in range(4)]
    rstdw = [A(f"rstdw{g}", [128, NW], F32) for g in range(2)]
    caf = A("caf", [8, SPAN], F32)
    spf = A("spf", [8, SPAN], F32)
    spf2 = A("spf2", [8, SPAN], F32)
    knf = A("knf", [8, SPAN], F32)
    qnf = A("qnf", [8, SPAN], F32)
    cakf = A("cakf", [8, SPAN], F32)
    decf = A("decf", [8, SPAN], F32)
    d2f = A("d2f", [8, SPAN], F32)
    auxf = A("auxf", [8, SPAN], F32)
    dcf = A("dcf", [8, NW], F32)
    cprev = A("cprev", [8, NW], F32)
    dsS = [A(f"dsS{g}", [128, 32], F32) for g in range(2)]
    dsI = [A(f"dsI{g}", [128, 32], F32) for g in range(2)]
    esS = [A(f"esS{g}", [128, 32], F32) for g in range(2)]
    esI = [A(f"esI{g}", [128, 32], F32) for g in range(2)]
    ctxf = xt[4:8]

    psb = [nc.alloc_psum_tensor(f"psA{i}", [128, 512], F32) for i in range(8)]
    ps_chain = [psb[0], psb[1]]
    ps_G = [psb[2], psb[3]]
    ps_R = [psb[4], psb[5]]
    ps_Y = [psb[6], psb[7]]

    seq = Seq(nc)

    def fconst(v):
        v.memset(ones_col[:], 1.0)
        v.memset(ones_row[:], 1.0)
        v.memset(wbd[0][:], 0.0)
        v.memset(wbd[1][:], 0.0)
        v.memset(mbd[0][:], 0.0)
        v.memset(mbd[1][:], 0.0)
        v.memset(u_t[0][:], 0.0)
        v.memset(u_t[1][:], 0.0)
        for i in range(4):
            v.memset(s_st[i][:], 0.0)
    seq.c("vector", fconst)
    seq.dma("sync", lambda s: s.dma_start(out=nwfm[:], in_=nwfm_in))
    seq.dma("sync", lambda s: s.dma_start(out=maskS[:], in_=maskS_in))
    seq.dma("sync", lambda s: s.dma_start(out=maskI[:], in_=maskI_in))
    seq.dma("sync", lambda s: s.dma_start(out=identt[:], in_=ident_in))
    seq.dma("sync", lambda s: s.dma_start(out=sel[:], in_=selq_in))
    seq.dma("sync", lambda s: s.dma_start(out=dtb[:], in_=dtb_in))
    seq.dma("sync", lambda s: s.dma_start(out=nega[:], in_=alog_in))

    def fnega(s):
        load_act_set(s, nc, SET_EXPLN)
        s.activation(nega[:], nega[:], AF.Exp)
    seq.c("scalar", fnega)
    seq.c("vector", lambda v: v.tensor_scalar(nega[:], nega[:], -1.0, None, AL.mult))

    for nm, ncol in [("wqT", 512), ("wkT", 512), ("wvT", 512), ("wgT", 512), ("wbT", 512), ("waT", 64)]:
        for dch in range(8):
            seq.dma("sync", lambda s, nm=nm, dch=dch, ncol=ncol: s.dma_start(
                out=wstage.ap()[:, 0:ncol],
                in_=w_ins[nm].tensor.ap()[128 * dch:128 * (dch + 1), :]))

            def fsp_(v, nm=nm, dch=dch, ncol=ncol):
                v.tensor_copy(WH[nm].ap()[:, ncol * dch:ncol * (dch + 1)], wstage.ap()[:, 0:ncol])
                v.tensor_tensor(WL[nm].ap()[:, ncol * dch:ncol * (dch + 1)], wstage.ap()[:, 0:ncol],
                                WH[nm].ap()[:, ncol * dch:ncol * (dch + 1)], AL.subtract)
            seq.c("vector", fsp_)

    ch = {"pe": 0, "dv": 0}
    ch_pe = nc.semaphore("ch_pe").__enter__()
    ch_dv = nc.semaphore("ch_dv").__enter__()

    for sp in range(NSP):
        toff = SPAN * sp

        def fxl(s, toff=toff):
            return [s.dma_start(out=xt[i][:],
                                in_=x_in.tensor.ap()[128 * i:128 * (i + 1), toff:toff + SPAN])
                    for i in range(8)]
        seq.dma("sync", fxl, n=8)

        ctxd = dict(xt=xt, ph=ph, pl=pl, sq=sqs.ap()[:, 0:512], pf=pf, stat=stat,
                    ones_col=ones_col, ones_row=ones_row, eps=1e-5,
                    ps_stA=psb[2].ap()[0:1, 0:512], ps_stB=psb[3].ap()[0:1, 0:512],
                    ps_rep=psb[4], ps_rep2=psb[5])
        emit_ln_prep(seq, nc, ctxd, SPAN, with_mean=True)
        if debug and sp == 0:
            seq.c("vector", lambda v: v.tensor_copy(pf[:], ph[0][:]))
            seq.dma("gpsimd", lambda sy: sy.dma_start(out=dbg.tensor.ap()[:, 2960:3472], in_=pf[:]))
            seq.dma("gpsimd", lambda sy: sy.dma_start(out=dbg.tensor.ap()[0:1, 2448:2960],
                                                      in_=stat.ap()[0:1, 512:1024]))
            LAST_MARKS["ln_dump"] = len(seq.items)

        def proj_fm(nm, outs):
            for ot in range(4):
                def fmm(t, nm=nm, ot=ot):
                    for dch in range(8):
                        wsh = WH[nm].ap()[:, 512 * dch + 128 * ot:512 * dch + 128 * (ot + 1)]
                        wsl = WL[nm].ap()[:, 512 * dch + 128 * ot:512 * dch + 128 * (ot + 1)]
                        t.matmul(psb[2][:], wsh, ph[dch][:], start=(dch == 0), stop=False)
                        t.matmul(psb[2][:], wsl, ph[dch][:], start=False, stop=False)
                        t.matmul(psb[2][:], wsh, pl[dch][:], start=False, stop=(dch == 7))
                seq.c("tensor", fmm)

                def fev(v, outs=outs, ot=ot):
                    v.tensor_copy(outs[ot][:], psb[2].ap()[:])
                seq.c("vector", fev)

        def tm_fold(src_tiles, fold_fn, e_cnt=64):
            LAGF = {"gpsimd": 5}
            for g in range(2):
                for blk in range(4):
                    slot = 4 * g + blk
                    stg = sqs.ap()[:, 0:256] if slot % 2 == 0 else sqs.ap()[:, 256:512]
                    lag0 = None if slot == 0 else LAGF

                    def ftr_(t, g=g, blk=blk):
                        for half in range(2):
                            t.transpose(psb[5].ap()[:, 128 * half:128 * (half + 1)],
                                        src_tiles[2 * g + half].ap()[:, 128 * blk:128 * (blk + 1)],
                                        identt[:])
                    seq.c("tensor", ftr_, dma_lag=lag0)

                    def fte_(v, stg=stg):
                        v.tensor_copy(stg, psb[5].ap()[:, 0:256])
                    seq.c("vector", fte_, dma_lag=lag0)

                    def fst_(sy, slot=slot, stg=stg):
                        return sy.dma_start(out=scr.ap()[slot:slot + 1, :], in_=stg)
                    seq.dma("gpsimd", fst_)

                    def ff_(sy, g=g, blk=blk, slot=slot, e_cnt=e_cnt):
                        out = []
                        for hh in range(4):
                            sap = mk_ap(scr, slot * 32768 + 64 * hh,
                                        [[256, 32], [32 * 256, 4], [1, e_cnt]])
                            out.append(sy.dma_start(out=fold_fn(g, blk, hh), in_=sap))
                        return out
                    seq.dma("gpsimd", ff_, n=4)

        def chain_dims(dst, blk, hh):
            return mk_ap(dst, 4 * blk * 64 + hh * 32 * NW * 64,
                         [[NW * 64, 32], [64, 4], [1, 64]])

        LAST_MARKS[f"pre_qk_{sp}"] = len(seq.items)
        proj_fm("wqT", qf)
        if debug and sp == 0:
            seq.dma("gpsimd", lambda sy: sy.dma_start(out=dbg.tensor.ap()[:, 3472:3984], in_=qf[0][:]))
            LAST_MARKS["q_dump"] = len(seq.items)
        proj_fm("wkT", kf)
        if debug and sp == 0:
            seq.dma("gpsimd", lambda sy: sy.dma_start(out=dbg.tensor.ap()[:, 1664:2176], in_=kf[0][:]))
            LAST_MARKS["k_dump"] = len(seq.items)
        LAST_MARKS[f"post_qk_{sp}"] = len(seq.items)

        def fmma(t):
            for dch in range(8):
                t.matmul(psb[5].ap()[0:64, 0:512], WH["waT"].ap()[:, 64 * dch:64 * (dch + 1)],
                         ph[dch][:], start=(dch == 0), stop=False)
                t.matmul(psb[5].ap()[0:64, 0:512], WL["waT"].ap()[:, 64 * dch:64 * (dch + 1)],
                         ph[dch][:], start=False, stop=False)
                t.matmul(psb[5].ap()[0:64, 0:512], WH["waT"].ap()[:, 64 * dch:64 * (dch + 1)],
                         pl[dch][:], start=False, stop=(dch == 7))
        seq.c("tensor", fmma)
        seq.c("vector", lambda v: v.tensor_copy(spf[:], psb[5].ap()[0:8, 0:512]))
        LAST_MARKS[f"a_copy_{sp}"] = len(seq.items)
        if debug and sp == 0:
            seq.dma("gpsimd", lambda sy: sy.dma_start(out=dbg.tensor.ap()[8:16, 0:512], in_=spf[:]))

        def fsp1(v):
            v.tensor_scalar(spf[:], spf[:], dtb[:], None, AL.add)
            v.tensor_scalar_max(spf2[:], spf[:], 0.0)
        seq.c("vector", fsp1)
        if debug and sp == 0:
            seq.dma("gpsimd", lambda sy: sy.dma_start(out=dbg.tensor.ap()[16:24, 0:512], in_=spf[:]))
            seq.dma("gpsimd", lambda sy: sy.dma_start(out=dbg.tensor.ap()[24:32, 0:512], in_=spf2[:]))

        def fsp2(s):
            load_act_set(s, nc, SET_EXPLN)
            s.activation(caf[:], spf[:], AF.Abs)
            s.activation(caf[:], caf[:], AF.Exp, scale=-1.0)
            s.activation(caf[:], caf[:], AF.Ln, bias=1.0)
        seq.c("scalar", fsp2)
        if debug and sp == 0:
            seq.dma("gpsimd", lambda sy: sy.dma_start(out=dbg.tensor.ap()[32:40, 0:512], in_=caf[:]))

        def fsp3(v):
            v.tensor_tensor(caf[:], caf[:], spf2[:], AL.add)
            v.tensor_scalar(caf[:], caf[:], nega[:], None, AL.mult)
            v.tensor_scalar(spf[:], spf[:], 0.0, 1.0, AL.mult, AL.add)
            v.tensor_tensor_scan(spf2[:], spf[:], caf[:], 0.0, AL.mult, AL.add)
            v.tensor_copy(caf[:], spf2[:])
            v.memset(cprev.ap()[0:8, 0:1], 0.0)
            v.tensor_copy(cprev.ap()[0:8, 1:NW], mk_ap(caf, 31, [[SPAN, 8], [32, NW - 1]]))
        seq.c("vector", fsp3)

        for nm, chd, do_sig in [("wvT", vch, False), ("wbT", bch, True)]:
            proj_fm(nm, vstage)
            if do_sig:
                def fsig(s):
                    load_act_set(s, nc, SET_SIGMOID)
                    for ot in range(4):
                        s.activation(vstage[ot][:], vstage[ot][:], AF.Sigmoid)
                seq.c("scalar", fsig)
            tm_fold(vstage, lambda g, blk, hh, chd=chd: chain_dims(chd[g], blk, hh))

        for src_t, fmdst in [(kf, knf), (qf, qnf)]:
            for tile in range(4):
                def fsq(s, src_t=src_t, tile=tile):
                    s.activation(pf[:], src_t[tile][:], AF.Square)
                seq.c("scalar", fsq)

                def fnm(t, tile=tile):
                    t.matmul(psb[2].ap()[0:64, 0:512], sel.ap()[:, 64 * tile:64 * (tile + 1)],
                             pf[:], start=(tile == 0), stop=(tile == 3))
                seq.c("tensor", fnm)

            def fcp(v, fmdst=fmdst):
                v.tensor_copy(fmdst[:], psb[2].ap()[0:8, 0:512])
            seq.c("vector", fcp)

            def fns(s, fmdst=fmdst):
                load_act_set(s, nc, SET_EXPLN)
                s.activation(spf2[:], fmdst[:], AF.Ln)
                s.activation(spf2[:], spf2[:], AF.Exp, scale=-0.5)
            seq.c("scalar", fns)

            def fnw(v, fmdst=fmdst):
                v.tensor_tensor(pf.ap()[0:8, 0:512], spf2[:], spf2[:], AL.mult)
                v.tensor_tensor(pf.ap()[0:8, 0:512], pf.ap()[0:8, 0:512], fmdst[:], AL.mult)
                v.tensor_scalar(pf.ap()[0:8, 0:512], pf.ap()[0:8, 0:512], -0.5, 1.5,
                                AL.mult, AL.add)
                v.tensor_tensor(fmdst[:], spf2[:], pf.ap()[0:8, 0:512], AL.mult)
            seq.c("vector", fnw)

        tm_fold(kf, lambda g, blk, hh: chain_dims(ktm[g], blk, hh))

        def fcak(s):
            s.activation(cakf[:], knf[:], AF.Ln)
            s.activation(auxf[:], qnf[:], AF.Ln)
        seq.c("scalar", fcak)

        def fcak2(v):
            v.tensor_tensor(cakf[:], cakf[:], caf[:], AL.add)
            v.tensor_tensor(auxf[:], auxf[:], caf[:], AL.add)
            v.tensor_scalar(auxf[:], auxf[:], -2.0794415416798357, None, AL.add)
        seq.c("vector", fcak2)

        def fdec(v):
            v.tensor_tensor(decf[:], caf[:], mk_ap(cprev, 0, [[NW, 8], [1, NW], [0, 32]]),
                            AL.subtract)
            v.tensor_tensor(d2f[:], caf[:], mk_ap(caf, 31, [[SPAN, 8], [32, NW], [0, 32]]),
                            AL.subtract)
        seq.c("vector", fdec)

        def fdece(s):
            s.activation(decf[:], decf[:], AF.Exp)
            s.activation(d2f[:], d2f[:], AF.Exp, scale=-1.0)
        seq.c("scalar", fdece)

        def fdc(v):
            v.tensor_copy(dcf[:], mk_ap(decf, 31, [[SPAN, 8], [32, NW]]))
            v.tensor_tensor(spf[:], decf[:], knf[:], AL.mult)
            v.tensor_scalar(spf[:], spf[:], -1.0, None, AL.mult)
            v.tensor_tensor(spf2[:], d2f[:], knf[:], AL.mult)
            v.tensor_tensor(decf[:], decf[:], qnf[:], AL.mult)
            v.tensor_scalar(decf[:], decf[:], 0.125, None, AL.mult)
            v.tensor_scalar(d2f[:], knf[:], -1.0, None, AL.mult)
        seq.c("vector", fdc)

        for i in range(4):
            for q, src_fm in enumerate([caf, knf, spf, spf2, decf, d2f]):
                for h in range(2):
                    def frc(sy, i=i, q=q, h=h, src_fm=src_fm):
                        sap = mk_ap(src_fm, (2 * i + h) * SPAN, [[SPAN, 1], [1, SPAN]])
                        dap = mk_ap(vstage[i], (64 * h + q) * SPAN, [[SPAN, 1], [1, SPAN]])
                        return sy.dma_start(out=dap, in_=sap)
                    seq.dma("gpsimd", frc)
        tm_fold(vstage, lambda g, blk, hh: mk_ap(colstk[g], 4 * blk * 8 + hh * 32 * NW * 8,
                                                 [[NW * 8, 32], [8, 4], [1, 8]]),
                e_cnt=8)

        def fstt(sy):
            return [sy.dma_start(out=scr.ap()[0:1, 0:8 * SPAN], in_=cakf[:]),
                    sy.dma_start(out=scr.ap()[1:2, 0:8 * SPAN], in_=auxf[:])]
        seq.dma("gpsimd", fstt, n=2)

        def ft12(sy):
            out = []
            for g in range(2):
                for hp in range(4):
                    sap = mk_ap(scr, (4 * g + hp) * SPAN, [[0, 32], [1, SPAN]])
                    dap = mk_ap(t1S[g], 32 * hp * 512, [[512, 32], [1, SPAN]])
                    out.append(sy.dma_start(out=dap, in_=sap))
                    sap = mk_ap(scr, 32768 + (4 * g + hp) * SPAN, [[0, 32], [1, SPAN]])
                    dap = mk_ap(t1I[g], 32 * hp * 512, [[512, 32], [1, SPAN]])
                    out.append(sy.dma_start(out=dap, in_=sap))
            return out
        seq.dma("gpsimd", ft12, n=16)

        seq.dma("gpsimd", lambda sy: sy.dma_start(out=scr.ap()[2:3, 0:8 * NW], in_=dcf[:]))

        def fdcc(sy):
            out = []
            for pr in range(4):
                for h in range(2):
                    sap = mk_ap(scr, 2 * 32768 + (2 * pr + h) * NW, [[0, 64], [1, NW]])
                    dap = mk_ap(dccol[pr], 64 * h * NW, [[NW, 64], [1, NW]])
                    out.append(sy.dma_start(out=dap, in_=sap))
            return out
        seq.dma("gpsimd", fdcc, n=8)

        for w in range(NW):
            def fGall(t, w=w):
                for g in range(2):
                    for hp in range(4):
                        ktile = kf[2 * g + hp // 2]
                        qtile = qf[2 * g + hp // 2]
                        base = (hp % 2) * 64
                        t.matmul(psb[2 + 2 * g].ap()[32 * hp:32 * (hp + 1), 0:32],
                                 ktile.ap()[base:base + 64, 32 * w:32 * (w + 1)],
                                 ktile.ap()[base:base + 64, 32 * w:32 * (w + 1)],
                                 start=True, stop=True, tile_position=(base, 32 * hp))
                        t.matmul(psb[3 + 2 * g].ap()[32 * hp:32 * (hp + 1), 0:32],
                                 ktile.ap()[base:base + 64, 32 * w:32 * (w + 1)],
                                 qtile.ap()[base:base + 64, 32 * w:32 * (w + 1)],
                                 start=True, stop=True, tile_position=(base, 32 * hp))
            seq.c("tensor", fGall)

            def fDall(v, w=w):
                for g in range(2):
                    v.scalar_tensor_tensor(dsS[g][:], t1S[g].ap()[:, 32 * w:32 * (w + 1)],
                                           colstk[g].ap()[:, 8 * w:8 * w + 1], maskS[:],
                                           AL.subtract, AL.add)
                    v.scalar_tensor_tensor(dsI[g][:], t1I[g].ap()[:, 32 * w:32 * (w + 1)],
                                           colstk[g].ap()[:, 8 * w:8 * w + 1], maskI[:],
                                           AL.subtract, AL.add)
            seq.c("vector", fDall)

            def fEall(s, w=w):
                for g in range(2):
                    s.activation(esS[g][:], dsS[g][:], AF.Exp)
                    s.activation(esI[g][:], dsI[g][:], AF.Exp)
            seq.c("scalar", fEall)

            def fWMev(v, w=w):
                for g in range(2):
                    for hp in range(4):
                        v.scalar_tensor_tensor(
                            wbd[g].ap()[32 * hp:32 * (hp + 1), 32 * hp:32 * (hp + 1)],
                            psb[2 + 2 * g].ap()[32 * hp:32 * (hp + 1), 0:32],
                            colstk[g].ap()[32 * hp:32 * (hp + 1), 8 * w + 5:8 * w + 6],
                            esS[g].ap()[32 * hp:32 * (hp + 1), 0:32], AL.mult, AL.mult)
                        v.scalar_tensor_tensor(
                            mbd[g].ap()[32 * hp:32 * (hp + 1), 32 * hp:32 * (hp + 1)],
                            psb[3 + 2 * g].ap()[32 * hp:32 * (hp + 1), 0:32],
                            colstk[g].ap()[32 * hp:32 * (hp + 1), 8 * w + 1:8 * w + 2],
                            esI[g].ap()[32 * hp:32 * (hp + 1), 0:32], AL.mult, AL.mult)
            seq.c("vector", fWMev)

            def fRQS(t, w=w):
                for g in range(2):
                    for hp in range(4):
                        ktile = kf[2 * g + hp // 2]
                        qtile = qf[2 * g + hp // 2]
                        base = (hp % 2) * 64
                        st = s_st[2 * g + hp // 2]
                        t.matmul(psb[2 + 2 * g].ap()[32 * hp:32 * (hp + 1), 0:64],
                                 ktile.ap()[base:base + 64, 32 * w:32 * (w + 1)],
                                 st.ap()[base:base + 64, 0:64],
                                 start=True, stop=True, tile_position=(base, 32 * hp))
                        t.matmul(psb[3 + 2 * g].ap()[32 * hp:32 * (hp + 1), 0:64],
                                 qtile.ap()[base:base + 64, 32 * w:32 * (w + 1)],
                                 st.ap()[base:base + 64, 0:64],
                                 start=True, stop=True, tile_position=(base, 32 * hp))
            seq.c("tensor", fRQS)

            def fvryt(v, w=w):
                for g in range(2):
                    v.scalar_tensor_tensor(vrS[g].ap()[:, 0:64],
                                           psb[2 + 2 * g].ap()[:, 0:64],
                                           colstk[g].ap()[:, 8 * w + 2:8 * w + 3],
                                           vch[g].ap()[:, 64 * w:64 * (w + 1)],
                                           AL.mult, AL.add)
                    v.tensor_scalar(ych[g].ap()[:, 64 * w:64 * (w + 1)],
                                    psb[3 + 2 * g].ap()[:, 0:64],
                                    colstk[g].ap()[:, 8 * w + 4:8 * w + 5], None, AL.mult)
            seq.c("vector", fvryt)

            ch["pe_base"] = ch["pe"]
            ch["dv_base"] = ch["dv"]

            def chain_T2(t, w=w, pe0=ch["pe"], dv0=ch["dv"]):
                for st_i in range(32):
                    for g in range(2):
                        t.wait_ge(ch_dv, dv0 + 2 * (st_i - 1) + g + 1 if st_i > 0 else dv0)
                        mm = t.matmul(ps_chain[g].ap()[:, 0:64], wbd[g][:], u_t[g][:],
                                      start=True, stop=True)
                        mm.then_inc(ch_pe, 1)
                t.matmul(psb[6].ap()[0:1, 64:65], ones_col.ap()[0:1, 0:1],
                         ones_col.ap()[0:1, 0:1], start=True, stop=True)

            def chain_V2(v, w=w, pe0=ch["pe"], dv0=ch["dv"]):
                for st_i in range(32):
                    for g in range(2):
                        v.wait_ge(ch_pe, pe0 + 2 * st_i + g + 1)
                        v.tensor_tensor(tmpc[g].ap()[:, 0:64], ps_chain[g].ap()[:, 0:64],
                                        vrS[g].ap()[:, 0:64], AL.add)
                        uop = v.tensor_tensor(u_t[g][:], bch[g].ap()[:, 64 * w:64 * (w + 1)],
                                              tmpc[g].ap()[:, 0:64], AL.mult)
                        uop.then_inc(ch_dv, 1)
                v.tensor_copy(dsS[0].ap()[0:1, 0:1], dsS[0].ap()[0:1, 0:1])

            ch["pe"] += 64
            ch["dv"] += 64
            seq.raw({"tensor": chain_T2, "vector": chain_V2})

            def fMU2(t, w=w):
                for g in range(2):
                    t.matmul(psb[6 + g].ap()[:, 0:64], mbd[g][:], u_t[g][:],
                             start=True, stop=True)
            seq.c("tensor", fMU2)

            def fyaU(v, w=w):
                for g in range(2):
                    v.tensor_tensor(ych[g].ap()[:, 64 * w:64 * (w + 1)],
                                    ych[g].ap()[:, 64 * w:64 * (w + 1)],
                                    psb[6 + g].ap()[:, 0:64], AL.add)
                    v.tensor_scalar(tmpc[g].ap()[:, 0:64], u_t[g].ap()[:, 0:64],
                                    colstk[g].ap()[:, 8 * w + 3:8 * w + 4], None, AL.mult)
            seq.c("vector", fyaU)

            def fKU2(t, w=w):
                for g in range(2):
                    for hp in range(4):
                        dstps = psb[2 + 2 * g] if hp < 2 else psb[3 + 2 * g]
                        t.matmul(dstps.ap()[64 * (hp % 2):64 * (hp % 2) + 64, 0:64],
                                 ktm[g].ap()[32 * hp:32 * (hp + 1), 64 * w:64 * (w + 1)],
                                 tmpc[g].ap()[32 * hp:32 * (hp + 1), 0:64],
                                 start=True, stop=True,
                                 tile_position=(32 * hp, 64 * (hp % 2)))
            seq.c("tensor", fKU2)

            def fSu2(v, w=w):
                for g in range(2):
                    v.scalar_tensor_tensor(s_st[2 * g][:], s_st[2 * g][:],
                                           dccol[2 * g].ap()[:, w:w + 1],
                                           psb[2 + 2 * g].ap()[:, 0:64], AL.mult, AL.add)
                    v.scalar_tensor_tensor(s_st[2 * g + 1][:], s_st[2 * g + 1][:],
                                           dccol[2 * g + 1].ap()[:, w:w + 1],
                                           psb[3 + 2 * g].ap()[:, 0:64], AL.mult, AL.add)
            seq.c("vector", fSu2)

        if debug and sp == 0:
            seq.dma("gpsimd", lambda sy: sy.dma_start(out=dbg.tensor.ap()[:, 2448:3472],
                                                      in_=ych[0][:]))

        for g in range(2):
            for w in range(NW):
                def fct(t, g=g, w=w):
                    t.transpose(psb[5].ap()[0:64, 0:128], ych[g].ap()[:, 64 * w:64 * (w + 1)],
                                identt[:])
                seq.c("tensor", fct)

                def fce(v, g=g, w=w):
                    for hp in range(4):
                        h = 4 * g + hp
                        v.tensor_copy(ctxf[h // 2].ap()[(h % 2) * 64:(h % 2) * 64 + 64,
                                                        32 * w:32 * (w + 1)],
                                      psb[5].ap()[0:64, 32 * hp:32 * (hp + 1)])
                seq.c("vector", fce)

        proj_fm("wgT", vstage)

        def fsilu(s):
            load_act_set(s, nc, SET_SILU)
            for ot in range(4):
                s.activation(vstage[ot][:], vstage[ot][:], AF.Silu)
        seq.c("scalar", fsilu)

        for tile in range(4):
            def fsqy(s, tile=tile):
                load_act_set(s, nc, SET_EXPLN)
                s.activation(pf[:], ctxf[tile][:], AF.Square)
            seq.c("scalar", fsqy)

            def fnmy(t, tile=tile):
                t.matmul(psb[2].ap()[0:64, 0:512], sel.ap()[:, 64 * tile:64 * (tile + 1)],
                         pf[:], start=(tile == 0), stop=(tile == 3))
            seq.c("tensor", fnmy)

        def fyrs(v):
            v.tensor_copy(spf2[:], psb[2].ap()[0:8, 0:512])
            v.tensor_scalar(spf2[:], spf2[:], 1.0 / 64.0, 1e-6, AL.mult, AL.add)
        seq.c("vector", fyrs)

        def fyr0(s):
            s.activation(spf[:], spf2[:], AF.Ln)
            s.activation(spf[:], spf[:], AF.Exp, scale=-0.5)
        seq.c("scalar", fyr0)

        def fyrn(v):
            v.tensor_tensor(pf.ap()[0:8, 0:512], spf[:], spf[:], AL.mult)
            v.tensor_tensor(pf.ap()[0:8, 0:512], pf.ap()[0:8, 0:512], spf2[:], AL.mult)
            v.tensor_scalar(pf.ap()[0:8, 0:512], pf.ap()[0:8, 0:512], -0.5, 1.5, AL.mult, AL.add)
            v.tensor_tensor(spf2[:], spf[:], pf.ap()[0:8, 0:512], AL.mult)
        seq.c("vector", fyrn)

        seq.dma("gpsimd", lambda sy: sy.dma_start(out=scr.ap()[3:4, 0:8 * SPAN], in_=spf2[:]))

        def frr(sy):
            out = []
            for i in range(4):
                for hh in range(2):
                    dst_t = bch[i // 2]
                    sap = mk_ap(scr, 3 * 32768 + (2 * i + hh) * SPAN, [[0, 64], [1, SPAN]])
                    dap = mk_ap(dst_t, (i % 2) * 512 + hh * 64 * 1024, [[1024, 64], [1, SPAN]])
                    out.append(sy.dma_start(out=dap, in_=sap))
            return out
        seq.dma("gpsimd", frr, n=8)

        def fapply(v):
            for i in range(4):
                rep = bch[i // 2].ap()
                rep = mk_ap(bch[i // 2], (i % 2) * 512, [[1024, 128], [1, 512]])
                v.tensor_tensor(ctxf[i][:], ctxf[i][:], rep, AL.mult)
                v.tensor_tensor(ctxf[i][:], ctxf[i][:], vstage[i][:], AL.mult)
                v.tensor_scalar(ctxf[i][:], ctxf[i][:], nwfm[:], None, AL.mult)
        seq.c("vector", fapply)

        def fcout(sy, toff=toff):
            return [sy.dma_start(out=ctx_hbm.tensor.ap()[128 * i:128 * (i + 1),
                                                         toff:toff + SPAN],
                                 in_=ctxf[i][:]) for i in range(4)]
        seq.dma("gpsimd", fcout, n=4)

        if debug and sp == 0:
            dumps = [(caf[:], 0, 8, 0, 512), (knf[:], 0, 8, 3584, 4096),
                     (colstk[0][:], 0, 128, 512, 640), (t1S[0][:], 0, 128, 640, 1152),
                     (wbd[0][:], 0, 128, 1152, 1280), (u_t[0][:], 0, 128, 1280, 1344),
                     (ych[0][:], 0, 128, 1344, 2368), (s_st[0][:], 0, 128, 2368, 2432),
                     (vrS[0][:], 0, 128, 3472, 3536), (dccol[0][:], 0, 128, 3536, 3552)]
            for src_ap, p0, p1, c0, c1 in dumps:
                seq.dma("gpsimd", lambda sy, src_ap=src_ap, p0=p0, p1=p1, c0=c0, c1=c1:
                        sy.dma_start(out=dbg.tensor.ap()[p0:p1, c0:c1], in_=src_ap))

    LAST_MARKS["end"] = len(seq.items)
    if trunc is not None:
        seq.items = seq.items[:trunc]
    with (
        nc.semaphore("c_sem") as c_sem,
        nc.semaphore("d_sync") as d_sync,
        nc.semaphore("d_gp") as d_gp,
        nc.Block() as block,
    ):
        d_sems = {"sync": d_sync, "gpsimd": d_gp}
        n_items, n_dma = seq.build(block, c_sem, d_sems)

        @block.gpsimd
        def _(g):
            g.wait_ge(c_sem, n_items)
            for k, s in d_sems.items():
                g.wait_ge(s, 16 * n_dma[k])
    return nc



def build_mamba_a_phase(T_SEQ=4096, pre_ln=False):
    SPAN = 512
    NSP = T_SEQ // SPAN
    NCH = SPAN // 64
    NPC = 1216
    nc = bass.Bass("TRN2", target_bir_lowering=False, detect_race_conditions=False)
    x_in = nc.dram_tensor("x", [D, T_SEQ], F32, kind="ExternalInput").ap()
    nw1_in = nc.dram_tensor("nw1", [128, 8], F32, kind="ExternalInput").ap()
    mask64_in = nc.dram_tensor("mask64", [128, 64], F32, kind="ExternalInput").ap()
    ident_in = nc.dram_tensor("ident", [128, 128], F32, kind="ExternalInput").ap()
    HDs = []
    for _sfx in ("", "_b"):
        HDs.append(dict(
            inwT=nc.dram_tensor("inwT" + _sfx, [D, NPC], F32, kind="ExternalInput").ap(),
            convw=nc.dram_tensor("convw" + _sfx, [128, 20], F32, kind="ExternalInput").ap(),
            convb=nc.dram_tensor("convb" + _sfx, [128, 5], F32, kind="ExternalInput").ap(),
            dtb=nc.dram_tensor("dtb" + _sfx, [8, 1], F32, kind="ExternalInput").ap(),
            alog=nc.dram_tensor("alog" + _sfx, [8, 1], F32, kind="ExternalInput").ap(),
            dcolf=nc.dram_tensor("dcolf" + _sfx, [128, 4], F32, kind="ExternalInput").ap(),
            y=nc.dram_tensor("y" + _sfx, [512, T_SEQ], F32, kind="ExternalOutput").ap(),
            z=nc.dram_tensor("z" + _sfx, [512, T_SEQ], F32, kind="ExternalOutput").ap(),
        ))
    ln_hbm = nc.dram_tensor("ln0", [D, T_SEQ], F32, kind="ExternalOutput").ap() if pre_ln else None
    scr = nc.dram_tensor("scr_m", [8, 32768], F32, kind="Internal")

    A = nc.alloc_sbuf_tensor
    WH = A("hw", [128, 8 * NPC], BF16)
    WL = A("lw", [128, 8 * NPC], BF16)
    xt = [A(f"xt{i}", [128, SPAN], F32) for i in range(8)]
    wstage = xt[0]
    ph = [A(f"ph{i}", [128, SPAN], BF16) for i in range(8)]
    pl = [A(f"pl{i}", [128, SPAN], BF16) for i in range(8)]
    pf = A("pf", [128, SPAN], F32)
    sqs = A("sqs", [128, 512], F32)
    stat = A("stat", [1, 3 * SPAN], F32)
    nw1 = A("nw1c", [128, 8], F32)
    convw = A("convw_s", [128, 20], F32)
    convb = A("convb_s", [128, 5], F32)
    dtb = A("dtb_s", [8, 1], F32)
    nega = A("nega_s", [8, 1], F32)
    dcolf = A("dcolf_s", [128, 4], F32)
    mask64 = A("mask64_s", [128, 64], F32)
    identt = A("identt", [128, 128], F32)
    ones_col = A("ones_col", [128, 1], F32)
    ones_row = A("ones_row", [1, 128], F32)
    cvs = [A(f"cvs{i}", [128, SPAN + 3], F32) for i in range(5)]
    xfm = [A(f"xfm{i}", [128, SPAN], F32) for i in range(4)]
    bcfm = A("bcfm", [128, SPAN], F32)
    cC = A("cC", [128, SPAN], F32)
    xch = [A(f"xch{i}", [128, SPAN], F32) for i in range(4)]
    btm2 = A("btm2", [128, SPAN], F32)
    csrep = [A(f"csrep{i}", [128, SPAN], F32) for i in range(4)]
    colm = [A(f"colm{i}", [128, 8 * NCH], F32) for i in range(4)]
    dccm = [A(f"dccm{i}", [128, NCH], F32) for i in range(4)]
    yfm = [A(f"yfm{i}", [128, SPAN], F32) for i in range(4)]
    gts2 = A("gts2", [128, 64 * NCH], F32)
    mt = [A(f"mt{i}", [128, 64], F32) for i in range(4)]
    dscm = [A(f"dscm{i}", [128, 64], F32) for i in range(4)]
    escm = [A(f"escm{i}", [128, 64], F32) for i in range(4)]
    xsc = [A(f"xsc{i}", [128, 64], F32) for i in range(4)]
    ytp = [A(f"ytp{i}", [128, 64], F32) for i in range(4)]
    s_st = [A(f"sst{i}", [128, 64], F32) for i in range(4)]
    dtraw = A("dtraw", [8, SPAN], F32)
    dtf = A("dtf", [8, SPAN], F32)
    csr = A("csr", [8, SPAN], F32)
    decdt = A("decdt", [8, SPAN], F32)
    yod = A("yod", [8, SPAN], F32)
    rsc1 = A("rsc1", [8, SPAN], F32)
    rsc2 = A("rsc2", [8, SPAN], F32)
    dcfm = A("dcfm", [8, NCH], F32)
    cpm = A("cpm", [8, NCH], F32)

    psb = [nc.alloc_psum_tensor(f"psM{i}", [128, 512], F32) for i in range(6)]
    ps_Yd, ps_Yo, ps_proj, ps_G, ps_S, ps_T = psb[0], psb[1], psb[2], psb[3], psb[4], psb[5]

    seq = Seq(nc)

    def fconst(v):
        v.memset(ones_col[:], 1.0)
        v.memset(ones_row[:], 1.0)
    seq.c("vector", fconst)
    for nm, dst, srcap in [("nw1", nw1, nw1_in), ("mask64", mask64, mask64_in),
                           ("ident", identt, ident_in)]:
        seq.dma("sync", lambda s, dst=dst, srcap=srcap: s.dma_start(out=dst[:], in_=srcap))

    if True:
      for HALF in range(2):
        HD = HDs[HALF]

        def fhc(s, HD=HD):
            return [s.dma_start(out=convw[:], in_=HD["convw"]),
                    s.dma_start(out=convb[:], in_=HD["convb"]),
                    s.dma_start(out=dtb[:], in_=HD["dtb"]),
                    s.dma_start(out=nega[:], in_=HD["alog"]),
                    s.dma_start(out=dcolf[:], in_=HD["dcolf"])]
        seq.dma("sync", fhc, n=5)

        def fnega(s):
            load_act_set(s, nc, SET_EXPLN)
            s.activation(nega[:], nega[:], AF.Exp)
        seq.c("scalar", fnega)

        def fhz(v):
            v.tensor_scalar(nega[:], nega[:], -1.0, None, AL.mult)
            for i in range(4):
                v.memset(s_st[i][:], 0.0)
            for i in range(5):
                v.memset(cvs[i].ap()[:, 0:3], 0.0)
        seq.c("vector", fhz)

        for dch in range(8):
            for pc in range(3):
                c0, c1 = 512 * pc, min(512 * (pc + 1), NPC)
                seq.dma("sync", lambda s, dch=dch, c0=c0, c1=c1, HD=HD: s.dma_start(
                    out=wstage.ap()[:, 0:c1 - c0],
                    in_=HD["inwT"].tensor.ap()[128 * dch:128 * (dch + 1), c0:c1]))

                def fsp_(v, dch=dch, c0=c0, c1=c1):
                    v.tensor_copy(WH.ap()[:, NPC * dch + c0:NPC * dch + c1],
                                  wstage.ap()[:, 0:c1 - c0])
                    v.tensor_tensor(WL.ap()[:, NPC * dch + c0:NPC * dch + c1],
                                    wstage.ap()[:, 0:c1 - c0],
                                    WH.ap()[:, NPC * dch + c0:NPC * dch + c1], AL.subtract)
                seq.c("vector", fsp_)

        for sp in range(NSP):
            toff = SPAN * sp

            def fxl(s, toff=toff):
                return [s.dma_start(out=xt[i][:],
                                    in_=x_in.tensor.ap()[128 * i:128 * (i + 1), toff:toff + SPAN])
                        for i in range(8)]
            seq.dma("sync", fxl, n=8)

            if pre_ln and HALF == 0:
                ctx0 = dict(xt=xt, ph=ph, pl=pl, sq=sqs.ap()[:, 0:512], pf=pf, stat=stat,
                            ones_col=ones_col, ones_row=ones_row, eps=1e-5, out_f32=True,
                            ps_stA=psb[2].ap()[0:1, 0:512], ps_stB=psb[3].ap()[0:1, 0:512],
                            ps_rep=psb[4], ps_rep2=psb[5])
                emit_ln_prep(seq, nc, ctx0, SPAN, with_mean=True)
                def flno(sy, toff=toff):
                    return [sy.dma_start(out=ln_hbm.tensor.ap()[128 * i:128 * (i + 1),
                                                                toff:toff + SPAN],
                                         in_=xt[i][:]) for i in range(8)]
                seq.dma("gpsimd", flno, n=8)
            ctxd = dict(xt=xt, ph=ph, pl=pl, sq=sqs.ap()[:, 0:512], pf=pf, stat=stat,
                        ones_col=ones_col, ones_row=ones_row, eps=1e-5,
                        wcol=[nw1.ap()[:, i:i + 1] for i in range(8)],
                        ps_stA=psb[2].ap()[0:1, 0:512], ps_stB=psb[3].ap()[0:1, 0:512],
                        ps_rep=psb[4], ps_rep2=psb[5])
            emit_ln_prep(seq, nc, ctxd, SPAN, with_mean=False)

            if sp > 0:
                def fcar(v):
                    for i in range(5):
                        v.tensor_copy(cvs[i].ap()[:, 0:3], cvs[i].ap()[:, SPAN:SPAN + 3])
                seq.c("vector", fcar)

            def fproj(ot, dst_ap, m=128):
                def fmm(t, ot=ot, m=m):
                    for dch in range(8):
                        wsh = WH.ap()[:, NPC * dch + 128 * ot:NPC * dch + 128 * ot + m]
                        wsl = WL.ap()[:, NPC * dch + 128 * ot:NPC * dch + 128 * ot + m]
                        t.matmul(ps_proj.ap()[0:m, 0:512], wsh, ph[dch][:], start=(dch == 0), stop=False)
                        t.matmul(ps_proj.ap()[0:m, 0:512], wsl, ph[dch][:], start=False, stop=False)
                        t.matmul(ps_proj.ap()[0:m, 0:512], wsh, pl[dch][:], start=False, stop=(dch == 7))
                seq.c("tensor", fmm)
                seq.c("vector", lambda v, dst_ap=dst_ap: v.tensor_copy(dst_ap, ps_proj.ap()[0:dst_ap.ap[0][1], 0:512]))

            for i in range(4):
                fproj(i, xt[i][:])

            def fzo(sy, toff=toff, HD=HD):
                return [sy.dma_start(out=HD["z"].tensor.ap()[128 * i:128 * (i + 1),
                                                             toff:toff + SPAN],
                                     in_=xt[i][:]) for i in range(4)]
            seq.dma("gpsimd", fzo, n=4)
            for i in range(4):
                fproj(4 + i, cvs[i].ap()[:, 3:SPAN + 3])
            fproj(8, cvs[4].ap()[:, 3:SPAN + 3])
            def fmmdt(t):
                for dch in range(8):
                    wsh = WH.ap()[:, NPC * dch + 1152:NPC * dch + 1216]
                    wsl = WL.ap()[:, NPC * dch + 1152:NPC * dch + 1216]
                    t.matmul(ps_proj.ap()[0:64, 0:512], wsh, ph[dch][:], start=(dch == 0), stop=False)
                    t.matmul(ps_proj.ap()[0:64, 0:512], wsl, ph[dch][:], start=False, stop=False)
                    t.matmul(ps_proj.ap()[0:64, 0:512], wsh, pl[dch][:], start=False, stop=(dch == 7))
            seq.c("tensor", fmmdt)
            seq.c("vector", lambda v: v.tensor_copy(dtraw[:], ps_proj.ap()[0:8, 0:512]))

            def fconv(v):
                for i in range(5):
                    out = xfm[i][:] if i < 4 else bcfm[:]
                    v.tensor_scalar(out, cvs[i].ap()[:, 3:SPAN + 3], convw.ap()[:, 4 * i + 3:4 * i + 4],
                                    convb.ap()[:, i:i + 1], AL.mult, AL.add)
                    for tap in range(3):
                        v.scalar_tensor_tensor(out, cvs[i].ap()[:, tap:tap + SPAN],
                                               convw.ap()[:, 4 * i + tap:4 * i + tap + 1],
                                               out, AL.mult, AL.add)
            seq.c("vector", fconv)

            def fsilu(s):
                load_act_set(s, nc, SET_SILU)
                for i in range(4):
                    s.activation(xfm[i][:], xfm[i][:], AF.Silu)
                s.activation(bcfm[:], bcfm[:], AF.Silu)
            seq.c("scalar", fsilu)

            def fdt1(v):
                v.tensor_scalar(dtraw[:], dtraw[:], dtb[:], None, AL.add)
                v.tensor_scalar_max(rsc1[:], dtraw[:], 0.0)
            seq.c("vector", fdt1)

            def fdt2(s):
                load_act_set(s, nc, SET_EXPLN)
                s.activation(dtf[:], dtraw[:], AF.Abs)
                s.activation(dtf[:], dtf[:], AF.Exp, scale=-1.0)
                s.activation(dtf[:], dtf[:], AF.Ln, bias=1.0)
            seq.c("scalar", fdt2)

            def fdt3(v):
                v.tensor_tensor(dtf[:], dtf[:], rsc1[:], AL.add)
                v.tensor_scalar(csr[:], dtf[:], nega[:], None, AL.mult)
                v.tensor_scalar(rsc1[:], rsc1[:], 0.0, 1.0, AL.mult, AL.add)
                v.tensor_tensor_scan(rsc2[:], rsc1[:], csr[:], 0.0, AL.mult, AL.add)
                v.tensor_copy(csr[:], rsc2[:])
                v.memset(cpm.ap()[0:8, 0:1], 0.0)
                v.tensor_copy(cpm.ap()[0:8, 1:NCH], mk_ap(csr, 63, [[SPAN, 8], [64, NCH - 1]]))
                v.tensor_tensor(decdt[:], csr[:], mk_ap(csr, 63, [[SPAN, 8], [64, NCH], [0, 64]]),
                                AL.subtract)
                v.tensor_tensor(yod[:], csr[:], mk_ap(cpm, 0, [[NCH, 8], [1, NCH], [0, 64]]),
                                AL.subtract)
            seq.c("vector", fdt3)

            def fdt4(s):
                s.activation(decdt[:], decdt[:], AF.Exp, scale=-1.0)
                s.activation(yod[:], yod[:], AF.Exp)
            seq.c("scalar", fdt4)

            def fdt5(v):
                v.tensor_tensor(decdt[:], decdt[:], dtf[:], AL.mult)
                v.tensor_copy(dcfm[:], mk_ap(yod, 63, [[SPAN, 8], [64, NCH]]))
            seq.c("vector", fdt5)

            def fst45(sy):
                return [sy.dma_start(out=scr.ap()[4:5, 0:8 * SPAN], in_=csr[:]),
                        sy.dma_start(out=scr.ap()[5:6, 0:8 * NCH], in_=dcfm[:])]
            seq.dma("gpsimd", fst45, n=2)

            def fcsdc(sy):
                out = []
                for i in range(4):
                    for hh in range(2):
                        sap = mk_ap(scr, 4 * 32768 + (2 * i + hh) * SPAN, [[0, 64], [1, SPAN]])
                        dap = mk_ap(csrep[i], hh * 64 * SPAN, [[SPAN, 64], [1, SPAN]])
                        out.append(sy.dma_start(out=dap, in_=sap))
                        sap = mk_ap(scr, 5 * 32768 + (2 * i + hh) * NCH, [[0, 64], [1, NCH]])
                        dap = mk_ap(dccm[i], hh * 64 * NCH, [[NCH, 64], [1, NCH]])
                        out.append(sy.dma_start(out=dap, in_=sap))
                return out
            seq.dma("gpsimd", fcsdc, n=16)

            def fcc(sy):
                sap = mk_ap(bcfm, 64 * SPAN, [[SPAN, 64], [1, SPAN]])
                return [sy.dma_start(out=mk_ap(cC, 0, [[SPAN, 64], [1, SPAN]]), in_=sap),
                        sy.dma_start(out=mk_ap(cC, 64 * SPAN, [[SPAN, 64], [1, SPAN]]), in_=sap)]
            seq.dma("gpsimd", fcc, n=2)

            tmk = [0]

            def emit_tmstage(src_tile, full=False):
                LAGM = None if full else {"gpsimd": 12}
                stg = sqs if tmk[0] % 2 == 0 else pf
                tmk[0] += 1

                def ftr(t, src_tile=src_tile):
                    for blk in range(4):
                        t.transpose(ps_T.ap()[:, 128 * blk:128 * (blk + 1)],
                                    src_tile.ap()[:, 128 * blk:128 * (blk + 1)], identt[:])
                seq.c("tensor", ftr, dma_lag=LAGM)
                seq.c("vector", lambda v, stg=stg: v.tensor_copy(stg.ap()[:, 0:512],
                                                                 ps_T.ap()[:, 0:512]),
                      dma_lag=LAGM)

                def fst(sy, stg=stg):
                    return [sy.dma_start(out=scr.ap()[blk:blk + 1, 0:16384],
                                         in_=stg.ap()[:, 128 * blk:128 * (blk + 1)])
                            for blk in range(4)]
                seq.dma("gpsimd", fst, n=4)

            for i in range(4):
                emit_tmstage(xfm[i], full=(i == 0))

                def fg(sy, i=i):
                    out = []
                    for blk in range(4):
                        for slot in range(2):
                            sap = mk_ap(scr, blk * 32768 + slot * 64,
                                        [[128, 64], [64 * 128, 2], [1, 64]])
                            dap = mk_ap(xch[i], slot * 64 * SPAN + blk * 128,
                                        [[SPAN, 64], [64, 2], [1, 64]])
                            out.append(sy.dma_start(out=dap, in_=sap))
                    return out
                seq.dma("gpsimd", fg, n=8)

            emit_tmstage(bcfm)

            def fgb(sy):
                out = []
                for blk in range(4):
                    for dup in range(2):
                        sap = mk_ap(scr, blk * 32768, [[128, 64], [64 * 128, 2], [1, 64]])
                        dap = mk_ap(btm2, dup * 64 * SPAN + blk * 128,
                                    [[SPAN, 64], [64, 2], [1, 64]])
                        out.append(sy.dma_start(out=dap, in_=sap))
                return out
            seq.dma("gpsimd", fgb, n=8)

            def frc(sy):
                out = []
                for i in range(4):
                    for q, src_fm in enumerate([csr, decdt, dtf, yod]):
                        for h in range(2):
                            sap = mk_ap(src_fm, (2 * i + h) * SPAN, [[SPAN, 1], [1, SPAN]])
                            dap = mk_ap(xt[i], (64 * h + q) * SPAN, [[SPAN, 1], [1, SPAN]])
                            out.append(sy.dma_start(out=dap, in_=sap))
                return out
            seq.dma("gpsimd", frc, n=32)
            for i in range(4):
                emit_tmstage(xt[i], full=(i == 0))

                def fgc(sy, i=i):
                    out = []
                    for blk in range(4):
                        for slot in range(2):
                            sap = mk_ap(scr, blk * 32768 + slot * 64,
                                        [[128, 64], [64 * 128, 2], [1, 8]])
                            dap = mk_ap(colm[i], slot * 64 * (8 * NCH) + blk * 16,
                                        [[8 * NCH, 64], [8, 2], [1, 8]])
                            out.append(sy.dma_start(out=dap, in_=sap))
                    return out
                seq.dma("gpsimd", fgc, n=8)

            for c in range(NCH):
                def fGt(t, c=c):
                    t.matmul(ps_G.ap()[0:64, 0:64], bcfm.ap()[0:64, 64 * c:64 * (c + 1)],
                             cC.ap()[0:64, 64 * c:64 * (c + 1)], start=True, stop=True)
                seq.c("tensor", fGt)
                seq.c("vector", lambda v, c=c: v.tensor_copy(gts2.ap()[0:64, 64 * c:64 * (c + 1)],
                                                             ps_G.ap()[0:64, 0:64]))

                def fgdup(sy, c=c):
                    sap = mk_ap(gts2, 64 * c, [[64 * NCH, 64], [1, 64]])
                    dap = mk_ap(gts2, 64 * 64 * NCH + 64 * c, [[64 * NCH, 64], [1, 64]])
                    return sy.dma_start(out=dap, in_=sap)
                seq.dma("gpsimd", fgdup)

                def fL(v, c=c):
                    for i in range(4):
                        v.scalar_tensor_tensor(dscm[i][:], csrep[i].ap()[:, 64 * c:64 * (c + 1)],
                                               colm[i].ap()[:, 8 * c:8 * c + 1], mask64[:],
                                               AL.subtract, AL.add)
                seq.c("vector", fL)

                def fEx(s, c=c):
                    for i in range(4):
                        s.activation(escm[i][:], dscm[i][:], AF.Exp)
                seq.c("scalar", fEx)

                def fM(v, c=c):
                    for i in range(4):
                        v.scalar_tensor_tensor(mt[i][:], gts2.ap()[:, 64 * c:64 * (c + 1)],
                                               colm[i].ap()[:, 8 * c + 2:8 * c + 3], escm[i][:],
                                               AL.mult, AL.mult)
                        v.tensor_scalar(xsc[i][:], xch[i].ap()[:, 64 * c:64 * (c + 1)],
                                        colm[i].ap()[:, 8 * c + 1:8 * c + 2], None, AL.mult)
                seq.c("vector", fM)

                def fmms(t, c=c):
                    for i in range(4):
                        co = 64 * i
                        for sl in range(2):
                            b2 = 64 * sl
                            t.matmul(ps_Yd.ap()[b2:b2 + 64, co:co + 64], mt[i].ap()[b2:b2 + 64, 0:64],
                                     xch[i].ap()[b2:b2 + 64, 64 * c:64 * (c + 1)],
                                     start=True, stop=True, tile_position=(b2, b2))
                            t.matmul(ps_Yo.ap()[b2:b2 + 64, co:co + 64],
                                     cC.ap()[b2:b2 + 64, 64 * c:64 * (c + 1)],
                                     s_st[i].ap()[b2:b2 + 64, 0:64],
                                     start=True, stop=True, tile_position=(b2, b2))
                            t.matmul(ps_S.ap()[b2:b2 + 64, co:co + 64],
                                     btm2.ap()[b2:b2 + 64, 64 * c:64 * (c + 1)],
                                     xsc[i].ap()[b2:b2 + 64, 0:64],
                                     start=True, stop=True, tile_position=(b2, b2))
                seq.c("tensor", fmms)

                def fyev(v, c=c):
                    for i in range(4):
                        co = 64 * i
                        v.tensor_copy(ytp[i][:], ps_Yd.ap()[:, co:co + 64])
                        v.scalar_tensor_tensor(ytp[i][:], ps_Yo.ap()[:, co:co + 64],
                                               colm[i].ap()[:, 8 * c + 3:8 * c + 4],
                                               ytp[i][:], AL.mult, AL.add)
                        v.scalar_tensor_tensor(ytp[i][:], xch[i].ap()[:, 64 * c:64 * (c + 1)],
                                               dcolf.ap()[:, i:i + 1], ytp[i][:], AL.mult, AL.add)
                        v.scalar_tensor_tensor(s_st[i][:], s_st[i][:],
                                               dccm[i].ap()[:, c:c + 1],
                                               ps_S.ap()[:, co:co + 64], AL.mult, AL.add)
                seq.c("vector", fyev)

                def fytr(t, c=c):
                    for i in range(4):
                        co = 64 * i
                        for sl in range(2):
                            b2 = 64 * sl
                            t.matmul(ps_T.ap()[b2:b2 + 64, co:co + 64], ytp[i].ap()[b2:b2 + 64, 0:64],
                                     identt.ap()[b2:b2 + 64, b2:b2 + 64],
                                     start=True, stop=True, tile_position=(b2, b2))
                seq.c("tensor", fytr)

                def fyfm(v, c=c):
                    for i in range(4):
                        co = 64 * i
                        for sl in range(2):
                            b2 = 64 * sl
                            v.tensor_copy(yfm[i].ap()[b2:b2 + 64, 64 * c:64 * (c + 1)],
                                          ps_T.ap()[b2:b2 + 64, co:co + 64])
                seq.c("vector", fyfm)

            def fyo(sy, toff=toff, HD=HD):
                return [sy.dma_start(out=HD["y"].tensor.ap()[128 * i:128 * (i + 1),
                                                             toff:toff + SPAN],
                                     in_=yfm[i][:]) for i in range(4)]
            seq.dma("gpsimd", fyo, n=4)

    with (
        nc.semaphore("c_sem") as c_sem,
        nc.semaphore("d_sync") as d_sync,
        nc.semaphore("d_gp") as d_gp,
        nc.Block() as block,
    ):
        d_sems = {"sync": d_sync, "gpsimd": d_gp}
        n_items, n_dma = seq.build(block, c_sem, d_sems)

        @block.gpsimd
        def _(g):
            g.wait_ge(c_sem, n_items)
            for k, s in d_sems.items():
                g.wait_ge(s, 16 * n_dma[k])
    return nc



def build_mamba_out_phase(TQ=1024, add_outer=False):
    SPAN = 512
    NSP = TQ // SPAN
    nc = bass.Bass("TRN2", target_bir_lowering=False, detect_race_conditions=False)
    x_in = nc.dram_tensor("x", [D, TQ], F32, kind="ExternalInput").ap()
    y_in = nc.dram_tensor("yf", [2048, TQ], F32, kind="ExternalInput").ap()
    z_in = nc.dram_tensor("zf", [2048, TQ], F32, kind="ExternalInput").ap()
    gw_in = nc.dram_tensor("gwc", [128, 16], F32, kind="ExternalInput").ap()
    owT_in = nc.dram_tensor("owT", [2048, 1024], F32, kind="ExternalInput").ap()
    xo_in = nc.dram_tensor("xo", [D, TQ], F32, kind="ExternalInput").ap() if add_outer else None
    out_hbm = nc.dram_tensor("out", [D, TQ], F32, kind="ExternalOutput").ap()

    A = nc.alloc_sbuf_tensor
    WH = A("how", [128, 16 * 1024], BF16)
    WL = A("low", [128, 16 * 1024], BF16)
    gt = [A(f"gt{i}", [128, SPAN], F32) for i in range(16)]
    wstage = gt[0]
    zst = A("zst", [128, SPAN], F32)
    xt = [A(f"xt{i}", [128, SPAN], F32) for i in range(8)]
    gh = [A(f"gh{i}", [128, SPAN], BF16) for i in range(16)]
    gl = [A(f"gl{i}", [128, SPAN], BF16) for i in range(16)]
    pf = A("pf", [128, SPAN], F32)
    sqs = A("sqs", [128, 512], F32)
    stat = A("stat", [1, 2 * SPAN], F32)
    gwc = A("gwc_s", [128, 16], F32)
    ones_col = A("ones_col", [128, 1], F32)
    ones_row = A("ones_row", [1, 128], F32)

    psb = [nc.alloc_psum_tensor(f"psO{i}", [128, 512], F32) for i in range(4)]
    ps_mm, ps_st, ps_rep = psb[0], psb[1], psb[2]

    seq = Seq(nc)
    seq.c("vector", lambda v: (v.memset(ones_col[:], 1.0), v.memset(ones_row[:], 1.0)))
    seq.dma("sync", lambda s: s.dma_start(out=gwc[:], in_=gw_in))
    for dch in range(16):
        for pc in range(2):
            c0 = 512 * pc
            seq.dma("sync", lambda s, dch=dch, c0=c0: s.dma_start(
                out=wstage.ap()[:, 0:512],
                in_=owT_in.tensor.ap()[128 * dch:128 * (dch + 1), c0:c0 + 512]))

            def fsp_(v, dch=dch, c0=c0):
                sl = slice(1024 * dch + c0, 1024 * dch + c0 + 512)
                v.tensor_copy(WH.ap()[:, sl], wstage.ap()[:, 0:512])
                v.tensor_tensor(WL.ap()[:, sl], wstage.ap()[:, 0:512], WH.ap()[:, sl],
                                AL.subtract)
            seq.c("vector", fsp_)

    for sp in range(NSP):
        toff = SPAN * sp

        def fxl(s, toff=toff):
            out = [s.dma_start(out=xt[i][:],
                               in_=x_in.tensor.ap()[128 * i:128 * (i + 1), toff:toff + SPAN])
                   for i in range(8)]
            out += [s.dma_start(out=gt[i][:],
                                in_=y_in.tensor.ap()[128 * i:128 * (i + 1), toff:toff + SPAN])
                    for i in range(16)]
            return out
        seq.dma("sync", fxl, n=24)
        for i in range(16):
            seq.dma("sync", lambda s, i=i, toff=toff: s.dma_start(
                out=zst[:], in_=z_in.tensor.ap()[128 * i:128 * (i + 1), toff:toff + SPAN]))

            def fg(sv, i=i):
                load_act_set(sv, nc, SET_SILU)
                sv.activation(zst[:], zst[:], AF.Silu)
            seq.c("scalar", fg)
            seq.c("vector", lambda v, i=i: v.tensor_tensor(gt[i][:], gt[i][:], zst[:], AL.mult))

        seq.c("scalar", lambda s: load_act_set(s, nc, SET_EXPLN))
        for i in range(16):
            seq.c("scalar", lambda s, i=i: s.activation(sqs.ap()[:, 0:512], gt[i][:], AF.Square))
            seq.c("tensor", lambda t, i=i: t.matmul(ps_st.ap()[0:1, 0:512], ones_col[:],
                                                    sqs.ap()[:, 0:512],
                                                    start=(i == 0), stop=(i == 15)))

        def fvar(v):
            v.tensor_copy(stat.ap()[0:1, 0:SPAN], ps_st.ap()[0:1, 0:512])
            v.tensor_scalar(stat.ap()[0:1, 0:SPAN], stat.ap()[0:1, 0:SPAN],
                            1.0 / 2048.0, 1e-5, AL.mult, AL.add)
        seq.c("vector", fvar)

        def frs(s):
            s.activation(stat.ap()[0:1, SPAN:2 * SPAN], stat.ap()[0:1, 0:SPAN], AF.Ln)
            s.activation(stat.ap()[0:1, SPAN:2 * SPAN], stat.ap()[0:1, SPAN:2 * SPAN],
                         AF.Exp, scale=-0.5)
        seq.c("scalar", frs)

        def fnw(v):
            v.tensor_tensor(pf.ap()[0:1, 0:SPAN], stat.ap()[0:1, SPAN:2 * SPAN],
                            stat.ap()[0:1, SPAN:2 * SPAN], AL.mult)
            v.tensor_tensor(pf.ap()[0:1, 0:SPAN], pf.ap()[0:1, 0:SPAN],
                            stat.ap()[0:1, 0:SPAN], AL.mult)
            v.tensor_scalar(pf.ap()[0:1, 0:SPAN], pf.ap()[0:1, 0:SPAN], -0.5, 1.5,
                            AL.mult, AL.add)
            v.tensor_tensor(stat.ap()[0:1, 0:SPAN], stat.ap()[0:1, SPAN:2 * SPAN],
                            pf.ap()[0:1, 0:SPAN], AL.mult)
        seq.c("vector", fnw)
        seq.c("tensor", lambda t: t.matmul(ps_rep[:], ones_row[:], stat.ap()[0:1, 0:SPAN],
                                           start=True, stop=True))
        for i in range(16):
            def fsc(v, i=i):
                v.tensor_tensor(gt[i][:], gt[i][:], ps_rep.ap()[:], AL.mult)
                v.tensor_scalar(gt[i][:], gt[i][:], gwc.ap()[:, i:i + 1], None, AL.mult)
                v.tensor_copy(gh[i][:], gt[i][:])
                v.tensor_tensor(gl[i][:], gt[i][:], gh[i][:], AL.subtract)
            seq.c("vector", fsc)

        for ot in range(8):
            def fmm(t, ot=ot):
                for dch in range(16):
                    wsh = WH.ap()[:, 1024 * dch + 128 * ot:1024 * dch + 128 * (ot + 1)]
                    wsl = WL.ap()[:, 1024 * dch + 128 * ot:1024 * dch + 128 * (ot + 1)]
                    t.matmul(ps_mm[:], wsh, gh[dch][:], start=(dch == 0), stop=False)
                    t.matmul(ps_mm[:], wsl, gh[dch][:], start=False, stop=False)
                    t.matmul(ps_mm[:], wsh, gl[dch][:], start=False, stop=(dch == 15))
            seq.c("tensor", fmm)
            seq.c("vector", lambda v, ot=ot: v.tensor_tensor(xt[ot][:], xt[ot][:],
                                                             ps_mm.ap()[:], AL.add))
            if add_outer:
                seq.dma("sync", lambda s, ot=ot, toff=toff: s.dma_start(
                    out=zst[:], in_=xo_in.tensor.ap()[128 * ot:128 * (ot + 1),
                                                      toff:toff + SPAN]))
                seq.c("vector", lambda v, ot=ot: v.tensor_tensor(xt[ot][:], xt[ot][:],
                                                                 zst[:], AL.add))
            seq.dma("gpsimd", lambda sy, ot=ot, toff=toff: sy.dma_start(
                out=out_hbm.tensor.ap()[128 * ot:128 * (ot + 1), toff:toff + SPAN],
                in_=xt[ot][:]))

    with (
        nc.semaphore("c_sem") as c_sem,
        nc.semaphore("d_sync") as d_sync,
        nc.semaphore("d_gp") as d_gp,
        nc.Block() as block,
    ):
        d_sems = {"sync": d_sync, "gpsimd": d_gp}
        n_items, n_dma = seq.build(block, c_sem, d_sems)

        @block.gpsimd
        def _(g):
            g.wait_ge(c_sem, n_items)
            for k, s in d_sems.items():
                g.wait_ge(s, 16 * n_dma[k])
    return nc
